# version 14
# speedup vs baseline: 1.4847x; 1.0252x over previous
import math
import numpy as np
import ml_dtypes
import concourse.bass as bass
import concourse.mybir as mybir
from concourse.bass_utils import run_bass_kernel_spmd

F32 = mybir.dt.float32
BF16 = mybir.dt.bfloat16
I8 = mybir.dt.int8
AF = mybir.ActivationFunctionType
ALU = mybir.AluOpType
AX = mybir.AxisListType

B, S, D = 2, 4096, 2048
NCORES = 8
TOK = (B * S) // NCORES
NT = TOK // 128
ND = D // 128
NEG = -1.0e30
MAGIC = 1.5 * 2.0 ** 23

L0_NH, L0_HD = 16, 128
L1_NHQ, L1_NHKV, L1_HD = 32, 8, 64

CBW = 1152
CB0 = 384

CONSTS = [1e-5, MAGIC, -MAGIC, 1.0, 0.5, -1.0, 1e-8, 0.0]


def consts_host():
    return np.tile(np.array(CONSTS, dtype=np.float32)[None, :], (128, 1))


def register_consts(nc, csb):
    for i, v in enumerate(CONSTS):
        nc.const_aps.register(F32, float(v), csb[:, i:i + 1])



def _l0_rope_tables():
    inv_freq = (1.0 / (10000.0 ** (np.arange(0, L0_HD, 2, dtype=np.float32) / np.float32(L0_HD)))).astype(np.float32)
    ang = np.arange(S, dtype=np.float32)[:, None] * inv_freq[None, :]
    cos = np.cos(ang).astype(np.float32)
    sin = np.sin(ang).astype(np.float32)
    return (np.ascontiguousarray(cos.T).astype(ml_dtypes.bfloat16),
            np.ascontiguousarray(sin.T).astype(ml_dtypes.bfloat16))


def _l1_rope_tables():
    half = L1_HD // 2
    inv_freq = (1.0 / (10000.0 ** (np.arange(half, dtype=np.float32) / np.float32(half)))).astype(np.float32)
    pos = np.arange(S, dtype=np.float32)
    freqs = np.outer(pos, inv_freq).astype(np.float32)
    cos = np.cos(freqs).astype(np.float32)
    sin = np.sin(freqs).astype(np.float32)
    return (np.ascontiguousarray(cos.T).astype(ml_dtypes.bfloat16),
            np.ascontiguousarray(sin.T).astype(ml_dtypes.bfloat16))


def _l0_bucket(d):
    NB, MAXD = 32, 128
    ME = NB // 2
    if d < ME:
        return d
    nf = np.float32(max(d, 1))
    large = int(np.float32(np.log(nf / np.float32(ME))) /
                np.float32(math.log(MAXD / ME)) * np.float32(NB - ME)) + ME
    return min(large, NB - 1)


def _l0_onehot():
    oh = np.zeros((32, 1536), dtype=np.float32)
    for d in range(256):
        oh[_l0_bucket(d), 512 + d] += 1.0
        oh[31, 512 + d] -= 1.0
    return oh


def _mbig():
    k = np.arange(128)[:, None]
    u = np.arange(CBW)[None, :] - CB0
    return np.where(u >= k, 0.0, NEG).astype(np.float32)


def _identity_bf16():
    return np.eye(128, dtype=ml_dtypes.bfloat16)



class Sem:
    def __init__(self, handle):
        self.h = handle
        self.n = 0

    def inc(self, instr, k=1):
        instr.then_inc(self.h, k)
        self.n += k
        return instr


class Ctx:

    def __init__(self, nc, block, es):
        self.nc = nc
        self.block = block
        def mk(name):
            return Sem(es.enter_context(nc.semaphore(name)))
        self.ld = mk("ld")
        self.ldg = mk("ldg")
        self.st = mk("st")
        self.mm = mk("mm")
        self.act = mk("act")
        self.dve = mk("dve")
        self.pool = mk("pool")
        self.cc = mk("cc")
        self.cc2 = mk("cc2")
        self.ev = mk("ev")
        self.rs = [mk("rs0"), mk("rs1")]
        self.stp = [mk("stp0"), mk("stp1")]
        self.wgp = [mk("wgp0"), mk("wgp1")]
        self.pc = mk("pc")


def emit_ln_transpose(ctx, x_sb, xn_sb, ident, tr_ps, stat, xnT_sb,
                      xnT_dram=None, wait=None):
    block = ctx.block
    dve0, mm0, act0 = ctx.dve.n, ctx.mm.n, ctx.act.n

    sq = ctx.sq_scratch

    @block.scalar
    def _(scalar):
        if wait is not None:
            scalar.wait_ge(wait[0].h, wait[1])
        for t in range(NT):
            if t >= 1:
                scalar.wait_ge(ctx.dve.h, dve0 + t)
            ctx.act.inc(scalar.activation(
                out=sq[:, :], in_=x_sb[:, D * t:D * (t + 1)], func=AF.Square))

    @block.vector
    def _(vector):
        if wait is not None:
            vector.wait_ge(wait[0].h, wait[1])
        for t in range(NT):
            xt = x_sb[:, D * t:D * (t + 1)]
            vector.reduce_sum(stat[:, t:t + 1], xt, axis=AX.X)
            vector.wait_ge(ctx.act.h, act0 + t + 1)
            ctx.dve.inc(vector.reduce_sum(stat[:, 8 + t:9 + t], sq[:, :], axis=AX.X))

    @block.vector
    def _(vector):
        for t in range(NT):
            ctx.dve.inc(vector.tensor_scalar(stat[:, 16 + t:17 + t],
                                             stat[:, t:t + 1],
                                             1.0 / D, None, ALU.mult))
            ctx.dve.inc(vector.tensor_scalar(stat[:, 24 + t:25 + t],
                                             stat[:, 8 + t:9 + t],
                                             1.0 / D, None, ALU.mult))
            vector.wait_ge(ctx.dve.h, ctx.dve.n)
            ctx.dve.inc(vector.tensor_tensor(stat[:, 48:49],
                                             stat[:, 16 + t:17 + t],
                                             stat[:, 16 + t:17 + t], ALU.mult))
            vector.wait_ge(ctx.dve.h, ctx.dve.n)
            ctx.dve.inc(vector.tensor_tensor(stat[:, 24 + t:25 + t],
                                             stat[:, 24 + t:25 + t],
                                             stat[:, 48:49], ALU.subtract))

    @block.scalar
    def _(scalar):
        for t in range(NT):
            scalar.wait_ge(ctx.dve.h, dve0 + NT + 4 * (t + 1))
            ctx.act.inc(scalar.activation(
                out=stat[:, 32 + t:33 + t], in_=stat[:, 24 + t:25 + t],
                func=AF.Sqrt, bias=1e-5, scale=1.0))

    @block.vector
    def _(vector):
        for t in range(NT):
            vector.wait_ge(ctx.act.h, act0 + NT + t + 1)
            ctx.dve.inc(vector.reciprocal(stat[:, 40 + t:41 + t],
                                          stat[:, 32 + t:33 + t]))
            vector.wait_ge(ctx.dve.h, ctx.dve.n)
            ctx.dve.inc(vector.tensor_scalar(
                xn_sb[:, D * t:D * (t + 1)], x_sb[:, D * t:D * (t + 1)],
                stat[:, 16 + t:17 + t], stat[:, 40 + t:41 + t],
                ALU.subtract, ALU.mult))

    ln_done = ctx.dve.n
    ev0 = ctx.ev.n

    @block.tensor
    def _(tensor):
        tensor.wait_ge(ctx.dve.h, ln_done)
        single = tr_ps[0] is tr_ps[1]
        i = 0
        for f in range(ND):
            for t in range(NT):
                if i >= (1 if single else 2):
                    tensor.wait_ge(ctx.ev.h, ev0 + (i if single else i - 1))
                ctx.mm.inc(tensor.transpose(
                    tr_ps[i % 2][:, :],
                    xn_sb[:, D * t + 128 * f:D * t + 128 * (f + 1)],
                    ident))
                i += 1

    @block.vector
    def _(vector):
        i = 0
        for f in range(ND):
            for t in range(NT):
                vector.wait_ge(ctx.mm.h, mm0 + i + 1)
                dst = xnT_sb[:, TOK * f + 128 * t:TOK * f + 128 * (t + 1)]
                ctx.ev.inc(vector.tensor_copy(dst, tr_ps[i % 2][:, :]))
                i += 1

    if xnT_dram is not None:
        @block.sync
        def _(sync):
            for f in range(ND):
                sync.wait_ge(ctx.ev.h, ev0 + NT * (f + 1))
                ctx.st.inc(sync.dma_start(
                    out=xnT_dram[128 * f:128 * (f + 1), :],
                    in_=xnT_sb[:, TOK * f:TOK * (f + 1)]), 16)
    return ctx.ev.n


import contextlib

RACE_DETECT = [False]


def _mk_bass():
    return bass.Bass("TRN2", target_bir_lowering=False, debug=False,
                     detect_race_conditions=RACE_DETECT[0])


def phase_ln(nc, ctx, es, x_ap, ident_ap, xnT_dram, sfx=""):
    block = ctx.block
    sqs = es.enter_context(nc.sbuf_tensor("sqs_ln" + sfx, [128, D], F32))
    ctx.sq_scratch = sqs[:, :]
    x_sb = es.enter_context(nc.sbuf_tensor("x_sb_ln" + sfx, [128, NT * D], F32))
    xn_sb = es.enter_context(nc.sbuf_tensor("xn_sb_ln" + sfx, [128, NT * D], BF16))
    xnT_sb = es.enter_context(nc.sbuf_tensor("xnT_sb_ln" + sfx, [128, ND * TOK], BF16))
    ident = es.enter_context(nc.sbuf_tensor("ident_ln" + sfx, [128, 128], BF16))
    stat = es.enter_context(nc.sbuf_tensor("stat_ln" + sfx, [128, 64], F32))
    tp0 = es.enter_context(nc.psum_tensor("tp0_ln" + sfx, [128, 128], BF16))
    tp1 = es.enter_context(nc.psum_tensor("tp1_ln" + sfx, [128, 128], BF16))
    ld0 = ctx.ld.n

    @block.sync
    def _(sync):
        ctx.ld.inc(sync.dma_start(
            out=x_sb[:, :].rearrange("p (t d) -> p t d", t=NT),
            in_=x_ap.rearrange("(t p) d -> p t d", p=128)), 16)
        ctx.ld.inc(sync.dma_start(out=ident[:, :], in_=ident_ap), 16)

    emit_ln_transpose(ctx, x_sb[:, :], xn_sb[:, :], ident[:, :],
                      (tp0, tp1), stat[:, :], xnT_sb[:, :],
                      xnT_dram=xnT_dram, wait=(ctx.ld, ctx.ld.n))


def build_ln_launch():
    nc = _mk_bass()
    x = nc.dram_tensor("x", [TOK, D], F32, kind="ExternalInput").ap()
    ident_in = nc.dram_tensor("ident", [128, 128], BF16, kind="ExternalInput").ap()
    consts_in = nc.dram_tensor("consts", [128, len(CONSTS)], F32, kind="ExternalInput").ap()
    xnT = nc.dram_tensor("xnT", [D, TOK], BF16, kind="ExternalOutput").ap()
    with contextlib.ExitStack() as es:
        csb = es.enter_context(nc.sbuf_tensor("csb", [128, len(CONSTS)], F32))
        register_consts(nc, csb)
        block = es.enter_context(nc.Block())
        ctx = Ctx(nc, block, es)

        @block.sync
        def _(sync):
            ctx.ld.inc(sync.dma_start(out=csb[:, :], in_=consts_in), 16)

        phase_ln(nc, ctx, es, x, ident_in, xnT)

        @block.sync
        def _(sync):
            sync.wait_ge(ctx.st.h, ctx.st.n)
    return nc


def _test_a0():
    import jax.numpy as jnp
    rng = np.random.default_rng(7)
    xs = [rng.standard_normal((TOK, D), dtype=np.float32) for _ in range(8)]
    nc = build_ln_launch()
    ident = np.eye(128, dtype=ml_dtypes.bfloat16)
    in_maps = [{"x": xs[c], "ident": ident, "consts": consts_host()} for c in range(8)]
    res = run_bass_kernel_spmd(nc, in_maps, core_ids=list(range(8)))
    for c in [0, 5]:
        x = xs[c]
        m = x.mean(axis=-1, keepdims=True)
        v = x.var(axis=-1, keepdims=True)
        ref = ((x - m) / np.sqrt(v + 1e-5)).astype(ml_dtypes.bfloat16).T
        got = res.results[c]["xnT"]
        diff = np.abs(got.astype(np.float32) - ref.astype(np.float32))
        rel = np.abs(diff).max()
        ok = rel < 0.05
        print(f"core{c} A0 ln+T: maxabs={rel:.4f} {'OK' if ok else 'FAIL'}")





class NS:
    def __init__(self, **kw):
        self.__dict__.update(kw)


NW = S // 512
PARANOID = [False]


def _emit_proj(ctx, bufs, xg, wlist, specs, rot):
    block = ctx.block
    if not callable(xg):
        _xgap = xg
        xg = lambda kt, w: _xgap[128 * kt:128 * (kt + 1), 512 * w:512 * (w + 1)]
    nm = len(wlist)
    rhs = (bufs.rhs0, bufs.rhs1, bufs.rhs2, bufs.rhs3)
    wsb = bufs.wsb
    qslot, vslot = 0, 0
    for sp in specs:
        for s in sp:
            if s.kind == "qk":
                s.slot = qslot
                qslot += 1
            else:
                s.slot = vslot
                vslot += 1

    def qraw_ap(s):
        return bufs.qraw[s.r0:s.r1, 512 * s.slot:512 * (s.slot + 1)]

    def qs_ap(s):
        return bufs.qs_sb[s.r0:s.r1, 512 * s.slot:512 * (s.slot + 1)]

    def vraw_ap(s, sc):
        return bufs.vraw[s.r0:s.r1, 512 * s.slot + 128 * sc:512 * s.slot + 128 * (sc + 1)]

    @block.sync
    def _(sync):
        sync.wait_ge(ctx.mm.h, ctx.mm.n)
        wg = getattr(ctx, "w_gate", None)
        if wg is not None:
            sync.wait_ge(wg[0].h, wg[1])
        for m, went in enumerate(wlist):
            parts = went if isinstance(went, list) else [(went, 0)]
            for wap, c0 in parts:
                cw = wap.shape[1]
                ctx.ld.inc(sync.dma_start(
                    out=wsb[:, 16 * 128 * m:16 * 128 * (m + 1)].rearrange(
                        "p (kt c) -> p kt c", kt=16)[:, :, c0:c0 + cw],
                    in_=wap.rearrange("(kt p) c -> p kt c", p=128)), 16)
    wready = ctx.ld.n

    mm0, act0, ev0 = ctx.mm.n, ctx.act.n, ctx.ev.n
    n_v = sum(1 for sp in specs for s in sp if s.kind == "v")
    n_qk = sum(1 for sp in specs for s in sp if s.kind == "qk")
    nsub = sum(len(sp) for sp in specs)
    mm_per_w = nm * ND + n_qk + n_v * 4
    act_per_w = nsub + n_qk

    rs_base = (ctx.rs[0].n, ctx.rs[1].n)

    def rhs_ld_after(i):
        return rs_base[i % 2] + 16 * (i // 2 + 1)

    def mm_after_kt(w, kt):
        return mm0 + w * mm_per_w + (kt + 1) * nm

    def mm_after_tail(w, j):
        return mm0 + w * mm_per_w + nm * ND + j + 1

    def act_after_copies(w):
        return act0 + act_per_w * w + nsub

    def act_after_w(w):
        return act0 + act_per_w * (w + 1)

    ev_per_w = n_qk + n_v * 4
    rs0_base = ctx.rs[0].n

    @block.sync
    def _(sync):
        rg = getattr(ctx, "rhs_gate", None)
        if rg is not None:
            sync.wait_ge(rg[0].h, rg[1])
        for i in range(NW * ND):
            w, kt = divmod(i, ND)
            if i >= 4:
                pw, pkt = divmod(i - 4, ND)
                sync.wait_ge(ctx.mm.h, mm_after_kt(pw, pkt))
            ctx.rs[0].inc(sync.dma_start(
                out=rhs[i % 4][:, :], in_=xg(kt, w)), 16)

    dve_entry = ctx.dve.n

    @block.tensor
    def _(tensor):
        tensor.wait_ge(ctx.ld.h, wready)
        tensor.wait_ge(ctx.dve.h, dve_entry)
        tensor.wait_ge(ctx.act.h, act0)
        cumc = []
        tot = 0
        for sp in specs:
            tot += len(sp)
            cumc.append(tot)
        for w in range(NW):
            for kt in range(ND):
                i = w * ND + kt
                tensor.wait_ge(ctx.rs[0].h, rs0_base + 16 * (i + 1))
                for m in range(nm):
                    if w > 0 and kt == 0:
                        tensor.wait_ge(ctx.act.h,
                                       act0 + act_per_w * (w - 1) + cumc[m])
                    ctx.mm.inc(tensor.matmul(
                        bufs.pp[m][:, :],
                        wsb[:, 128 * (16 * m + kt):128 * (16 * m + kt + 1)],
                        rhs[i % 4][:, :],
                        start=(kt == 0), stop=(kt == ND - 1)))
            ci = 0
            qkj = 0
            for m, sp in enumerate(specs):
                for s in sp:
                    ci += 1
                    if s.kind != "qk":
                        continue
                    tensor.wait_ge(ctx.act.h, act0 + act_per_w * w + ci)
                    if qkj > 0:
                        tensor.wait_ge(ctx.act.h,
                                       act0 + act_per_w * w + nsub + qkj)
                    elif w > 0:
                        tensor.wait_ge(ctx.act.h, act_after_w(w - 1))
                    hd = s.r1 - s.r0
                    ctx.mm.inc(tensor.matmul(
                        bufs.psh[s.r0:s.r1, :], rot[0:hd, s.r0:s.r1],
                        qraw_ap(s), start=True, stop=True))
                    qkj += 1
            ci = 0
            vj = 0
            for m, sp in enumerate(specs):
                for s in sp:
                    ci += 1
                    if s.kind != "v":
                        continue
                    tensor.wait_ge(ctx.act.h, act0 + act_per_w * w + ci)
                    hd = s.r1 - s.r0
                    for sc in range(4):
                        g = w * (n_v * 4) + vj
                        if g >= 1:
                            pw2, pj2 = divmod(g - 1, n_v * 4)
                            tensor.wait_ge(ctx.ev.h,
                                           ev0 + pw2 * ev_per_w + n_qk + pj2 + 1)
                        ctx.mm.inc(tensor.transpose(
                            bufs.tp[0:128, 0:hd],
                            vraw_ap(s, sc),
                            bufs.ident[s.r0:s.r1, s.r0:s.r1]))
                        vj += 1

    @block.scalar
    def _(scalar):
        for w in range(NW):
            for m, sp in enumerate(specs):
                scalar.wait_ge(ctx.mm.h, mm_after_kt(w, ND - 1) - (nm - m - 1))
                for s in sp:
                    if w > 0:
                        if s.kind == "qk":
                            scalar.wait_ge(ctx.ev.h, ev0 + ev_per_w * (w - 1) + n_qk)
                        else:
                            scalar.wait_ge(ctx.mm.h, mm0 + w * mm_per_w)
                    dst = (qraw_ap(s) if s.kind == "qk"
                           else bufs.vraw[s.r0:s.r1, 512 * s.slot:512 * (s.slot + 1)])
                    ctx.act.inc(scalar.activation(
                        out=dst, in_=bufs.pp[m][s.r0:s.r1, :], func=AF.Copy,
                        bias=0.0,
                        scale=(s.scale if (s.kind == "qk" and s.scale) else 1.0)))
            sj = 0
            for m, sp in enumerate(specs):
                for s in sp:
                    if s.kind != "qk":
                        continue
                    scalar.wait_ge(ctx.mm.h, mm_after_tail(w, sj))
                    if w > 0:
                        scalar.wait_ge(ctx.ev.h, ev0 + ev_per_w * (w - 1) + n_qk)
                    ctx.act.inc(scalar.activation(
                        out=qs_ap(s),
                        in_=bufs.psh[s.r0:s.r1, :], func=AF.Copy,
                        bias=0.0, scale=1.0))
                    sj += 1

    @block.vector
    def _(vector):
        for w in range(NW):
            qki = 0
            for m, sp in enumerate(specs):
                for s in sp:
                    if s.kind != "qk":
                        continue
                    vector.wait_ge(ctx.act.h,
                                   act0 + act_per_w * w + nsub + qki + 1)
                    cw = slice(512 * w, 512 * (w + 1))
                    rr = slice(s.r0, s.r1)
                    vector.tensor_tensor(bufs.tmp1[rr, :], qraw_ap(s),
                                         s.cosrows[rr, cw], ALU.mult)
                    ctx.dve.inc(vector.tensor_tensor(bufs.tmp2[rr, :], qs_ap(s),
                                                     s.sinrows[rr, cw], ALU.mult))
                    vector.wait_ge(ctx.dve.h, ctx.dve.n)
                    ctx.ev.inc(vector.tensor_tensor(
                        s.dst(w)[rr, :], bufs.tmp1[rr, :], bufs.tmp2[rr, :],
                        ALU.add))
                    qki += 1
            vj = 0
            for m, sp in enumerate(specs):
                for s in sp:
                    if s.kind != "v":
                        continue
                    hd = s.r1 - s.r0
                    vst = getattr(s, "vstep", hd)
                    for sc in range(4):
                        vector.wait_ge(ctx.mm.h, mm_after_tail(w, n_qk + vj))
                        slot = s.vslot0 + 4 * w + sc
                        ctx.ev.inc(vector.tensor_copy(
                            s.vtm[:, vst * slot:vst * slot + hd],
                            bufs.tp[0:128, 0:hd]))
                        vj += 1
    return ctx.ev.n


def _emit_norm_max(ctx, bufs, src_fn, hd, out_cell, wait_ev=None):
    block = ctx.block
    act0, mm0, dve0 = ctx.act.n, ctx.mm.n, ctx.dve.n
    prow = (bufs.pp[4], bufs.pp[5])

    @block.scalar
    def _(scalar):
        if wait_ev is not None:
            scalar.wait_ge(ctx.ev.h, wait_ev)
        for c in range(NW):
            if c >= 1:
                scalar.wait_ge(ctx.mm.h, mm0 + c)
            ctx.act.inc(scalar.activation(
                out=bufs.sqs[0:hd, :], in_=src_fn(c), func=AF.Square))

    ev_entry = ctx.ev.n

    @block.tensor
    def _(tensor):
        tensor.wait_ge(ctx.dve.h, dve0)
        tensor.wait_ge(ctx.ev.h, ev_entry)
        for c in range(NW):
            tensor.wait_ge(ctx.act.h, act0 + c + 1)
            if c >= 2:
                tensor.wait_ge(ctx.dve.h, dve0 + c - 1)
            ctx.mm.inc(tensor.matmul(
                prow[c % 2][0:1, :], bufs.onescol[0:hd, :], bufs.sqs[0:hd, :],
                start=True, stop=True))

    @block.vector
    def _(vector):
        for c in range(NW):
            vector.wait_ge(ctx.mm.h, mm0 + c + 1)
            ctx.dve.inc(vector.reduce_max(
                bufs.nrow[0:1, c:c + 1], prow[c % 2][0:1, :], axis=AX.X))
        vector.wait_ge(ctx.dve.h, ctx.dve.n)
        ctx.dve.inc(vector.reduce_max(out_cell, bufs.nrow[0:1, 0:NW], axis=AX.X))
        vector.wait_ge(ctx.dve.h, ctx.dve.n)


def _emit_bound(ctx, bufs, qcell, kcell, mb_cell, expb_dst):
    block = ctx.block
    nr = bufs.nrow

    @block.vector
    def _(vector):
        vector.wait_ge(ctx.dve.h, ctx.dve.n)
        ctx.dve.inc(vector.tensor_tensor(nr[0:1, 10:11], qcell, kcell, ALU.mult))
        vector.wait_ge(ctx.dve.h, ctx.dve.n)
    dve_t = ctx.dve.n

    @block.scalar
    def _(scalar):
        scalar.wait_ge(ctx.dve.h, dve_t)
        ctx.act.inc(scalar.activation(out=nr[0:1, 11:12], in_=nr[0:1, 10:11],
                                      func=AF.Sqrt, bias=0.0, scale=1.0404))
    act_b = ctx.act.n

    @block.vector
    def _(vector):
        vector.wait_ge(ctx.act.h, act_b)
        if mb_cell is not None:
            ctx.dve.inc(vector.tensor_tensor(nr[0:1, 11:12], nr[0:1, 11:12],
                                             mb_cell, ALU.add))
            vector.wait_ge(ctx.dve.h, ctx.dve.n)
        ctx.dve.inc(vector.tensor_scalar(nr[0:1, 12:13], nr[0:1, 11:12],
                                         -1.0, None, ALU.mult))
        vector.wait_ge(ctx.dve.h, ctx.dve.n)
    dve_nb = ctx.dve.n

    @block.tensor
    def _(tensor):
        tensor.wait_ge(ctx.dve.h, dve_nb)
        ctx.mm.inc(tensor.matmul(bufs.pp[5][:, 0:1], bufs.onesrow32[0:1, :],
                                 nr[0:1, 12:13], start=True, stop=True))
    mm_b = ctx.mm.n

    @block.vector
    def _(vector):
        vector.wait_ge(ctx.mm.h, mm_b)
        ctx.dve.inc(vector.tensor_copy(expb_dst, bufs.pp[5][:, 0:1]))
    return ctx.dve.n


def _emit_flash(ctx, bufs, lay, hd, kt_ap_fn, vtm, qt_fn, expb, cb, ctxT_dram,
                row0):
    block = ctx.block
    dep = 3
    pt = (bufs.ptile0, bufs.ptile1, bufs.ptile2)
    if lay == 1:
        s_ps = (bufs.pp[0], bufs.pp[1], bufs.pp[3])
        b_ps = (bufs.pp[4], bufs.pp[5])
    else:
        s_ps = (bufs.pp[0], bufs.pp[1], bufs.pp[4])
        b_ps = (bufs.pp[5], bufs.psh)
    o_ps = bufs.pp[2]
    d_ps = bufs.pp[3]
    ctile = (bufs.ctile0, bufs.ctile1)
    fold = 1 if lay == 1 else 0
    vstep = hd + fold
    oraw = bufs.oraw[ctx._fl_call % 2]
    hist = ctx._fl_hist

    items = []
    for qc in range(NW):
        for kt in range(4 * qc + 4):
            items.append((qc, kt))
    n_it = len(items)
    mm0, act0, dve0, ev0 = ctx.mm.n, ctx.act.n, ctx.dve.n, ctx.ev.n

    def is_near(qc, kt):
        if lay == 0:
            return kt >= 4 * qc - 1
        return kt >= 4 * qc

    mm_s_after, mm_pv_after, mm_d_after, mm_bc_after = {}, {}, {}, {}
    exp_after, evac_after, add_after, dencp_after = {}, {}, {}, {}
    mult_ev_after, st_after, recip_after = {}, {}, {}
    mm_c, act_c, dve_c, ev_c = mm0, act0, dve0, ev0
    for j in range(min(dep, n_it)):
        mm_c += 1
        mm_s_after[j] = mm_c
    for j, (qc, kt) in enumerate(items):
        if is_near(qc, kt):
            dve_c += 1
            add_after[j] = dve_c
        act_c += 1
        exp_after[j] = act_c
        mm_c += 1
        mm_pv_after[j] = mm_c
        if lay == 0:
            mm_c += 1
            mm_d_after[j] = mm_c
        if j + dep < n_it:
            mm_c += 1
            mm_s_after[j + dep] = mm_c
        if kt == 4 * qc + 3:
            act_c += 1
            evac_after[qc] = act_c
            dve_c += 1
            dencp_after[qc] = dve_c
    for qc in range(NW):
        mm_c += 1
        mm_bc_after[qc] = mm_c
        dve_c += 1
        recip_after[qc] = dve_c
        ev_c += 1
        mult_ev_after[qc] = ev_c
        st_after[qc] = ctx.stp[qc % 2].n + 16 * (qc // 2 + 1)

    @block.tensor
    def _(tensor):
        def emit_s(j):
            qc, kt = items[j]
            if j >= dep:
                tensor.wait_ge(ctx.act.h, exp_after[j - dep])
            else:
                tensor.wait_ge(ctx.act.h, act0)
                tensor.wait_ge(ctx.dve.h, dve0)
            ctx.mm.inc(tensor.matmul(
                s_ps[j % dep][:, :], kt_ap_fn(kt), qt_fn(qc),
                start=True, stop=True))

        for j in range(min(dep, n_it)):
            emit_s(j)
        for j, (qc, kt) in enumerate(items):
            tensor.wait_ge(ctx.act.h, exp_after[j])
            if kt == 0 and qc > 0:
                tensor.wait_ge(ctx.act.h, evac_after[qc - 1])
                tensor.wait_ge(ctx.dve.h, dencp_after[qc - 1])
            last = (kt == 4 * qc + 3)
            ctx.mm.inc(tensor.matmul(
                o_ps[0:hd + fold, :],
                vtm[:, vstep * kt:vstep * kt + hd + fold],
                pt[j % dep][:, :], start=(kt == 0), stop=last))
            if lay == 0:
                ctx.mm.inc(tensor.matmul(
                    d_ps[0:1, :], bufs.onescol[0:128, :], pt[j % dep][:, :],
                    start=(kt == 0), stop=last))
            if j + dep < n_it:
                emit_s(j + dep)
        for qc in range(NW):
            if qc == 0:
                tensor.wait_ge(ctx.act.h, act0)
            tensor.wait_ge(ctx.dve.h, dencp_after[qc])
            if qc >= 2:
                tensor.wait_ge(ctx.dve.h, recip_after[qc - 2])
            elif hist:
                tensor.wait_ge(ctx.dve.h, hist[-1]["recip"][6 + qc])
            ctx.mm.inc(tensor.matmul(
                b_ps[qc % 2][:, :], bufs.onesrow32[0:1, :],
                bufs.den_row[0:1, 512 * qc:512 * (qc + 1)],
                start=True, stop=True))

    @block.scalar
    def _(scalar):
        for j, (qc, kt) in enumerate(items):
            if j == 0:
                scalar.wait_ge(ctx.dve.h, dve0)
            if is_near(qc, kt):
                scalar.wait_ge(ctx.dve.h, add_after[j])
            else:
                scalar.wait_ge(ctx.mm.h, mm_s_after[j])
            if j >= dep:
                scalar.wait_ge(ctx.mm.h,
                               (mm_d_after if lay == 0 else mm_pv_after)[j - dep])
            ctx.act.inc(scalar.activation(
                out=pt[j % dep][:, :], in_=s_ps[j % dep][:, :], func=AF.Exp,
                bias=expb, scale=1.0))
            if kt == 4 * qc + 3:
                scalar.wait_ge(ctx.mm.h, mm_pv_after[j])
                if len(hist) >= 2:
                    scalar.wait_ge(ctx.ev.h, hist[-2]["mult_ev"][qc])
                ctx.act.inc(scalar.activation(
                    out=oraw[0:hd, 512 * qc:512 * (qc + 1)],
                    in_=o_ps[0:hd, :], func=AF.Copy, bias=0.0, scale=1.0))

    @block.vector
    def _(vector):
        for j, (qc, kt) in enumerate(items):
            if is_near(qc, kt):
                vector.wait_ge(ctx.mm.h, mm_s_after[j])
                cst = CB0 + 512 * qc - 128 * kt
                src = cb if cb is not None else bufs.mbig
                ctx.dve.inc(vector.tensor_tensor(
                    s_ps[j % dep][:, :], s_ps[j % dep][:, :],
                    src[:, cst:cst + 512], ALU.add))
            if kt == 4 * qc + 3:
                vector.wait_ge(ctx.mm.h,
                               (mm_pv_after if lay == 1 else mm_d_after)[j])
                ctx.dve.inc(vector.tensor_copy(
                    bufs.den_row[0:1, 512 * qc:512 * (qc + 1)],
                    o_ps[hd:hd + 1, :] if lay == 1 else d_ps[0:1, :]))
        for qc in range(NW):
            vector.wait_ge(ctx.mm.h, mm_bc_after[qc])
            ctx.dve.inc(vector.reciprocal(bufs.brec[qc % 2][0:hd, :],
                                          b_ps[qc % 2][0:hd, :]))
            vector.wait_ge(ctx.act.h, evac_after[qc])
            if qc >= 2:
                vector.wait_ge(ctx.stp[qc % 2].h, st_after[qc - 2])
            elif hist:
                vector.wait_ge(ctx.stp[qc % 2].h, hist[-1]["st"][6 + qc])
            ctx.ev.inc(vector.tensor_tensor(
                ctile[qc % 2][0:hd, :], oraw[0:hd, 512 * qc:512 * (qc + 1)],
                bufs.brec[qc % 2][0:hd, :], ALU.mult))

    @block.gpsimd
    def _(gpsimd):
        for qc in range(NW):
            gpsimd.wait_ge(ctx.ev.h, mult_ev_after[qc])
            ctx.stp[qc % 2].inc(gpsimd.dma_start(
                out=ctxT_dram[row0:row0 + hd, 512 * qc:512 * (qc + 1)],
                in_=ctile[qc % 2][0:hd, :]), 16)

    hist.append({"mult_ev": mult_ev_after, "st": st_after,
                 "recip": recip_after})
    if len(hist) > 2:
        hist.pop(0)
    ctx._fl_call += 1
    return None


L0SC = 1.0 / math.sqrt(L0_HD)
L1SC = 1.0 / math.sqrt(L1_HD)


def _rot_host(lay):
    if lay == 0:
        r = np.roll(np.eye(L0_HD, dtype=np.float32), L0_HD // 2, axis=0)
    else:
        r64 = np.roll(np.eye(64, dtype=np.float32), 32, axis=0)
        r = np.zeros((128, 128), dtype=np.float32)
        r[0:64, 0:64] = r64
        r[64:128, 64:128] = r64
    return r.astype(ml_dtypes.bfloat16)


def _l0_cos_sin_rows():
    c, s = _l0_rope_tables()
    cosr = np.concatenate([c, c], axis=0)
    sinr = np.concatenate([-s.astype(np.float32), s.astype(np.float32)],
                          axis=0).astype(ml_dtypes.bfloat16)
    return cosr, sinr


def _l1_cos_sin_rows():
    c, s = _l1_rope_tables()
    cosr = np.concatenate([c, c, c, c], axis=0)
    sf = s.astype(np.float32)
    sinr = np.concatenate([-sf, sf, -sf, sf], axis=0).astype(ml_dtypes.bfloat16)
    return cosr, sinr


def _alloc_attn_bufs(nc, es, lay):
    sb = lambda n, shp, dt: es.enter_context(nc.sbuf_tensor(n, shp, dt))[:, :]
    bufs = NS()
    bufs.rhs0 = sb("rhs0", [128, 512], BF16)
    bufs.rhs1 = sb("rhs1", [128, 512], BF16)
    bufs.rhs2 = sb("rhs2", [128, 512], BF16)
    bufs.rhs3 = sb("rhs3", [128, 512], BF16)
    bufs.wsb = sb("wsb", [128, 6 * 16 * 128], BF16)
    bufs.qraw = sb("qraw", [128, 4 * 512], BF16)
    bufs.qs_sb = sb("qs_sb", [128, 4 * 512], BF16)
    bufs.vraw = sb("vraw", [128, 2 * 512], BF16)
    bufs.tmp1 = sb("tmp1", [128, 512], BF16)
    bufs.tmp2 = sb("tmp2", [128, 512], BF16)
    bufs.sqs = sb("sqs", [128, 512], BF16)
    bufs.ptile0 = sb("ptile0", [128, 512], BF16)
    bufs.ptile1 = sb("ptile1", [128, 512], BF16)
    bufs.ptile2 = sb("ptile2", [128, 512], BF16)
    bufs.ctile0 = sb("ctile0", [128, 512], BF16)
    bufs.ctile1 = sb("ctile1", [128, 512], BF16)
    bufs.oraw = [sb("oraw0", [128, NW * 512], F32),
                 sb("oraw1", [128, NW * 512], F32)]
    bufs.den_row = sb("den_row", [1, NW * 512], F32)
    bufs.brec = [sb("brec0", [128, 512], F32), sb("brec1", [128, 512], F32)]
    bufs.dbgrow = sb("dbgrow", [1, 1024], F32)
    bufs.nrow = sb("nrow", [1, 64], F32)
    bufs.expb = sb("expb", [128, 8], F32)
    bufs.onescol = sb("onescol", [128, 1], BF16)
    bufs.onesrow32 = sb("onesrow32", [1, 128], F32)
    bufs.ident = sb("identsb", [128, 128], BF16)
    bufs.rot = sb("rotsb", [128, 128], BF16)
    bufs.cosr = sb("cosr_sb", [128, S], BF16)
    bufs.sinr = sb("sinr_sb", [128, S], BF16)
    bufs.mbig = sb("mbigsb", [128, CBW], F32)
    bufs.pp = [es.enter_context(nc.psum_tensor(f"pp{i}", [128, 512], F32))[:, :]
               for i in range(6)]
    bufs.tp = es.enter_context(nc.psum_tensor("tpps", [128, 128], BF16))[:, :]
    bufs.psh = es.enter_context(nc.psum_tensor("pshps", [128, 512], F32))[:, :]
    if lay == 0:
        bufs.QT = sb("QT", [128, 2 * S], BF16)
        bufs.KT = sb("KT", [128, 2 * S], BF16)
        bufs.VT = [sb("VT0", [128, 32 * 128], BF16), sb("VT1", [128, 32 * 128], BF16)]
        bufs.cb = [sb(f"cb{h}", [128, CBW], F32) for h in range(4)]
        bufs.vex = sb("vex", [4, 1536], F32)
        bufs.tbl = sb("tbl_sb", [32, 4], F32)
        bufs.oh = bufs.oraw[1][0:32, 0:1536]
        bufs.id16 = sb("id16_sb", [16, 16], F32)
        bufs.mbrow = sb("mbrow", [1, 4], F32)
        bufs.mbcol = sb("mbcol", [4, 1], F32)
        bufs.jrev = sb("jrev_sb", [128, 128], F32)
        bufs.cbtmp = bufs.oraw[0][:, 0:CBW]
    else:
        bufs.QT = sb("QT", [128, 2 * S], BF16)
        bufs.KT = sb("KT", [128, S], BF16)
        bufs.VT = [sb("VT0", [128, 32 * 64], BF16)]
        bufs.cb = None
    return bufs




def _dbg_dump(ctx, src_ap, dst_ap, wait=None):
    block = ctx.block

    @block.sync
    def _(sync):
        if wait is not None:
            kind, val = wait
            sem = {"ev": ctx.ev, "dve": ctx.dve, "act": ctx.act,
                   "mm": ctx.mm}[kind]
            sync.wait_ge(sem.h, val)
        ctx.st.inc(sync.dma_start(out=dst_ap, in_=src_ap), 16)


def _dbg_stage_attn0(ctx, bufs, ctxT, stage):
    if stage == 1.5:
        _dbg_dump(ctx, bufs.cb[0][:, 0:512], ctxT[0:128, 0:1024].bitcast(F32),
                  wait=("dve", ctx.dve.n))
        _dbg_dump(ctx, bufs.cb[3][:, 512:1024], ctxT[0:128, 1024:2048].bitcast(F32))
        _dbg_dump(ctx, bufs.mbrow, ctxT[384:385, 0:32].bitcast(F32))


def phase_attn0(nc, ctx, es, io):
    import dataclasses
    block = ctx.block
    sfx = getattr(io, "sfx", "")
    vext_d = nc.dram_tensor("vext_d" + sfx, [4, 1536], F32, kind="Internal")
    bufs = _alloc_attn_bufs(nc, es, 0)
    ctx.fdbg = None
    ctx.pdbg = None
    ctx._fl_hist = []
    ctx._fl_call = 0

    @block.sync
    def _(sync):
        for dst, srcap in [(bufs.mbig, io.mbig), (bufs.cosr, io.cosr),
                           (bufs.sinr, io.sinr), (bufs.ident, io.ident),
                           (bufs.id16, io.id16), (bufs.rot, io.rot),
                           (bufs.tbl, io.tbl), (bufs.oh, io.oh),
                           (bufs.jrev, io.jrev)]:
            ctx.ld.inc(sync.dma_start(out=dst, in_=srcap), 16)
    ld_pre = ctx.ld.n

    @block.vector
    def _(vector):
        vector.wait_ge(ctx.ld.h, ld_pre)
        vector.memset(bufs.onescol, 1.0)
        ctx.dve.inc(vector.memset(bufs.onesrow32, 1.0))
    dve_ones = ctx.dve.n

    @block.tensor
    def _(tensor):
        tensor.wait_ge(ctx.ld.h, ld_pre)
        tensor.wait_ge(ctx.dve.h, ctx.dve.n)
        for c in range(3):
            ctx.mm.inc(tensor.matmul(
                bufs.pp[c][0:4, :], bufs.tbl, bufs.oh[:, 512 * c:512 * (c + 1)],
                start=True, stop=True))
    mm_vex = ctx.mm.n

    @block.scalar
    def _(scalar):
        scalar.wait_ge(ctx.mm.h, mm_vex)
        for c in range(3):
            ctx.act.inc(scalar.activation(
                out=bufs.vex[0:4, 512 * c:512 * (c + 1)],
                in_=bufs.pp[c][0:4, :], func=AF.Copy, bias=0.0, scale=1.0))
    act_vex = ctx.act.n

    @block.sync
    def _(sync):
        sync.wait_ge(ctx.act.h, act_vex)
        ctx.st.inc(sync.dma_start(out=vext_d.ap(), in_=bufs.vex[0:4, :]), 16)
    st_vex = ctx.st.n

    @block.gpsimd
    def _(gpsimd):
        gpsimd.wait_ge(ctx.st.h, st_vex)
        for h in range(4):
            shifted = dataclasses.replace(
                vext_d.ap(), offset=1536 * h + 1, ap=[[1, 128], [1, CBW]])
            ctx.ldg.inc(gpsimd.dma_start(out=bufs.cb[h], in_=shifted), 16)
    ld_cb = ctx.ldg.n

    NCHUNK = [512, 512, 128]
    ctx._cb_dve0 = ctx.dve.n

    @block.tensor
    def _(tensor):
        tensor.wait_ge(ctx.ldg.h, ld_cb)
        for h in range(4):
            if h > 0:
                tensor.wait_ge(ctx.dve.h, ctx._cb_dve0 + h * 3)
            off = 0
            for ci, cn in enumerate(NCHUNK):
                ctx.mm.inc(tensor.matmul(
                    bufs.pp[ci][:, 0:cn], bufs.jrev,
                    bufs.cb[h][:, off:off + cn], start=True, stop=True))
                off += cn
    mm_cb0 = ctx.mm.n - 12

    @block.vector
    def _(vector):
        for h in range(4):
            off = 0
            for ci, cn in enumerate(NCHUNK):
                vector.wait_ge(ctx.mm.h, mm_cb0 + h * 3 + ci + 1)
                ctx.dve.inc(vector.tensor_tensor(
                    bufs.cb[h][:, off:off + cn], bufs.pp[ci][:, 0:cn],
                    bufs.mbig[:, off:off + cn], ALU.add))
                off += cn

        vector.wait_ge(ctx.act.h, act_vex)
        ctx.dve.inc(vector.reduce_max(bufs.mbcol[0:4, 0:1], bufs.vex[0:4, :],
                                      axis=AX.X))
    dve_mb = ctx.dve.n

    @block.tensor
    def _(tensor):
        tensor.wait_ge(ctx.dve.h, dve_mb)
        ctx.mm.inc(tensor.transpose(bufs.pp[3][0:1, 0:4],
                                    bufs.mbcol[0:4, 0:1], bufs.id16[0:4, 0:4]))
    mm_mb = ctx.mm.n

    @block.vector
    def _(vector):
        vector.wait_ge(ctx.mm.h, mm_mb)
        ctx.dve.inc(vector.tensor_scalar(bufs.mbrow, bufs.pp[3][0:1, 0:4],
                                         0.0, None, ALU.max))

    for p in range(2):
        wlist, specs = [], []
        for lh in range(2):
            l = 2 * p + lh
            for kind, wap in (("q", io.wq), ("k", io.wk), ("v", io.wv)):
                wlist.append(wap[:, 128 * l:128 * (l + 1)])
                if kind == "q":
                    specs.append([NS(kind="qk", r0=0, r1=128,
                                     dst=(lambda w, lh=lh: bufs.QT[:, S * lh + 512 * w:S * lh + 512 * (w + 1)]),
                                     scale=L0SC, cosrows=bufs.cosr,
                                     sinrows=bufs.sinr, vtm=None, vslot0=0)])
                elif kind == "k":
                    specs.append([NS(kind="qk", r0=0, r1=128,
                                     dst=(lambda w, lh=lh: bufs.KT[:, S * lh + 512 * w:S * lh + 512 * (w + 1)]),
                                     scale=None, cosrows=bufs.cosr,
                                     sinrows=bufs.sinr, vtm=None, vslot0=0)])
                else:
                    specs.append([NS(kind="v", r0=0, r1=128, dst=None,
                                     scale=None, cosrows=None, sinrows=None,
                                     vtm=bufs.VT[lh], vslot0=0)])
        ev_proj = _emit_proj(ctx, bufs, io.xg, wlist, specs, bufs.rot)
        for lh in range(2):
            l = 2 * p + lh
            _emit_norm_max(ctx, bufs,
                           (lambda c, lh=lh: bufs.QT[0:128, S * lh + 512 * c:S * lh + 512 * (c + 1)]),
                           128, bufs.nrow[0:1, 8:9], wait_ev=ev_proj)
            _emit_norm_max(ctx, bufs,
                           (lambda c, lh=lh: bufs.KT[0:128, S * lh + 512 * c:S * lh + 512 * (c + 1)]),
                           128, bufs.nrow[0:1, 9:10])
            _emit_bound(ctx, bufs, bufs.nrow[0:1, 8:9], bufs.nrow[0:1, 9:10],
                        bufs.mbrow[0:1, l:l + 1], bufs.expb[:, l:l + 1])
            _emit_flash(ctx, bufs, 0, 128,
                        (lambda kt, lh=lh: bufs.KT[0:128, S * lh + 128 * kt:S * lh + 128 * (kt + 1)]),
                        bufs.VT[lh],
                        (lambda qc, lh=lh: bufs.QT[0:128, S * lh + 512 * qc:S * lh + 512 * (qc + 1)]),
                        bufs.expb[:, l:l + 1], bufs.cb[l], io.ctxT, 128 * l)


def build_attn0_launch(stage=4):
    nc = _mk_bass()
    io = NS()
    io.xg = nc.dram_tensor("xg", [D, S], BF16, kind="ExternalInput").ap()
    io.wq = nc.dram_tensor("wq", [D, 512], F32, kind="ExternalInput").ap()
    io.wk = nc.dram_tensor("wk", [D, 512], F32, kind="ExternalInput").ap()
    io.wv = nc.dram_tensor("wv", [D, 512], F32, kind="ExternalInput").ap()
    io.tbl = nc.dram_tensor("tbl", [32, 4], F32, kind="ExternalInput").ap()
    io.oh = nc.dram_tensor("oh", [32, 1536], F32, kind="ExternalInput").ap()
    io.mbig = nc.dram_tensor("mbig", [128, CBW], F32, kind="ExternalInput").ap()
    io.cosr = nc.dram_tensor("cosr", [128, S], BF16, kind="ExternalInput").ap()
    io.sinr = nc.dram_tensor("sinr", [128, S], BF16, kind="ExternalInput").ap()
    io.ident = nc.dram_tensor("ident", [128, 128], BF16, kind="ExternalInput").ap()
    io.id16 = nc.dram_tensor("id16", [16, 16], F32, kind="ExternalInput").ap()
    io.rot = nc.dram_tensor("rot", [128, 128], BF16, kind="ExternalInput").ap()
    io.jrev = nc.dram_tensor("jrev", [128, 128], F32, kind="ExternalInput").ap()
    consts_in = nc.dram_tensor("consts", [128, len(CONSTS)], F32, kind="ExternalInput").ap()
    io.ctxT = nc.dram_tensor("ctxT", [512, S], BF16, kind="ExternalOutput").ap()
    with contextlib.ExitStack() as es:
        csb = es.enter_context(nc.sbuf_tensor("csb", [128, len(CONSTS)], F32))
        register_consts(nc, csb)
        block = es.enter_context(nc.Block())
        ctx = Ctx(nc, block, es)

        @block.sync
        def _(sync):
            ctx.ld.inc(sync.dma_start(out=csb[:, :], in_=consts_in), 16)

        phase_attn0(nc, ctx, es, io)

        @block.sync
        def _(sync):
            sync.wait_ge(ctx.st.h, ctx.st.n)
            sync.wait_ge(ctx.stp[0].h, ctx.stp[0].n)
            sync.wait_ge(ctx.stp[1].h, ctx.stp[1].n)
    return nc


def _test_a1(stage=4):
    import jax
    import jax.numpy as jnp
    rng = np.random.default_rng(3)
    xn = rng.standard_normal((S, D), dtype=np.float32) * 0.9
    xnT_bf = xn.T.astype(ml_dtypes.bfloat16)
    qkv_w = (rng.standard_normal((D, 3 * D), dtype=np.float32) * 0.02)
    table = (rng.standard_normal((32, 16), dtype=np.float32) * 0.1)

    nc = build_attn0_launch(stage)
    cosr, sinr = _l0_cos_sin_rows()
    in_maps = []
    for c in range(8):
        j = c % 4
        in_maps.append({
            "xg": xnT_bf,
            "wq": np.ascontiguousarray(qkv_w[:, 512 * j:512 * (j + 1)]),
            "wk": np.ascontiguousarray(qkv_w[:, D + 512 * j:D + 512 * (j + 1)]),
            "wv": np.ascontiguousarray(qkv_w[:, 2 * D + 512 * j:2 * D + 512 * (j + 1)]),
            "tbl": np.ascontiguousarray(table[:, 4 * j:4 * (j + 1)]), "oh": _l0_onehot(), "mbig": _mbig(),
            "cosr": cosr, "sinr": sinr,
            "ident": np.eye(128, dtype=ml_dtypes.bfloat16),
            "id16": np.eye(16, dtype=np.float32),
            "rot": _rot_host(0), "consts": consts_host(),
            "jrev": np.eye(128, dtype=np.float32)[::-1].copy(),
        })
    res = run_bass_kernel_spmd(nc, in_maps, core_ids=list(range(8)))

    cpu = jax.devices("cpu")[0]
    jax.config.update("jax_default_device", cpu)
    x = jnp.asarray(xn)[None]
    qkv = x @ jnp.asarray(qkv_w)
    q, k, v = jnp.split(qkv, 3, axis=-1)
    q = q.reshape(1, S, 16, 128)
    k = k.reshape(1, S, 16, 128)
    v = v.reshape(1, S, 16, 128)
    inv_freq = 1.0 / 10000.0 ** (jnp.arange(0, 128, 2, dtype=jnp.float32) / 128)
    ang = jnp.arange(S, dtype=jnp.float32)[:, None] * inv_freq[None, :]
    cos, sin = jnp.cos(ang), jnp.sin(ang)

    def rope(t):
        t1, t2 = t[..., :64], t[..., 64:]
        c = cos[None, :, None, :]
        s = sin[None, :, None, :]
        return jnp.concatenate([t1 * c - t2 * s, t2 * c + t1 * s], axis=-1)

    q, k = rope(q), rope(k)
    scl = 1.0 / math.sqrt(128.0)
    scores = jnp.einsum("bqhd,bkhd->bhqk", q, k) * scl
    ctxq = jnp.arange(S)[:, None]
    mem = jnp.arange(S)[None, :]
    n = jnp.maximum(ctxq - mem, 0)
    me = 16
    is_small = n < me
    nf = jnp.maximum(n, 1).astype(jnp.float32)
    large = (jnp.log(nf / me) / math.log(128 / me) * 16).astype(jnp.int32) + me
    bucket = jnp.where(is_small, n, jnp.minimum(large, 31))
    bias = jnp.asarray(table)[bucket]
    scores = scores + jnp.transpose(bias, (2, 0, 1))[None]
    causal = jnp.tril(jnp.ones((S, S), dtype=bool))
    scores = jnp.where(causal[None, None], scores, jnp.asarray(np.float32(-3.4e38)))
    probs = jax.nn.softmax(scores, axis=-1)
    ctx_ref = jnp.einsum("bhqk,bkhd->bqhd", probs, v)
    ctx_ref = np.asarray(ctx_ref)[0]

    qn = np.asarray(q)[0]
    kn = np.asarray(k)[0]
    vn = np.asarray(v)[0]
    for c in range(8):
        j = c % 4
        got = res.results[c]["ctxT"].astype(np.float32)
        if stage == 1.5:
            raw = res.results[c]["ctxT"]
            cb0 = np.frombuffer(raw[0:128, 0:1024].view(np.uint16).tobytes(),
                                dtype=np.float32).reshape(128, 512)
            vext = np.zeros(1536, dtype=np.float32)
            tblf = table.astype(np.float32)
            for dd in range(256):
                vext[512 + dd] = tblf[_l0_bucket(dd), 0] - tblf[31, 0]
            kk = np.arange(128)[:, None]
            cc = np.arange(512)[None, :]
            exp0 = vext[128 + cc - kk] + np.where((cc - CB0) >= kk, 0.0, NEG)
            mask = (cc - CBW // 2 - (CB0 - CBW // 2)) >= kk
            mask = (cc - CB0) >= kk
            d = np.abs(cb0 - exp0)
            print(f"core{c} cb0 unmasked: maxabs={d[mask].max():.4e} "
                  f"masked rel: {(d[~mask] / 1e30).max():.1e}")
            mrow = np.frombuffer(raw[384:385, 0:32].view(np.uint16).tobytes(),
                                 dtype=np.float32)[0:16]
            mexp = np.maximum(0.0, np.array([max(tblf[_l0_bucket(dd), h] - tblf[31, h]
                                for dd in range(256)) for h in range(16)]))
            print(f"core{c} mbrow err: {np.abs(mrow - mexp).max():.4e}")
            continue
        if stage < 3:
            qe = qn[:, 4 * j, :] * scl
            ke = kn[:, 4 * j, :]
            g = got[0:128].T
            print(f"core{c} QT: rel={np.linalg.norm(g - qe) / np.linalg.norm(qe):.4e}")
            g = got[128:256].T
            print(f"core{c} KT: rel={np.linalg.norm(g - ke) / np.linalg.norm(ke):.4e}")
            g256 = got[256:384]
            v_rec = np.concatenate([g256[:, 128 * t:128 * (t + 1)] for t in range(32)], axis=0)
            ve = vn[:, 4 * j, :]
            print(f"core{c} VT: rel={np.linalg.norm(v_rec - ve) / np.linalg.norm(ve):.4e}")
            continue
        if stage < 4:
            raw = res.results[c]["ctxT"]
            ex = np.frombuffer(raw[0:128, 0:16].view(np.uint16).tobytes(),
                               dtype=np.float32).reshape(128, 8)
            print(f"core{c} expb row0: {ex[0, 0:4]} row64: {ex[64, 0:4]}")
            nr = np.frombuffer(raw[200:201, 0:128].view(np.uint16).tobytes(),
                               dtype=np.float32)
            print(f"core{c} nrow chunks: {nr[0:8]}")
            print(f"core{c} q/k/t/B/nB: {nr[8]:.3f} {nr[9]:.3f} {nr[10]:.3f} {nr[11]:.3f} {nr[12]:.3f}")
            sq = raw[256:384, 0:512].astype(np.float32)
            qe = (qn[:, 4 * j, :] * scl)
            sq_exp = (qe[512 * 7:512 * 8].astype(ml_dtypes.bfloat16).astype(np.float32) ** 2).T
            print(f"core{c} sqs(last chunk K?) sample got={sq[0, 0:3]}")
            qmax_exp = ((qe.astype(ml_dtypes.bfloat16).astype(np.float32) ** 2).sum(-1)).max()
            kmax_exp = ((kn[:, 4 * j, :].astype(ml_dtypes.bfloat16).astype(np.float32) ** 2).sum(-1)).max()
            print(f"core{c} expected qmax={qmax_exp:.3f} kmax={kmax_exp:.3f} "
                  f"B={1.02 * math.sqrt(qmax_exp * kmax_exp):.3f}")
            continue
        for lh in ([0, 3] if c < 2 else [0]):
            g = got[128 * lh:128 * (lh + 1)].T
            ref = ctx_ref[:, 4 * j + lh, :]
            d = np.abs(g - ref)
            rel = np.linalg.norm(g - ref) / np.linalg.norm(ref)
            print(f"core{c} head{4 * j + lh}: rel_l2={rel:.4e} maxabs={d.max():.4f}")





def emit_outproj(ctx, bufs, ctxT_my, w_dram, x_sb, wait_ld=None):
    block = ctx.block

    @block.sync
    def _(sync):
        sync.wait_ge(ctx.mm.h, ctx.mm.n)
        if wait_ld is not None:
            sync.wait_ge(wait_ld[0].h, wait_ld[1])
        wg = getattr(ctx, "w_gate", None)
        if wg is not None:
            sync.wait_ge(wg[0].h, wg[1])
        ctx.ld.inc(sync.dma_start(
            out=bufs.wo_sb.rearrange("p (kt c) -> p kt c", kt=16),
            in_=w_dram.rearrange("(kt p) c -> p kt c", p=128)), 16)
    w_ready = ctx.ld.n

    rs_base = (ctx.rs[0].n, ctx.rs[1].n)
    mm0, act0, dve0, ev0 = ctx.mm.n, ctx.act.n, ctx.dve.n, ctx.ev.n

    def mm_after_t(t):
        return mm0 + 64 * (t + 1)

    def ev_after_t(t):
        return ev0 + 4 * (t + 1)

    @block.sync
    def _(sync):
        def emit_loads(tok0):
            for t in range(NT):
                if t >= 2:
                    sync.wait_ge(ctx.mm.h, mm_after_t(t - 2))
                sync.dma_start(
                    out=bufs.cxt[t % 2].rearrange("p (kt c) -> p kt c", kt=16),
                    in_=ctxT_my[:, tok0 + 128 * t:tok0 + 128 * (t + 1)].rearrange(
                        "(kt p) c -> p kt c", p=128)).then_inc(
                    ctx.rs[t % 2].h, 16)

        og = getattr(ctx, "op_gate", None)
        if og is not None:
            sync.wait_ge(og[0].h, og[1])
        if getattr(ctx, "op_switch", False):
            pid = sync.partition_id()
            for bi in sync.Switch(pid, 8):
                emit_loads(TOK * (bi % 4))
        else:
            emit_loads(0)
        for t in range(NT):
            ctx.rs[t % 2].n += 16

    @block.tensor
    def _(tensor):
        tensor.wait_ge(ctx.ld.h, w_ready)
        tensor.wait_ge(ctx.dve.h, dve0)
        tensor.wait_ge(ctx.act.h, act0)
        for t in range(NT):
            tensor.wait_ge(ctx.rs[t % 2].h, rs_base[t % 2] + 16 * (t // 2 + 1))
            if t >= 1:
                tensor.wait_ge(ctx.ev.h, ev_after_t(t - 1))
            for kt in range(ND):
                for nch in range(4):
                    ctx.mm.inc(tensor.matmul(
                        bufs.pp[nch][:, :],
                        bufs.cxt[t % 2][:, 128 * kt:128 * (kt + 1)],
                        bufs.wo_sb[:, D * kt + 512 * nch:D * kt + 512 * (nch + 1)],
                        start=(kt == 0), stop=(kt == ND - 1)))

    def dv(t, k):
        return dve0 + 9 * t + k

    @block.vector
    def _(vector):
        for t in range(NT):
            vector.wait_ge(ctx.mm.h, mm_after_t(t))
            if t >= 1:
                vector.wait_ge(ctx.act.h, act0 + 2 * (t - 1) + 1)
            for nch in range(4):
                ctx.ev.inc(vector.tensor_copy(
                    bufs.y[:, 512 * nch:512 * (nch + 1)], bufs.pp[nch][:, :]))
            ctx.dve.inc(vector.reduce_sum(bufs.stat[:, t:t + 1], bufs.y,
                                          axis=AX.X))
            vector.wait_ge(ctx.act.h, act0 + 2 * t + 1)
            ctx.dve.inc(vector.reduce_sum(bufs.stat[:, 8 + t:9 + t], bufs.tmp,
                                          axis=AX.X))
            vector.wait_ge(ctx.dve.h, dv(t, 2))
            ctx.dve.inc(vector.tensor_scalar(bufs.stat[:, 16 + t:17 + t],
                                             bufs.stat[:, t:t + 1], 1.0 / D,
                                             None, ALU.mult))
            ctx.dve.inc(vector.tensor_scalar(bufs.stat[:, 24 + t:25 + t],
                                             bufs.stat[:, 8 + t:9 + t], 1.0 / D,
                                             None, ALU.mult))
            vector.wait_ge(ctx.dve.h, dv(t, 4))
            ctx.dve.inc(vector.tensor_tensor(bufs.stat[:, 48:49],
                                             bufs.stat[:, 16 + t:17 + t],
                                             bufs.stat[:, 16 + t:17 + t], ALU.mult))
            vector.wait_ge(ctx.dve.h, dv(t, 5))
            ctx.dve.inc(vector.tensor_tensor(bufs.stat[:, 24 + t:25 + t],
                                             bufs.stat[:, 24 + t:25 + t],
                                             bufs.stat[:, 48:49], ALU.subtract))
            vector.wait_ge(ctx.act.h, act0 + 2 * t + 2)
            ctx.dve.inc(vector.reciprocal(bufs.stat[:, 40 + t:41 + t],
                                          bufs.stat[:, 32 + t:33 + t]))
            vector.wait_ge(ctx.dve.h, dv(t, 7))
            ctx.dve.inc(vector.tensor_scalar(
                bufs.tmp, bufs.y, bufs.stat[:, 16 + t:17 + t],
                bufs.stat[:, 40 + t:41 + t], ALU.subtract, ALU.mult))
            vector.wait_ge(ctx.dve.h, dv(t, 8))
            ctx.dve.inc(vector.tensor_tensor(
                x_sb[:, D * t:D * (t + 1)], x_sb[:, D * t:D * (t + 1)],
                bufs.tmp, ALU.add))

    @block.scalar
    def _(scalar):
        for t in range(NT):
            scalar.wait_ge(ctx.ev.h, ev_after_t(t))
            if t >= 1:
                scalar.wait_ge(ctx.dve.h, dv(t - 1, 9))
            ctx.act.inc(scalar.activation(out=bufs.tmp, in_=bufs.y,
                                          func=AF.Square))
            scalar.wait_ge(ctx.dve.h, dv(t, 6))
            ctx.act.inc(scalar.activation(
                out=bufs.stat[:, 32 + t:33 + t], in_=bufs.stat[:, 24 + t:25 + t],
                func=AF.Sqrt, bias=1e-5, scale=1.0))
    return ctx.dve.n


def phase_outproj(nc, ctx, es, ctxT_my, w_ap, x_src, x_dst, sfx=""):
    block = ctx.block
    sb = lambda n, shp, dt: es.enter_context(nc.sbuf_tensor(n + sfx, shp, dt))[:, :]
    bufs = NS()
    bufs.wo_sb = sb("wo_sb", [128, 16 * D], BF16)
    bufs.cxt = [sb("cxt0", [128, 16 * 128], BF16), sb("cxt1", [128, 16 * 128], BF16)]
    bufs.y = sb("y_sb", [128, D], F32)
    bufs.tmp = sb("tmp_sb", [128, D], F32)
    bufs.stat = sb("stat", [128, 64], F32)
    bufs.pp = [es.enter_context(nc.psum_tensor(f"pp{i}" + sfx, [128, 512], F32))[:, :]
               for i in range(4)]
    x_sb = es.enter_context(nc.sbuf_tensor("x_sb" + sfx, [128, NT * D], F32))
    ld0 = ctx.ld.n

    @block.sync
    def _(sync):
        ctx.ld.inc(sync.dma_start(
            out=x_sb[:, :].rearrange("p (t d) -> p t d", t=NT),
            in_=x_src.rearrange("(t p) d -> p t d", p=128)), 16)

    emit_outproj(ctx, bufs, ctxT_my, w_ap, x_sb[:, :], wait_ld=(ctx.ld, ctx.ld.n))

    @block.sync
    def _(sync):
        sync.wait_ge(ctx.dve.h, ctx.dve.n)
        ctx.st.inc(sync.dma_start(
            out=x_dst.rearrange("(t p) d -> p t d", p=128),
            in_=x_sb[:, :].rearrange("p (t d) -> p t d", t=NT)), 16)


def build_outproj_launch():
    nc = _mk_bass()
    x_in = nc.dram_tensor("x", [TOK, D], F32, kind="ExternalInput").ap()
    ctxT_my = nc.dram_tensor("ctxT_my", [D, TOK], BF16, kind="ExternalInput").ap()
    w_in = nc.dram_tensor("w", [D, D], F32, kind="ExternalInput").ap()
    consts_in = nc.dram_tensor("consts", [128, len(CONSTS)], F32, kind="ExternalInput").ap()
    xo = nc.dram_tensor("xo", [TOK, D], F32, kind="ExternalOutput").ap()
    with contextlib.ExitStack() as es:
        csb = es.enter_context(nc.sbuf_tensor("csb", [128, len(CONSTS)], F32))
        register_consts(nc, csb)
        block = es.enter_context(nc.Block())
        ctx = Ctx(nc, block, es)

        @block.sync
        def _(sync):
            ctx.ld.inc(sync.dma_start(out=csb[:, :], in_=consts_in), 16)

        phase_outproj(nc, ctx, es, ctxT_my, w_in, x_in, xo)

        @block.sync
        def _(sync):
            sync.wait_ge(ctx.st.h, ctx.st.n)
    return nc


def _test_a2():
    rng = np.random.default_rng(11)
    x = rng.standard_normal((TOK, D), dtype=np.float32)
    cT = (rng.standard_normal((D, TOK), dtype=np.float32) * 0.5).astype(ml_dtypes.bfloat16)
    w = rng.standard_normal((D, D), dtype=np.float32) * 0.02
    nc = build_outproj_launch()
    in_maps = [{"x": x, "ctxT_my": cT, "w": w, "consts": consts_host()}
               for _ in range(8)]
    res = run_bass_kernel_spmd(nc, in_maps, core_ids=list(range(8)))
    y = cT.astype(np.float32).T @ w
    m = y.mean(-1, keepdims=True); v = y.var(-1, keepdims=True)
    ref = x + (y - m) / np.sqrt(v + 1e-5)
    for c in [0, 5]:
        got = res.results[c]["xo"]
        rel = np.linalg.norm(got - ref) / np.linalg.norm(ref)
        print(f"core{c} A2 outproj: rel={rel:.4e} maxabs={np.abs(got - ref).max():.4f}")


def _l1_cos_sin_interleaved():
    half = 32
    inv_freq = (1.0 / (10000.0 ** (np.arange(half, dtype=np.float32) / np.float32(half)))).astype(np.float32)
    pos = np.arange(S, dtype=np.float32)
    fr = np.outer(pos, inv_freq).astype(np.float32)
    c = np.cos(fr).astype(np.float32).T
    s = np.sin(fr).astype(np.float32).T
    cosr = np.repeat(c, 2, axis=0)
    sinr = np.empty((64, S), dtype=np.float32)
    sinr[0::2] = -s
    sinr[1::2] = s
    return (cosr.astype(ml_dtypes.bfloat16), sinr.astype(ml_dtypes.bfloat16))


def _rot_l1_host():
    r = np.zeros((128, 128), dtype=np.float32)
    for i in range(32):
        r[2 * i, 2 * i + 1] = 1.0
        r[2 * i + 1, 2 * i] = 1.0
    return r.astype(ml_dtypes.bfloat16)


def phase_attn1(nc, ctx, es, io):
    block = ctx.block
    sb = lambda n, shp, dt: es.enter_context(nc.sbuf_tensor(n + "_a1", shp, dt))[:, :]
    bufs = NS()
    bufs.rhs0 = sb("rhs0", [128, 512], BF16)
    bufs.rhs1 = sb("rhs1", [128, 512], BF16)
    bufs.rhs2 = sb("rhs2", [128, 512], BF16)
    bufs.rhs3 = sb("rhs3", [128, 512], BF16)
    bufs.wsb = sb("wsb", [128, 5 * 16 * 128], BF16)
    bufs.qraw = sb("qraw", [128, 5 * 512], BF16)
    bufs.qs_sb = sb("qs_sb", [128, 5 * 512], BF16)
    bufs.vraw = sb("vraw", [128, 512], BF16)
    bufs.tmp1 = sb("tmp1", [128, 512], BF16)
    bufs.tmp2 = sb("tmp2", [128, 512], BF16)
    bufs.sqs = sb("sqs", [128, 512], BF16)
    bufs.ptile0 = sb("ptile0", [128, 512], BF16)
    bufs.ptile1 = sb("ptile1", [128, 512], BF16)
    bufs.ptile2 = sb("ptile2", [128, 512], BF16)
    bufs.ctile0 = sb("ctile0", [128, 512], BF16)
    bufs.ctile1 = sb("ctile1", [128, 512], BF16)
    bufs.oraw = [sb("oraw0", [128, NW * 512], F32),
                 sb("oraw1", [128, NW * 512], F32)]
    bufs.den_row = sb("den_row", [1, NW * 512], F32)
    bufs.brec = [sb("brec0", [128, 512], F32), sb("brec1", [128, 512], F32)]
    bufs.dbgrow = sb("dbgrow", [1, 1024], F32)
    bufs.nrow = sb("nrow", [1, 64], F32)
    bufs.expb = sb("expb", [128, 8], F32)
    bufs.onescol = sb("onescol", [128, 1], BF16)
    bufs.onesrow32 = sb("onesrow32", [1, 128], F32)
    bufs.ident = sb("identsb", [128, 128], BF16)
    bufs.rot = sb("rotsb", [128, 128], BF16)
    bufs.cosr = sb("cosr_sb", [64, S], BF16)
    bufs.sinr = sb("sinr_sb", [64, S], BF16)
    bufs.mbig = sb("mbigsb", [128, CBW], F32)
    bufs.QT = sb("QT", [64, 4 * S], BF16)
    bufs.KT = sb("KT", [64, S], BF16)
    bufs.VT = [sb("VT0", [128, 32 * 65], BF16)]
    bufs.cb = None
    bufs.pp = [es.enter_context(nc.psum_tensor(f"pp{i}_a1", [128, 512], F32))[:, :]
               for i in range(6)]
    bufs.tp = es.enter_context(nc.psum_tensor("tpps_a1", [128, 128], BF16))[:, :]
    bufs.psh = es.enter_context(nc.psum_tensor("pshps_a1", [128, 512], F32))[:, :]
    ctx.fdbg = None
    ctx.pdbg = None
    ctx._fl_hist = []
    ctx._fl_call = 0

    @block.sync
    def _(sync):
        for dst, srcap in [(bufs.mbig, io.mbig), (bufs.cosr, io.cosr),
                           (bufs.sinr, io.sinr), (bufs.ident, io.ident),
                           (bufs.rot, io.rot)]:
            ctx.ld.inc(sync.dma_start(out=dst, in_=srcap), 16)
    ld_pre = ctx.ld.n

    @block.vector
    def _(vector):
        vector.wait_ge(ctx.ld.h, ld_pre)
        vector.memset(bufs.onescol, 1.0)
        for k in range(32):
            vector.memset(bufs.VT[0][:, 65 * k + 64:65 * k + 65], 1.0)
        ctx.dve.inc(vector.memset(bufs.onesrow32, 1.0))

    for g in range(2):
        wlist = []
        specs = []
        for hh in range(4):
            wlist.append(io.wq[:, 64 * (4 * g + hh):64 * (4 * g + hh + 1)])
            specs.append([NS(kind="qk", r0=0, r1=64,
                             dst=(lambda w, hh=hh: bufs.QT[0:64, S * hh + 512 * w:S * hh + 512 * (w + 1)]),
                             scale=L1SC, cosrows=bufs.cosr,
                             sinrows=bufs.sinr, vtm=None, vslot0=0)])
        wlist.append([(io.wk[:, 64 * g:64 * (g + 1)], 0),
                      (io.wv[:, 64 * g:64 * (g + 1)], 64)])
        specs.append([NS(kind="qk", r0=0, r1=64,
                         dst=(lambda w: bufs.KT[0:64, 512 * w:512 * (w + 1)]),
                         scale=None, cosrows=bufs.cosr,
                         sinrows=bufs.sinr, vtm=None, vslot0=0),
                      NS(kind="v", r0=64, r1=128, dst=None, scale=None,
                         cosrows=None, sinrows=None, vtm=bufs.VT[0], vslot0=0,
                         vstep=65)])
        ev_proj = _emit_proj(ctx, bufs, io.xg, wlist, specs, bufs.rot)
        _emit_norm_max(ctx, bufs,
                       (lambda c: bufs.KT[0:64, 512 * c:512 * (c + 1)]),
                       64, bufs.nrow[0:1, 9:10], wait_ev=ev_proj)
        for hh in range(4):
            l = 4 * g + hh
            _emit_norm_max(ctx, bufs,
                           (lambda c, hh=hh: bufs.QT[0:64, S * hh + 512 * c:S * hh + 512 * (c + 1)]),
                           64, bufs.nrow[0:1, 8:9])
            _emit_bound(ctx, bufs, bufs.nrow[0:1, 8:9], bufs.nrow[0:1, 9:10],
                        None, bufs.expb[:, l:l + 1])
            _emit_flash(ctx, bufs, 1, 64,
                        (lambda kt: bufs.KT[0:64, 128 * kt:128 * (kt + 1)]),
                        bufs.VT[0],
                        (lambda qc, hh=hh: bufs.QT[0:64, S * hh + 512 * qc:S * hh + 512 * (qc + 1)]),
                        bufs.expb[:, l:l + 1], None, io.ctxT, 64 * l)


def build_attn1_launch():
    nc = _mk_bass()
    io = NS()
    io.xg = nc.dram_tensor("xg", [D, S], BF16, kind="ExternalInput").ap()
    io.wq = nc.dram_tensor("wq", [D, 512], F32, kind="ExternalInput").ap()
    io.wk = nc.dram_tensor("wk", [D, 128], F32, kind="ExternalInput").ap()
    io.wv = nc.dram_tensor("wv", [D, 128], F32, kind="ExternalInput").ap()
    io.mbig = nc.dram_tensor("mbig", [128, CBW], F32, kind="ExternalInput").ap()
    io.cosr = nc.dram_tensor("cosr", [64, S], BF16, kind="ExternalInput").ap()
    io.sinr = nc.dram_tensor("sinr", [64, S], BF16, kind="ExternalInput").ap()
    io.ident = nc.dram_tensor("ident", [128, 128], BF16, kind="ExternalInput").ap()
    io.rot = nc.dram_tensor("rot", [128, 128], BF16, kind="ExternalInput").ap()
    consts_in = nc.dram_tensor("consts", [128, len(CONSTS)], F32, kind="ExternalInput").ap()
    io.ctxT = nc.dram_tensor("ctxT", [512, S], BF16, kind="ExternalOutput").ap()
    with contextlib.ExitStack() as es:
        csb = es.enter_context(nc.sbuf_tensor("csb", [128, len(CONSTS)], F32))
        register_consts(nc, csb)
        block = es.enter_context(nc.Block())
        ctx = Ctx(nc, block, es)

        @block.sync
        def _(sync):
            ctx.ld.inc(sync.dma_start(out=csb[:, :], in_=consts_in), 16)

        phase_attn1(nc, ctx, es, io)

        @block.sync
        def _(sync):
            sync.wait_ge(ctx.st.h, ctx.st.n)
            sync.wait_ge(ctx.stp[0].h, ctx.stp[0].n)
            sync.wait_ge(ctx.stp[1].h, ctx.stp[1].n)
    return nc


def _test_b1():
    import jax
    jax.config.update("jax_default_device", jax.devices("cpu")[0])
    import jax.numpy as jnp
    rng = np.random.default_rng(5)
    xn = rng.standard_normal((S, D), dtype=np.float32) * 0.9
    xnT_bf = xn.T.astype(ml_dtypes.bfloat16)
    Wq = rng.standard_normal((D, D), dtype=np.float32) * 0.02
    Wk = rng.standard_normal((D, 512), dtype=np.float32) * 0.02
    Wv = rng.standard_normal((D, 512), dtype=np.float32) * 0.02
    nc = build_attn1_launch()
    cosr, sinr = _l1_cos_sin_interleaved()
    in_maps = []
    for c in range(8):
        j = c % 4
        in_maps.append({
            "xg": xnT_bf,
            "wq": np.ascontiguousarray(Wq[:, 512 * j:512 * (j + 1)]),
            "wk": np.ascontiguousarray(Wk[:, 128 * j:128 * (j + 1)]),
            "wv": np.ascontiguousarray(Wv[:, 128 * j:128 * (j + 1)]),
            "mbig": _mbig(), "cosr": cosr, "sinr": sinr,
            "ident": np.eye(128, dtype=ml_dtypes.bfloat16),
            "rot": _rot_l1_host(), "consts": consts_host(),
        })
    res = run_bass_kernel_spmd(nc, in_maps, core_ids=list(range(8)))

    x = jnp.asarray(xn)[None]
    q = (x @ jnp.asarray(Wq)).reshape(1, S, 32, 64).transpose(0, 2, 1, 3)
    k = (x @ jnp.asarray(Wk)).reshape(1, S, 8, 64).transpose(0, 2, 1, 3)
    v = (x @ jnp.asarray(Wv)).reshape(1, S, 8, 64).transpose(0, 2, 1, 3)
    half = 32
    inv_freq = 1.0 / 10000.0 ** (jnp.arange(half, dtype=jnp.float32) / half)
    pos = jnp.arange(S, dtype=jnp.float32)
    fr = jnp.outer(pos, inv_freq)
    cos, sin = jnp.cos(fr), jnp.sin(fr)

    def rope(t):
        e = t[..., ::2]; o = t[..., 1::2]
        c = cos[None, None]; s = sin[None, None]
        re = e * c - o * s
        ro = e * s + o * c
        return jnp.stack((re, ro), axis=-1).reshape(t.shape)

    q, k = rope(q), rope(k)
    kr = jnp.repeat(k, 4, axis=1)
    vr = jnp.repeat(v, 4, axis=1)
    sc = jnp.einsum("bhqd,bhkd->bhqk", q, kr) / jnp.sqrt(jnp.float32(64))
    causal = jnp.tril(jnp.ones((S, S), dtype=bool))
    sc = jnp.where(causal[None, None], sc, -jnp.finfo(jnp.float32).max)
    pr = jax.nn.softmax(sc.astype(jnp.float32), axis=-1)
    ctx_ref = np.asarray(jnp.einsum("bhqk,bhkd->bhqd", pr, vr))[0]

    for c in [0, 1]:
        j = c % 4
        got = res.results[c]["ctxT"].astype(np.float32)
        for l in [0, 5]:
            h = 8 * j + l
            g = got[64 * l:64 * (l + 1)].T
            ref = ctx_ref[h]
            rel = np.linalg.norm(g - ref) / np.linalg.norm(ref)
            print(f"core{c} qhead{h}: rel={rel:.4e} maxabs={np.abs(g - ref).max():.4f}")




def phase_ffn1(nc, ctx, es, io, x_src, x_dst):
    block = ctx.block
    sb = lambda n, shp, dt: es.enter_context(nc.sbuf_tensor(n + "_f1", shp, dt))[:, :]
    bufs = NS()
    x_sb = es.enter_context(nc.sbuf_tensor("x_sb_f1", [128, NT * D], F32))
    bufs.xnT = sb("xnT", [128, ND * TOK], BF16)
    bufs.fuseT = sb("fuseT", [128, 32 * 512], BF16)
    bufs.xn = bufs.fuseT
    bufs.wsb = sb("wsb", [128, 2 * 32 * 128], BF16)
    bufs.wdb = sb("wdb", [128, 3 * 16 * 128], BF16)
    bufs.yfm = sb("yfm", [128, 16 * 512], F32)
    bufs.y = sb("y_sb", [128, D], F32)
    bufs.tmp = sb("tmp_sb", [128, D], F32)
    bufs.gt = sb("gt", [128, 512], BF16)
    bufs.stat = sb("stat", [128, 64], F32)
    bufs.ident = sb("ident_sb", [128, 128], BF16)
    bufs.ident32 = sb("ident32_sb", [128, 128], F32)
    sgs = sb("sgs", [128, 64], F32)
    sus = sb("sus", [128, 64], F32)
    sds = sb("sds", [128, 16], F32)
    bufs.pg = [es.enter_context(nc.psum_tensor(f"pg{i}_f1", [128, 512], F32))[:, :] for i in range(2)]
    bufs.pu = [es.enter_context(nc.psum_tensor(f"pu{i}_f1", [128, 512], F32))[:, :] for i in range(2)]
    bufs.pd = [es.enter_context(nc.psum_tensor(f"pd{i}_f1", [128, 512], F32))[:, :] for i in range(2)]
    bufs.tp32 = es.enter_context(nc.psum_tensor("tp32_f1", [128, 128], F32))[:, :]
    bufs.tpL = es.enter_context(nc.psum_tensor("tpL_f1", [128, 128], BF16))
    bufs.tp32b = bufs.pd[0][:, 0:128]
    bufs.tpL2 = bufs.pd[1][:, 0:64].bitcast(BF16)

    @block.sync
    def _(sync):
        ctx.ld.inc(sync.dma_start(
            out=x_sb[:, :].rearrange("p (t d) -> p t d", t=NT),
            in_=x_src.rearrange("(t p) d -> p t d", p=128)), 16)
        for dst, srcap in [(bufs.ident, io.ident), (bufs.ident32, io.ident32),
                           (sgs, io.sg), (sus, io.su), (sds, io.sd)]:
            ctx.ld.inc(sync.dma_start(out=dst, in_=srcap), 16)
    ld_pre = ctx.ld.n

    ctx.sq_scratch = bufs.tmp
    emit_ffn1(ctx, bufs, x_sb[:, :], io.wg, io.wu, io.wd, sgs, sus, sds,
              wait=(ctx.ld, ld_pre), pcw=getattr(io, "pcw", None))

    @block.sync
    def _(sync):
        sync.wait_ge(ctx.dve.h, ctx.dve.n)
        ctx.st.inc(sync.dma_start(
            out=x_dst.rearrange("(t p) d -> p t d", p=128),
            in_=x_sb[:, :].rearrange("p (t d) -> p t d", t=NT)), 16)


def build_ffn1_launch():
    nc = _mk_bass()
    io = NS()
    x_in = nc.dram_tensor("x", [TOK, D], F32, kind="ExternalInput").ap()
    wg_i8 = nc.dram_tensor("wg", [D, 8192], I8, kind="ExternalInput").ap()
    wu_i8 = nc.dram_tensor("wu", [D, 8192], I8, kind="ExternalInput").ap()
    wd_i8 = nc.dram_tensor("wd", [8192, D], I8, kind="ExternalInput").ap()
    io.wg = nc.dram_tensor("wg_b", [D, 8192], BF16, kind="Internal").ap()
    io.wu = nc.dram_tensor("wu_b", [D, 8192], BF16, kind="Internal").ap()
    io.wd = nc.dram_tensor("wd_b", [8192, D], BF16, kind="Internal").ap()
    io.sg = nc.dram_tensor("sg", [128, 64], F32, kind="ExternalInput").ap()
    io.su = nc.dram_tensor("su", [128, 64], F32, kind="ExternalInput").ap()
    io.sd = nc.dram_tensor("sd", [128, 16], F32, kind="ExternalInput").ap()
    io.ident = nc.dram_tensor("ident", [128, 128], BF16, kind="ExternalInput").ap()
    io.ident32 = nc.dram_tensor("ident32", [128, 128], F32, kind="ExternalInput").ap()
    consts_in = nc.dram_tensor("consts", [128, len(CONSTS)], F32, kind="ExternalInput").ap()
    xo = nc.dram_tensor("xo", [TOK, D], F32, kind="ExternalOutput").ap()
    with contextlib.ExitStack() as es:
        csb = es.enter_context(nc.sbuf_tensor("csb", [128, len(CONSTS)], F32))
        register_consts(nc, csb)
        block = es.enter_context(nc.Block())
        ctx = Ctx(nc, block, es)

        @block.sync
        def _(sync):
            ctx.ld.inc(sync.dma_start(out=csb[:, :], in_=consts_in), 16)

        @block.gpsimd
        def _(gpsimd):
            for src_ap, dst_ap, rstep, nr in (
                    (wg_i8, io.wg, 512, D), (wu_i8, io.wu, 512, D),
                    (wd_i8, io.wd, 2048, 8192)):
                for r0 in range(0, nr, rstep):
                    ctx.pc.inc(gpsimd.dma_start(
                        out=dst_ap[r0:r0 + rstep, :],
                        in_=src_ap[r0:r0 + rstep, :]), 16)

        io.pcw = (ctx.pc, 192)
        phase_ffn1(nc, ctx, es, io, x_in, xo)

        @block.sync
        def _(sync):
            sync.wait_ge(ctx.st.h, ctx.st.n)
    return nc


def _test_f1():
    rng = np.random.default_rng(13)
    x = rng.standard_normal((TOK, D), dtype=np.float32)
    wg = rng.integers(-128, 128, size=(D, 8192)).astype(np.int8)
    wu = rng.integers(-128, 128, size=(D, 8192)).astype(np.int8)
    wd = rng.integers(-128, 128, size=(8192, D)).astype(np.int8)
    sg = (rng.random(8192, dtype=np.float32) * 0.02 + 0.001)
    su = (rng.random(8192, dtype=np.float32) * 0.02 + 0.001)
    sd = (rng.random(D, dtype=np.float32) * 0.02 + 0.001)
    nc = build_ffn1_launch()
    in_maps = [{
        "x": x, "wg": wg, "wu": wu, "wd": wd,
        "sg": np.ascontiguousarray(sg.reshape(64, 128).T),
        "su": np.ascontiguousarray(su.reshape(64, 128).T),
        "sd": np.ascontiguousarray(sd.reshape(16, 128).T),
        "ident": np.eye(128, dtype=ml_dtypes.bfloat16),
        "ident32": np.eye(128, dtype=np.float32),
        "consts": consts_host(),
    } for _ in range(8)]
    res = run_bass_kernel_spmd(nc, in_maps, core_ids=list(range(8)))
    m = x.mean(-1, keepdims=True); v = x.var(-1, keepdims=True)
    xn = (x - m) / np.sqrt(v + 1e-5)
    gate = xn @ wg.astype(np.float32) * sg
    gate = gate / (1.0 + np.exp(-gate))
    up = xn @ wu.astype(np.float32) * su
    y = (gate * up) @ wd.astype(np.float32) * sd
    m2 = y.mean(-1, keepdims=True); v2 = y.var(-1, keepdims=True)
    ref = x + (y - m2) / np.sqrt(v2 + 1e-5)
    for c in [0, 3]:
        got = res.results[c]["xo"]
        rel = np.linalg.norm(got - ref) / np.linalg.norm(ref)
        print(f"core{c} F1 ffn1: rel={rel:.4e} maxabs={np.abs(got - ref).max():.4f}")


def emit_ln_residual(ctx, bufs, x_sb, t, wait_y):
    block = ctx.block
    tt = t % 8
    act0, dve0 = ctx.act.n, ctx.dve.n

    @block.scalar
    def _(scalar):
        scalar.wait_ge(wait_y[0].h, wait_y[1])
        scalar.wait_ge(ctx.dve.h, dve0)
        ctx.act.inc(scalar.activation(out=bufs.tmp, in_=bufs.y, func=AF.Square))
        scalar.wait_ge(ctx.dve.h, dve0 + 6)
        ctx.act.inc(scalar.activation(
            out=bufs.stat[:, 32 + tt:33 + tt], in_=bufs.stat[:, 24 + tt:25 + tt],
            func=AF.Sqrt, bias=1e-5, scale=1.0))

    @block.vector
    def _(vector):
        vector.wait_ge(wait_y[0].h, wait_y[1])
        ctx.dve.inc(vector.reduce_sum(bufs.stat[:, tt:tt + 1], bufs.y, axis=AX.X))
        vector.wait_ge(ctx.act.h, act0 + 1)
        ctx.dve.inc(vector.reduce_sum(bufs.stat[:, 8 + tt:9 + tt], bufs.tmp,
                                      axis=AX.X))
        vector.wait_ge(ctx.dve.h, dve0 + 2)
        ctx.dve.inc(vector.tensor_scalar(bufs.stat[:, 16 + tt:17 + tt],
                                         bufs.stat[:, tt:tt + 1], 1.0 / D,
                                         None, ALU.mult))
        ctx.dve.inc(vector.tensor_scalar(bufs.stat[:, 24 + tt:25 + tt],
                                         bufs.stat[:, 8 + tt:9 + tt], 1.0 / D,
                                         None, ALU.mult))
        vector.wait_ge(ctx.dve.h, dve0 + 4)
        ctx.dve.inc(vector.tensor_tensor(bufs.stat[:, 48:49],
                                         bufs.stat[:, 16 + tt:17 + tt],
                                         bufs.stat[:, 16 + tt:17 + tt], ALU.mult))
        vector.wait_ge(ctx.dve.h, dve0 + 5)
        ctx.dve.inc(vector.tensor_tensor(bufs.stat[:, 24 + tt:25 + tt],
                                         bufs.stat[:, 24 + tt:25 + tt],
                                         bufs.stat[:, 48:49], ALU.subtract))
        vector.wait_ge(ctx.act.h, act0 + 2)
        ctx.dve.inc(vector.reciprocal(bufs.stat[:, 40 + tt:41 + tt],
                                      bufs.stat[:, 32 + tt:33 + tt]))
        vector.wait_ge(ctx.dve.h, dve0 + 7)
        ctx.dve.inc(vector.tensor_scalar(
            bufs.tmp, bufs.y, bufs.stat[:, 16 + tt:17 + tt],
            bufs.stat[:, 40 + tt:41 + tt], ALU.subtract, ALU.mult))
        vector.wait_ge(ctx.dve.h, dve0 + 8)
        ctx.dve.inc(vector.tensor_tensor(
            x_sb[:, D * t:D * (t + 1)], x_sb[:, D * t:D * (t + 1)],
            bufs.tmp, ALU.add))
    return ctx.act.n, ctx.dve.n


def emit_ffn1(ctx, bufs, x_sb, wg_d, wu_d, wd_d, sg_sb, su_sb, sd_sb, wait=None, pcw=None):
    block = ctx.block

    emit_ln_transpose(ctx, x_sb, bufs.xn, bufs.ident, (bufs.tpL, bufs.tpL2),
                      bufs.stat, bufs.xnT, xnT_dram=None, wait=wait)
    ln_ev = ctx.ev.n

    for tc in range(2):
        for half in range(2):
            mi0 = 32 * half
            wg_base = (ctx.wgp[0].n, ctx.wgp[1].n)
            mm0, act0, dve0 = ctx.mm.n, ctx.act.n, ctx.dve.n

            @block.sync
            def _(sync):
                if tc == 0 and half == 0 and pcw is not None:
                    sync.wait_ge(pcw[0].h, pcw[1])
                for i in range(32):
                    mi = mi0 + i
                    sb0 = 4096 * (i % 2)
                    sync.wait_ge(ctx.mm.h, mm0 + 32 * (i - 1) if i >= 2 else mm0)
                    ctx.wgp[0].inc(sync.dma_start(
                        out=bufs.wsb[:, sb0:sb0 + 16 * 128].rearrange(
                            "p (kt c) -> p kt c", kt=16),
                        in_=wg_d[:, 128 * mi:128 * (mi + 1)].rearrange(
                            "(kt p) c -> p kt c", p=128)), 16)
                    ctx.wgp[0].inc(sync.dma_start(
                        out=bufs.wsb[:, sb0 + 16 * 128:sb0 + 32 * 128].rearrange(
                            "p (kt c) -> p kt c", kt=16),
                        in_=wu_d[:, 128 * mi:128 * (mi + 1)].rearrange(
                            "(kt p) c -> p kt c", p=128)), 16)

            @block.tensor
            def _(tensor):
                tensor.wait_ge(ctx.ev.h, ln_ev)
                tensor.wait_ge(ctx.act.h, act0)
                tensor.wait_ge(ctx.dve.h, dve0)
                for i in range(32):
                    tensor.wait_ge(ctx.wgp[0].h, wg_base[0] + 32 * (i + 1))
                    if i >= 2:
                        tensor.wait_ge(ctx.act.h, act0 + 2 * (i - 1))
                    sb0 = 4096 * (i % 2)
                    for kt in range(ND):
                        ctx.mm.inc(tensor.matmul(
                            bufs.pg[i % 2][:, :],
                            bufs.wsb[:, sb0 + 128 * kt:sb0 + 128 * (kt + 1)],
                            bufs.xnT[:, TOK * kt + 512 * tc:TOK * kt + 512 * (tc + 1)],
                            start=(kt == 0), stop=(kt == ND - 1)))
                        ctx.mm.inc(tensor.matmul(
                            bufs.pu[i % 2][:, :],
                            bufs.wsb[:, sb0 + 128 * (16 + kt):sb0 + 128 * (17 + kt)],
                            bufs.xnT[:, TOK * kt + 512 * tc:TOK * kt + 512 * (tc + 1)],
                            start=(kt == 0), stop=(kt == ND - 1)))

            @block.scalar
            def _(scalar):
                for i in range(32):
                    mi = mi0 + i
                    scalar.wait_ge(ctx.mm.h, mm0 + 32 * (i + 1))
                    if i >= 1:
                        scalar.wait_ge(ctx.dve.h, dve0 + i)
                    ctx.act.inc(scalar.activation(
                        out=bufs.gt, in_=bufs.pg[i % 2][:, :], func=AF.Silu,
                        bias=0.0, scale=sg_sb[:, mi:mi + 1]))
                    ctx.act.inc(scalar.activation(
                        out=bufs.fuseT[:, 512 * i:512 * (i + 1)],
                        in_=bufs.pu[i % 2][:, :], func=AF.Copy,
                        bias=0.0, scale=su_sb[:, mi:mi + 1]))

            @block.vector
            def _(vector):
                for i in range(32):
                    vector.wait_ge(ctx.act.h, act0 + 2 * (i + 1))
                    ctx.dve.inc(vector.tensor_tensor(
                        bufs.fuseT[:, 512 * i:512 * (i + 1)], bufs.gt,
                        bufs.fuseT[:, 512 * i:512 * (i + 1)], ALU.mult))
            fuse_dve = ctx.dve.n

            wd_base = ctx.wgp[1].n
            mm1, act1, dve1 = ctx.mm.n, ctx.act.n, ctx.dve.n

            @block.sync
            def _(sync):
                for me in range(ND):
                    for h in range(2):
                        g = 2 * me + h
                        s = g % 3
                        if g >= 3:
                            pme, ph = divmod(g - 3, 2)
                            sync.wait_ge(ctx.mm.h,
                                         mm1 + 32 * pme + 16 * (ph + 1))
                        else:
                            sync.wait_ge(ctx.mm.h, mm0)
                        ctx.wgp[1].inc(sync.dma_start(
                            out=bufs.wdb[:, 2048 * s:2048 * (s + 1)].rearrange(
                                "p (kt c) -> p kt c", kt=16),
                            in_=wd_d[32 * 128 * half + 16 * 128 * h:
                                     32 * 128 * half + 16 * 128 * (h + 1),
                                     128 * me:128 * (me + 1)].rearrange(
                                "(kt p) c -> p kt c", p=128)), 16)

            @block.tensor
            def _(tensor):
                tensor.wait_ge(ctx.dve.h, fuse_dve)
                for me in range(ND):
                    if me >= 2:
                        tensor.wait_ge(ctx.dve.h, dve1 + me - 1)
                    for kt in range(32):
                        if kt % 16 == 0:
                            g = 2 * me + kt // 16
                            tensor.wait_ge(ctx.wgp[1].h,
                                           wd_base + 16 * (g + 1))
                            db0 = 2048 * (g % 3)
                        ctx.mm.inc(tensor.matmul(
                            bufs.pd[me % 2][:, :],
                            bufs.wdb[:, db0 + 128 * (kt % 16):
                                     db0 + 128 * (kt % 16 + 1)],
                            bufs.fuseT[:, 512 * kt:512 * (kt + 1)],
                            start=(kt == 0), stop=(kt == 31)))

            @block.vector
            def _(vector):
                for me in range(ND):
                    vector.wait_ge(ctx.mm.h, mm1 + 32 * (me + 1))
                    yslice = bufs.yfm[:, 512 * me:512 * (me + 1)]
                    if half == 0:
                        ctx.dve.inc(vector.tensor_scalar(
                            yslice, bufs.pd[me % 2][:, :], sd_sb[:, me:me + 1],
                            None, ALU.mult))
                    else:
                        ctx.dve.inc(vector.scalar_tensor_tensor(
                            yslice, bufs.pd[me % 2][:, :], sd_sb[:, me:me + 1],
                            yslice, ALU.mult, ALU.add))
            if half == 1:
                yfm_dve = ctx.dve.n

        if getattr(ctx, "f1dbg", None) is not None and tc == 0:
            @block.sync
            def _(sync):
                sync.wait_ge(ctx.dve.h, yfm_dve)
                ctx.st.inc(sync.dma_start(out=ctx.f1dbg, in_=bufs.yfm), 16)

        for st in range(4):
            t = 4 * tc + st
            mm2, ev2, dve2 = ctx.mm.n, ctx.ev.n, ctx.dve.n

            tpp = (bufs.tp32, bufs.tp32b)

            @block.tensor
            def _(tensor):
                tensor.wait_ge(ctx.dve.h, yfm_dve)
                tensor.wait_ge(ctx.ev.h, ev2)
                for me in range(ND):
                    if me >= 2:
                        tensor.wait_ge(ctx.ev.h, ev2 + me - 1)
                    ctx.mm.inc(tensor.transpose(
                        tpp[me % 2][:, :],
                        bufs.yfm[:, 512 * me + 128 * st:512 * me + 128 * (st + 1)],
                        bufs.ident32))

            @block.vector
            def _(vector):
                vector.wait_ge(ctx.act.h, ctx.act.n)
                for me in range(ND):
                    vector.wait_ge(ctx.mm.h, mm2 + me + 1)
                    ctx.ev.inc(vector.tensor_copy(
                        bufs.y[:, 128 * me:128 * (me + 1)], tpp[me % 2][:, :]))
            emit_ln_residual(ctx, bufs, x_sb, t, (ctx.ev, ctx.ev.n))
    return ctx.dve.n




def emit_ffn0(ctx, bufs, x_sb, w1_d, w2_d, b1_sb, b2_sb, wait=None, pcw=None):
    if not callable(w1_d):
        _w1 = w1_d
        w1_d = lambda mi: _w1[:, 128 * mi:128 * (mi + 1)]
    if not callable(w2_d):
        _w2 = w2_d
        w2_d = lambda half, me: _w2[32 * 128 * half:32 * 128 * (half + 1),
                                    128 * me:128 * (me + 1)]
    """L0 FFN (token-local): x += LN(gelu(LN(x) @ W1qT + b1) @ W2qT + b2).
    w1_d [D, 8192] bf16 DRAM (quantized, transposed), w2_d [8192, D] bf16.
    b1_sb [128, 64] f32, b2_sb [128, 16] f32 (host-transposed layouts).
    """
    block = ctx.block

    emit_ln_transpose(ctx, x_sb, bufs.xn, bufs.ident, (bufs.tpL, bufs.tpL2),
                      bufs.stat, bufs.xnT, xnT_dram=None, wait=wait)
    ln_ev = ctx.ev.n

    for tc in range(2):
        for half in range(2):
            mi0 = 32 * half
            wg_base = ctx.wgp[0].n
            mm0, act0, dve0 = ctx.mm.n, ctx.act.n, ctx.dve.n

            @block.sync
            def _(sync):
                if tc == 0 and half == 0 and pcw is not None:
                    sync.wait_ge(pcw[0].h, pcw[1])
                for i in range(32):
                    mi = mi0 + i
                    sb0 = 2048 * (i % 2)
                    sync.wait_ge(ctx.mm.h, mm0 + 16 * (i - 1) if i >= 2 else mm0)
                    ctx.wgp[0].inc(sync.dma_start(
                        out=bufs.wsb[:, sb0:sb0 + 16 * 128].rearrange(
                            "p (kt c) -> p kt c", kt=16),
                        in_=w1_d(mi).rearrange(
                            "(kt p) c -> p kt c", p=128)), 16)

            @block.tensor
            def _(tensor):
                tensor.wait_ge(ctx.ev.h, ln_ev)
                tensor.wait_ge(ctx.act.h, act0)
                tensor.wait_ge(ctx.dve.h, dve0)
                for i in range(32):
                    tensor.wait_ge(ctx.wgp[0].h, wg_base + 16 * (i + 1))
                    if i >= 2:
                        tensor.wait_ge(ctx.act.h, act0 + (i - 1))
                    sb0 = 2048 * (i % 2)
                    for kt in range(ND):
                        ctx.mm.inc(tensor.matmul(
                            bufs.pg[i % 2][:, :],
                            bufs.wsb[:, sb0 + 128 * kt:sb0 + 128 * (kt + 1)],
                            bufs.xnT[:, TOK * kt + 512 * tc:TOK * kt + 512 * (tc + 1)],
                            start=(kt == 0), stop=(kt == ND - 1)))

            @block.scalar
            def _(scalar):
                for i in range(32):
                    mi = mi0 + i
                    scalar.wait_ge(ctx.mm.h, mm0 + 16 * (i + 1))
                    if i >= 2:
                        pass
                    ctx.act.inc(scalar.activation(
                        out=bufs.fuseT[:, 512 * i:512 * (i + 1)],
                        in_=bufs.pg[i % 2][:, :], func=AF.Gelu,
                        bias=b1_sb[:, mi:mi + 1], scale=1.0))
            h_act = ctx.act.n

            wd_base = ctx.wgp[1].n
            mm1, act1, dve1 = ctx.mm.n, ctx.act.n, ctx.dve.n

            @block.sync
            def _(sync):
                for me in range(ND):
                    for h in range(2):
                        g = 2 * me + h
                        s = g % 3
                        if g >= 3:
                            pme, ph = divmod(g - 3, 2)
                            sync.wait_ge(ctx.mm.h,
                                         mm1 + 32 * pme + 16 * (ph + 1))
                        else:
                            sync.wait_ge(ctx.mm.h, mm0)
                        w2t = w2_d(half, me)
                        ctx.wgp[1].inc(sync.dma_start(
                            out=bufs.wdb[:, 2048 * s:2048 * (s + 1)].rearrange(
                                "p (kt c) -> p kt c", kt=16),
                            in_=w2t[16 * 128 * h:16 * 128 * (h + 1), :].rearrange(
                                "(kt p) c -> p kt c", p=128)), 16)

            @block.tensor
            def _(tensor):
                tensor.wait_ge(ctx.act.h, h_act)
                tensor.wait_ge(ctx.dve.h, dve1)
                for me in range(ND):
                    if me >= 2:
                        tensor.wait_ge(ctx.dve.h, dve1 + me - 1)
                    for kt in range(32):
                        if kt % 16 == 0:
                            g = 2 * me + kt // 16
                            tensor.wait_ge(ctx.wgp[1].h,
                                           wd_base + 16 * (g + 1))
                            db0 = 2048 * (g % 3)
                        ctx.mm.inc(tensor.matmul(
                            bufs.pd[me % 2][:, :],
                            bufs.wdb[:, db0 + 128 * (kt % 16):
                                     db0 + 128 * (kt % 16 + 1)],
                            bufs.fuseT[:, 512 * kt:512 * (kt + 1)],
                            start=(kt == 0), stop=(kt == 31)))

            @block.vector
            def _(vector):
                for me in range(ND):
                    vector.wait_ge(ctx.mm.h, mm1 + 32 * (me + 1))
                    yslice = bufs.yfm[:, 512 * me:512 * (me + 1)]
                    if half == 0:
                        ctx.dve.inc(vector.tensor_scalar(
                            yslice, bufs.pd[me % 2][:, :], b2_sb[:, me:me + 1],
                            None, ALU.add))
                    else:
                        ctx.dve.inc(vector.tensor_tensor(
                            yslice, bufs.pd[me % 2][:, :], yslice, ALU.add))
            if half == 1:
                yfm_dve = ctx.dve.n

        for st in range(4):
            t = 4 * tc + st
            mm2, ev2, dve2 = ctx.mm.n, ctx.ev.n, ctx.dve.n

            tpp = (bufs.tp32, bufs.tp32b)

            @block.tensor
            def _(tensor):
                tensor.wait_ge(ctx.dve.h, yfm_dve)
                tensor.wait_ge(ctx.ev.h, ev2)
                for me in range(ND):
                    if me >= 2:
                        tensor.wait_ge(ctx.ev.h, ev2 + me - 1)
                    ctx.mm.inc(tensor.transpose(
                        tpp[me % 2][:, :],
                        bufs.yfm[:, 512 * me + 128 * st:512 * me + 128 * (st + 1)],
                        bufs.ident32))

            @block.vector
            def _(vector):
                vector.wait_ge(ctx.act.h, ctx.act.n)
                for me in range(ND):
                    vector.wait_ge(ctx.mm.h, mm2 + me + 1)
                    ctx.ev.inc(vector.tensor_copy(
                        bufs.y[:, 128 * me:128 * (me + 1)], tpp[me % 2][:, :]))
            emit_ln_residual(ctx, bufs, x_sb, t, (ctx.ev, ctx.ev.n))
    return ctx.dve.n


def phase_quant(nc, ctx, es, io):
    block = ctx.block
    ar_in = nc.dram_tensor("ar_in_q", [1, 8], F32, kind="Internal")
    ar_out = nc.dram_tensor("ar_out_q", [1, 8], F32, kind="Internal")
    sb = lambda n, shp, dt: es.enter_context(nc.sbuf_tensor(n + "_q", shp, dt))[:, :]
    wt = [sb("wt0", [128, 8192], F32), sb("wt1", [128, 8192], F32)]
    qt = sb("qt", [128, 8192], F32)
    qb = sb("qb", [128, 8192], BF16)
    pc = [sb("pc0", [128, 512], BF16), sb("pc1", [128, 512], BF16)]
    acc = sb("acc", [128, 32], F32)
    row = sb("row", [1, 16], F32)
    rsb = sb("rsb", [128, 2], F32)
    onescol32 = sb("onescol32", [128, 1], F32)
    onesrow32 = sb("onesrow32", [1, 128], F32)
    ident = sb("ident_sb", [128, 128], BF16)
    ps = es.enter_context(nc.psum_tensor("ps_q", [128, 512], F32))[:, :]
    tpg = [es.enter_context(nc.psum_tensor(f"tpg{i}_q", [128, 512], BF16))[:, :]
           for i in range(2)]



    @block.sync
    def _(sync):
        ctx.ld.inc(sync.dma_start(out=ident, in_=io.ident), 16)
    ld_pre = ctx.ld.n

    @block.vector
    def _(vector):
        vector.wait_ge(ctx.ld.h, ld_pre)
        vector.memset(onescol32, 1.0)
        ctx.dve.inc(vector.memset(onesrow32, 1.0))

    tiles = ([("w1", i, io.w1s[128 * i:128 * (i + 1), :], 2048) for i in range(8)]
             + [("w2", i, io.w2s[128 * i:128 * (i + 1), :], 8192) for i in range(2)])
    ntr_before = [0]
    for _, _, _, wdt in tiles:
        ntr_before.append(ntr_before[-1] + wdt // 128)

    rs_base = (ctx.rs[0].n, ctx.rs[1].n)
    p1_dve0 = ctx.dve.n

    @block.sync
    def _(sync):
        for ti, (_, _, ap, wdt) in enumerate(tiles):
            if ti >= 2:
                sync.wait_ge(ctx.dve.h, p1_dve0 + ti - 1)
            ctx.rs[ti % 2].inc(sync.dma_start(out=wt[ti % 2][:, 0:wdt], in_=ap), 16)

    @block.vector
    def _(vector):
        for ti, (wn, i, _, wdt) in enumerate(tiles):
            vector.wait_ge(ctx.rs[ti % 2].h, rs_base[ti % 2] + 16 * (ti // 2 + 1))
            ctx.dve.inc(vector.tensor_reduce(
                acc[:, ti:ti + 1], wt[ti % 2][:, 0:wdt], AX.X, ALU.add,
                apply_absolute_value=True))
        vector.wait_ge(ctx.dve.h, ctx.dve.n)
        ctx.dve.inc(vector.tensor_reduce(acc[:, 16:17], acc[:, 0:8], AX.X, ALU.add))
        ctx.dve.inc(vector.tensor_reduce(acc[:, 17:18], acc[:, 8:10], AX.X, ALU.add))
        vector.wait_ge(ctx.dve.h, ctx.dve.n)
    sums_dve = ctx.dve.n

    @block.tensor
    def _(tensor):
        tensor.wait_ge(ctx.dve.h, sums_dve)
        ctx.mm.inc(tensor.matmul(ps[0:1, 0:2], onescol32, acc[:, 16:18],
                                 start=True, stop=True))
    mm_sum = ctx.mm.n

    @block.vector
    def _(vector):
        vector.wait_ge(ctx.mm.h, mm_sum)
        vector.memset(row[0:1, 2:8], 0.0)
        ctx.dve.inc(vector.tensor_copy(row[0:1, 0:2], ps[0:1, 0:2]))
        vector.wait_ge(ctx.dve.h, ctx.dve.n)
    row_dve = ctx.dve.n

    @block.sync
    def _(sync):
        sync.wait_ge(ctx.dve.h, row_dve)
        ctx.st.inc(sync.dma_start(out=ar_in.ap(), in_=row[0:1, 0:8]), 16)
    st_ar = ctx.st.n

    @block.gpsimd
    def _(gpsimd):
        gpsimd.wait_ge(ctx.st.h, st_ar)
        gpsimd.collective_compute(
            "AllReduce", ALU.add, replica_groups=[list(range(8))],
            ins=[ar_in.ap()], outs=[ar_out.ap()]).then_inc(ctx.cc.h, 1)
        ctx.cc.n += 1
        gpsimd.wait_ge(ctx.cc.h, ctx.cc.n)
        ctx.ldg.inc(gpsimd.dma_start(out=row[0:1, 8:16], in_=ar_out.ap()), 16)
    ar_ld = ctx.ldg.n

    @block.vector
    def _(vector):
        vector.wait_ge(ctx.ldg.h, ar_ld)
        ctx.dve.inc(vector.tensor_scalar(row[0:1, 2:3], row[0:1, 8:9],
                                         1.0 / (8192.0 * D), 1e-8,
                                         ALU.mult, ALU.max))
        ctx.dve.inc(vector.tensor_scalar(row[0:1, 3:4], row[0:1, 9:10],
                                         1.0 / (8192.0 * D), 1e-8,
                                         ALU.mult, ALU.max))
        vector.wait_ge(ctx.dve.h, ctx.dve.n)
        ctx.dve.inc(vector.reciprocal(row[0:1, 4:6], row[0:1, 2:4]))
        vector.wait_ge(ctx.dve.h, ctx.dve.n)
    rs_dve = ctx.dve.n

    @block.tensor
    def _(tensor):
        tensor.wait_ge(ctx.dve.h, rs_dve)
        ctx.mm.inc(tensor.matmul(ps[:, 0:2], onesrow32, row[0:1, 4:6],
                                 start=True, stop=True))
    mm_bc = ctx.mm.n

    @block.vector
    def _(vector):
        vector.wait_ge(ctx.mm.h, mm_bc)
        ctx.dve.inc(vector.tensor_copy(rsb[:, 0:2], ps[:, 0:2]))
        vector.wait_ge(ctx.dve.h, ctx.dve.n)
    rsb_dve = ctx.dve.n

    rs_base2 = (ctx.rs[0].n, ctx.rs[1].n)
    act0 = ctx.act.n
    dve0 = ctx.dve.n
    mm0 = ctx.mm.n
    ev0 = ctx.ev.n
    st0s = (ctx.stp[0].n, ctx.stp[1].n)

    @block.sync
    def _(sync):
        for ti, (_, _, ap, wdt) in enumerate(tiles):
            if ti < 2:
                sync.wait_ge(ctx.dve.h, sums_dve)
            else:
                sync.wait_ge(ctx.act.h, act0 + ti - 1)
            ctx.rs[ti % 2].inc(sync.dma_start(out=wt[ti % 2][:, 0:wdt], in_=ap), 16)

    @block.scalar
    def _(scalar):
        for ti, (wn, i, _, wdt) in enumerate(tiles):
            scalar.wait_ge(ctx.rs[ti % 2].h, rs_base2[ti % 2] + 16 * (ti // 2 + 1))
            scalar.wait_ge(ctx.dve.h, rsb_dve)
            if ti >= 1:
                scalar.wait_ge(ctx.dve.h, dve0 + 2 * ti)
            ridx = 0 if wn == "w1" else 1
            ctx.act.inc(scalar.activation(
                out=qt[:, 0:wdt], in_=wt[ti % 2][:, 0:wdt], func=AF.Copy,
                bias=MAGIC, scale=rsb[:, ridx:ridx + 1]))

    ngrp_before = [0]
    for _, _, _, wdt in tiles:
        ngrp_before.append(ngrp_before[-1] + wdt // 512)

    @block.tensor
    def _(tensor):
        g = 0
        for ti, (wn, i, _, wdt) in enumerate(tiles):
            tensor.wait_ge(ctx.dve.h, dve0 + 2 * (ti + 1))
            for gl in range(wdt // 512):
                if g >= 2:
                    tensor.wait_ge(ctx.ev.h, ev0 + g - 1)
                for a in range(4):
                    sc = 4 * gl + a
                    ctx.mm.inc(tensor.transpose(
                        tpg[g % 2][:, 128 * a:128 * (a + 1)],
                        qb[:, 128 * sc:128 * (sc + 1)], ident))
                g += 1

    @block.vector
    def _(vector):
        g = 0
        for ti, (wn, i, _, wdt) in enumerate(tiles):
            vector.wait_ge(ctx.act.h, act0 + ti + 1)
            ctx.dve.inc(vector.tensor_scalar(
                qt[:, 0:wdt], qt[:, 0:wdt], MAGIC, -1.0,
                ALU.subtract, ALU.max))
            vector.wait_ge(ctx.dve.h, ctx.dve.n)
            if ti >= 1:
                vector.wait_ge(ctx.mm.h, mm0 + 4 * ngrp_before[ti])
            ctx.dve.inc(vector.tensor_scalar(
                qb[:, 0:wdt], qt[:, 0:wdt], 1.0, None, ALU.min))
            for gl in range(wdt // 512):
                vector.wait_ge(ctx.mm.h, mm0 + 4 * (g + 1))
                if g >= 2:
                    vector.wait_ge(ctx.stp[g % 2].h,
                                   st0s[g % 2] + 16 * (g // 2))
                ctx.ev.inc(vector.tensor_copy(pc[g % 2], tpg[g % 2]))
                g += 1

    @block.gpsimd
    def _(gpsimd):
        g = 0
        for ti, (wn, i, _, wdt) in enumerate(tiles):
            for gl in range(wdt // 512):
                gpsimd.wait_ge(ctx.ev.h, ev0 + g + 1)
                if wn == "w1":
                    dst = io.w1qT[512 * gl:512 * (gl + 1),
                                  128 * i:128 * (i + 1)]
                else:
                    dst = io.w2qT[512 * gl:512 * (gl + 1),
                                  128 * i:128 * (i + 1)]
                ctx.stp[g % 2].inc(gpsimd.dma_start(
                    out=dst.rearrange("(a p) c -> p a c", p=128),
                    in_=pc[g % 2].rearrange("p (a c) -> p a c", a=4)), 16)
                g += 1
        gpsimd.wait_ge(ctx.stp[0].h, ctx.stp[0].n)
        gpsimd.wait_ge(ctx.stp[1].h, ctx.stp[1].n)


def build_quant_launch():
    nc = _mk_bass()
    io = NS()
    io.w1s = nc.dram_tensor("w1s", [1024, D], F32, kind="ExternalInput").ap()
    io.w2s = nc.dram_tensor("w2s", [256, 8192], F32, kind="ExternalInput").ap()
    io.ident = nc.dram_tensor("ident", [128, 128], BF16, kind="ExternalInput").ap()
    consts_in = nc.dram_tensor("consts", [128, len(CONSTS)], F32, kind="ExternalInput").ap()
    io.w1qT = nc.dram_tensor("w1qT", [D, 1024], BF16, kind="ExternalOutput").ap()
    io.w2qT = nc.dram_tensor("w2qT", [8192, 256], BF16, kind="ExternalOutput").ap()
    with contextlib.ExitStack() as es:
        csb = es.enter_context(nc.sbuf_tensor("csb", [128, len(CONSTS)], F32))
        register_consts(nc, csb)
        block = es.enter_context(nc.Block())
        ctx = Ctx(nc, block, es)

        @block.sync
        def _(sync):
            ctx.ld.inc(sync.dma_start(out=csb[:, :], in_=consts_in), 16)

        phase_quant(nc, ctx, es, io)

        @block.sync
        def _(sync):
            sync.wait_ge(ctx.stp[0].h, ctx.stp[0].n)
            sync.wait_ge(ctx.stp[1].h, ctx.stp[1].n)
    return nc


def phase_ffn0(nc, ctx, es, io, x_src, x_dst):
    block = ctx.block
    sb = lambda n, shp, dt: es.enter_context(nc.sbuf_tensor(n + "_f0", shp, dt))[:, :]
    bufs = NS()
    x_sb = es.enter_context(nc.sbuf_tensor("x_sb_f0", [128, NT * D], F32))
    bufs.xnT = sb("xnT", [128, ND * TOK], BF16)
    bufs.fuseT = sb("fuseT", [128, 32 * 512], BF16)
    bufs.xn = bufs.fuseT
    bufs.wsb = sb("wsb", [128, 2 * 16 * 128], BF16)
    bufs.wdb = sb("wdb", [128, 3 * 16 * 128], BF16)
    bufs.yfm = sb("yfm", [128, 16 * 512], F32)
    bufs.y = sb("y_sb", [128, D], F32)
    bufs.tmp = sb("tmp_sb", [128, D], F32)
    bufs.stat = sb("stat", [128, 64], F32)
    bufs.ident = sb("ident_sb", [128, 128], BF16)
    bufs.ident32 = sb("ident32_sb", [128, 128], F32)
    b1s = sb("b1s", [128, 64], F32)
    b2s = sb("b2s", [128, 16], F32)
    bufs.pg = [es.enter_context(nc.psum_tensor(f"pg{i}_f0", [128, 512], F32))[:, :] for i in range(2)]
    bufs.pd = [es.enter_context(nc.psum_tensor(f"pd{i}_f0", [128, 512], F32))[:, :] for i in range(2)]
    bufs.tp32 = es.enter_context(nc.psum_tensor("tp32_f0", [128, 128], F32))[:, :]
    bufs.tpL = es.enter_context(nc.psum_tensor("tpL_f0", [128, 128], BF16))
    bufs.tp32b = es.enter_context(nc.psum_tensor("tp32b_f0", [128, 128], F32))[:, :]
    bufs.tpL2 = es.enter_context(nc.psum_tensor("tpL2_f0", [128, 128], BF16))

    @block.sync
    def _(sync):
        ctx.ld.inc(sync.dma_start(
            out=x_sb[:, :].rearrange("p (t d) -> p t d", t=NT),
            in_=x_src.rearrange("(t p) d -> p t d", p=128)), 16)
        for dst, srcap in [(bufs.ident, io.ident), (bufs.ident32, io.ident32),
                           (b1s, io.b1t), (b2s, io.b2t)]:
            ctx.ld.inc(sync.dma_start(out=dst, in_=srcap), 16)
    ld_pre = ctx.ld.n

    ctx.sq_scratch = bufs.tmp
    emit_ffn0(ctx, bufs, x_sb[:, :], io.w1qT, io.w2qT, b1s, b2s,
              wait=(ctx.ld, ld_pre), pcw=getattr(io, "pcw", None))

    @block.sync
    def _(sync):
        sync.wait_ge(ctx.dve.h, ctx.dve.n)
        ctx.st.inc(sync.dma_start(
            out=x_dst.rearrange("(t p) d -> p t d", p=128),
            in_=x_sb[:, :].rearrange("p (t d) -> p t d", t=NT)), 16)


def build_ffn0_launch():
    nc = _mk_bass()
    io = NS()
    x_in = nc.dram_tensor("x", [TOK, D], F32, kind="ExternalInput").ap()
    io.w1qT = nc.dram_tensor("w1qT", [D, 8192], BF16, kind="ExternalInput").ap()
    io.w2qT = nc.dram_tensor("w2qT", [8192, D], BF16, kind="ExternalInput").ap()
    io.b1t = nc.dram_tensor("b1t", [128, 64], F32, kind="ExternalInput").ap()
    io.b2t = nc.dram_tensor("b2t", [128, 16], F32, kind="ExternalInput").ap()
    io.ident = nc.dram_tensor("ident", [128, 128], BF16, kind="ExternalInput").ap()
    io.ident32 = nc.dram_tensor("ident32", [128, 128], F32, kind="ExternalInput").ap()
    consts_in = nc.dram_tensor("consts", [128, len(CONSTS)], F32, kind="ExternalInput").ap()
    xo = nc.dram_tensor("xo", [TOK, D], F32, kind="ExternalOutput").ap()
    with contextlib.ExitStack() as es:
        csb = es.enter_context(nc.sbuf_tensor("csb", [128, len(CONSTS)], F32))
        register_consts(nc, csb)
        block = es.enter_context(nc.Block())
        ctx = Ctx(nc, block, es)

        @block.sync
        def _(sync):
            ctx.ld.inc(sync.dma_start(out=csb[:, :], in_=consts_in), 16)

        phase_ffn0(nc, ctx, es, io, x_in, xo)

        @block.sync
        def _(sync):
            sync.wait_ge(ctx.st.h, ctx.st.n)
    return nc


def _np_ternary(w):
    s = max(np.abs(w).mean(dtype=np.float64), 1e-8)
    return np.clip(np.round(w / np.float32(s)), -1, 1).astype(np.float32)


def _test_q0():
    rng = np.random.default_rng(17)
    w1 = (rng.standard_normal((8192, D), dtype=np.float32) * 0.02)
    w2 = (rng.standard_normal((D, 8192), dtype=np.float32) * 0.02)
    nc = build_quant_launch()
    in_maps = []
    for c in range(8):
        in_maps.append({
            "w1s": np.ascontiguousarray(w1[1024 * c:1024 * (c + 1)]),
            "w2s": np.ascontiguousarray(w2[256 * c:256 * (c + 1)]),
            "ident": np.eye(128, dtype=ml_dtypes.bfloat16),
            "consts": consts_host(),
        })
    res = run_bass_kernel_spmd(nc, in_maps, core_ids=list(range(8)))
    w1q = _np_ternary(w1)
    w2q = _np_ternary(w2)
    for c in [0, 6]:
        g1 = res.results[c]["w1qT"].astype(np.float32)
        e1 = w1q[1024 * c:1024 * (c + 1)].T
        g2 = res.results[c]["w2qT"].astype(np.float32)
        e2 = w2q[256 * c:256 * (c + 1)].T
        print(f"core{c} q0: w1 mismatches={int((g1 != e1).sum())} "
              f"w2 mismatches={int((g2 != e2).sum())}")


def _test_f0():
    rng = np.random.default_rng(19)
    x = rng.standard_normal((TOK, D), dtype=np.float32)
    w1 = (rng.standard_normal((8192, D), dtype=np.float32) * 0.02)
    w2 = (rng.standard_normal((D, 8192), dtype=np.float32) * 0.02)
    b1 = rng.standard_normal(8192).astype(np.float32) * 0.02
    b2 = rng.standard_normal(D).astype(np.float32) * 0.02
    w1q = _np_ternary(w1)
    w2q = _np_ternary(w2)
    nc = build_ffn0_launch()
    in_maps = [{
        "x": x,
        "w1qT": np.ascontiguousarray(w1q.T).astype(ml_dtypes.bfloat16),
        "w2qT": np.ascontiguousarray(w2q.T).astype(ml_dtypes.bfloat16),
        "b1t": np.ascontiguousarray(b1.reshape(64, 128).T),
        "b2t": np.ascontiguousarray(b2.reshape(16, 128).T),
        "ident": np.eye(128, dtype=ml_dtypes.bfloat16),
        "ident32": np.eye(128, dtype=np.float32),
        "consts": consts_host(),
    } for _ in range(8)]
    res = run_bass_kernel_spmd(nc, in_maps, core_ids=list(range(8)))
    import scipy.special as sps
    m = x.mean(-1, keepdims=True); v = x.var(-1, keepdims=True)
    xn = (x - m) / np.sqrt(v + 1e-5)
    h = xn @ w1q.T + b1
    h = h * 0.5 * (1.0 + sps.erf(h / np.sqrt(2.0)))
    y = h @ w2q.T + b2
    m2 = y.mean(-1, keepdims=True); v2 = y.var(-1, keepdims=True)
    ref = x + (y - m2) / np.sqrt(v2 + 1e-5)
    for c in [0, 4]:
        got = res.results[c]["xo"]
        rel = np.linalg.norm(got - ref) / np.linalg.norm(ref)
        print(f"core{c} F0 ffn0: rel={rel:.4e} maxabs={np.abs(got - ref).max():.4f}")



_NC_CACHE = {}


def _get_nc(name, builder):
    if name not in _NC_CACHE:
        _NC_CACHE[name] = builder()
    return _NC_CACHE[name]


def kernel_unfused(**inputs):
    f32 = np.float32
    x = np.ascontiguousarray(np.asarray(inputs["L0_attn_x"], dtype=f32).reshape(B * S, D))
    qkv_w = np.asarray(inputs["L0_attn_qkv_w"], dtype=f32)
    dense_w = np.ascontiguousarray(np.asarray(inputs["L0_attn_dense_w"], dtype=f32))
    table = np.asarray(inputs["L0_attn_rel_attn_table"], dtype=f32)
    w1 = np.asarray(inputs["L0_ffn_w1"], dtype=f32)
    b1 = np.asarray(inputs["L0_ffn_b1"], dtype=f32)
    w2 = np.asarray(inputs["L0_ffn_w2"], dtype=f32)
    b2 = np.asarray(inputs["L0_ffn_b2"], dtype=f32)
    Wq = np.asarray(inputs["L1_attn_Wq"], dtype=f32)
    Wk = np.asarray(inputs["L1_attn_Wk"], dtype=f32)
    Wv = np.asarray(inputs["L1_attn_Wv"], dtype=f32)
    Wo = np.ascontiguousarray(np.asarray(inputs["L1_attn_Wo"], dtype=f32))
    wgI = np.ascontiguousarray(np.asarray(inputs["L1_ffn_w_gate"], dtype=np.int8))
    wuI = np.ascontiguousarray(np.asarray(inputs["L1_ffn_w_up"], dtype=np.int8))
    wdI = np.ascontiguousarray(np.asarray(inputs["L1_ffn_w_down"], dtype=np.int8))
    sg = np.asarray(inputs["L1_ffn_s_gate"], dtype=f32)
    su = np.asarray(inputs["L1_ffn_s_up"], dtype=f32)
    sd = np.asarray(inputs["L1_ffn_s_down"], dtype=f32)

    CH = consts_host()
    EYE = np.eye(128, dtype=ml_dtypes.bfloat16)
    EYE32 = np.eye(128, dtype=np.float32)
    MB = _mbig()
    cos0, sin0 = _l0_cos_sin_rows()
    cos1, sin1 = _l1_cos_sin_interleaved()
    cores = list(range(NCORES))

    def xsl(c):
        return np.ascontiguousarray(x[TOK * c:TOK * (c + 1)])

    nc = _get_nc("quant", build_quant_launch)
    res = run_bass_kernel_spmd(nc, [{
        "w1s": np.ascontiguousarray(w1[1024 * c:1024 * (c + 1)]),
        "w2s": np.ascontiguousarray(w2[256 * c:256 * (c + 1)]),
        "ident": EYE, "consts": CH} for c in cores], core_ids=cores)
    w1qT = np.ascontiguousarray(np.concatenate(
        [res.results[c]["w1qT"] for c in cores], axis=1))
    w2qT = np.ascontiguousarray(np.concatenate(
        [res.results[c]["w2qT"] for c in cores], axis=1))

    def run_ln():
        nc = _get_nc("ln", build_ln_launch)
        res = run_bass_kernel_spmd(nc, [{
            "x": xsl(c), "ident": EYE, "consts": CH} for c in cores],
            core_ids=cores)
        xg = [np.ascontiguousarray(np.concatenate(
            [res.results[4 * b + k]["xnT"] for k in range(4)], axis=1))
            for b in range(2)]
        return xg

    def reshard_ctx(res):
        ctxf = [np.concatenate([res.results[4 * b + k]["ctxT"] for k in range(4)],
                               axis=0) for b in range(2)]
        return [np.ascontiguousarray(
            ctxf[c // 4][:, TOK * (c % 4):TOK * (c % 4 + 1)]) for c in cores]

    def run_outproj(ctx_my, w):
        nc = _get_nc("outproj", build_outproj_launch)
        res = run_bass_kernel_spmd(nc, [{
            "x": xsl(c), "ctxT_my": ctx_my[c], "w": w, "consts": CH}
            for c in cores], core_ids=cores)
        return np.concatenate([res.results[c]["xo"] for c in cores], axis=0)

    xg = run_ln()
    nc = _get_nc("attn0", build_attn0_launch)
    res = run_bass_kernel_spmd(nc, [{
        "xg": xg[c // 4],
        "wq": np.ascontiguousarray(qkv_w[:, 512 * (c % 4):512 * (c % 4 + 1)]),
        "wk": np.ascontiguousarray(qkv_w[:, D + 512 * (c % 4):D + 512 * (c % 4 + 1)]),
        "wv": np.ascontiguousarray(qkv_w[:, 2 * D + 512 * (c % 4):2 * D + 512 * (c % 4 + 1)]),
        "tbl": np.ascontiguousarray(table[:, 4 * (c % 4):4 * (c % 4 + 1)]),
        "oh": _l0_onehot(), "mbig": MB, "cosr": cos0, "sinr": sin0,
        "ident": EYE, "id16": np.eye(16, dtype=np.float32),
        "rot": _rot_host(0), "consts": CH, "jrev": EYE32[::-1].copy()}
        for c in cores], core_ids=cores)
    x = run_outproj(reshard_ctx(res), dense_w)

    nc = _get_nc("ffn0", build_ffn0_launch)
    res = run_bass_kernel_spmd(nc, [{
        "x": xsl(c), "w1qT": w1qT, "w2qT": w2qT,
        "b1t": np.ascontiguousarray(b1.reshape(64, 128).T),
        "b2t": np.ascontiguousarray(b2.reshape(16, 128).T),
        "ident": EYE, "ident32": EYE32, "consts": CH} for c in cores],
        core_ids=cores)
    x = np.concatenate([res.results[c]["xo"] for c in cores], axis=0)

    xg = run_ln()
    nc = _get_nc("attn1", build_attn1_launch)
    res = run_bass_kernel_spmd(nc, [{
        "xg": xg[c // 4],
        "wq": np.ascontiguousarray(Wq[:, 512 * (c % 4):512 * (c % 4 + 1)]),
        "wk": np.ascontiguousarray(Wk[:, 128 * (c % 4):128 * (c % 4 + 1)]),
        "wv": np.ascontiguousarray(Wv[:, 128 * (c % 4):128 * (c % 4 + 1)]),
        "mbig": MB, "cosr": cos1, "sinr": sin1,
        "ident": EYE, "rot": _rot_l1_host(), "consts": CH}
        for c in cores], core_ids=cores)
    x = run_outproj(reshard_ctx(res), Wo)

    nc = _get_nc("ffn1", build_ffn1_launch)
    res = run_bass_kernel_spmd(nc, [{
        "x": xsl(c), "wg": wgI, "wu": wuI, "wd": wdI,
        "sg": np.ascontiguousarray(sg.reshape(64, 128).T),
        "su": np.ascontiguousarray(su.reshape(64, 128).T),
        "sd": np.ascontiguousarray(sd.reshape(16, 128).T),
        "ident": EYE, "ident32": EYE32, "consts": CH} for c in cores],
        core_ids=cores)
    x = np.concatenate([res.results[c]["xo"] for c in cores], axis=0)

    return x.reshape(B, S, D)



def emit_fence(ctx):
    block = ctx.block
    sems = [ctx.ld, ctx.ldg, ctx.st, ctx.mm, ctx.act, ctx.dve, ctx.pool,
            ctx.cc, ctx.ev, ctx.rs[0], ctx.rs[1], ctx.stp[0], ctx.stp[1],
            ctx.wgp[0], ctx.wgp[1]]
    if getattr(ctx, "cc2", None) is not None:
        sems = sems + [ctx.cc2]
    vals = [(s.h, s.n) for s in sems]

    for mk in (block.sync, block.tensor, block.vector, block.scalar,
               block.gpsimd):
        @mk
        def _(eng):
            for h, v in vals:
                if v:
                    eng.wait_ge(h, v)


def emit_ag(ctx, in_ap, out_ap, groups, sem=None):
    block = ctx.block
    s = sem if sem is not None else ctx.cc

    @block.gpsimd
    def _(gpsimd):
        gpsimd.collective_compute(
            "AllGather", ALU.bypass, replica_groups=groups,
            ins=[in_ap], outs=[out_ap]).then_inc(s.h, 1)
        s.n += 1


def build_fused():
    nc = _mk_bass()
    io = NS()
    ein = lambda n, shp, dt: nc.dram_tensor(n, shp, dt, kind="ExternalInput").ap()
    x0 = ein("x0", [TOK, D], F32)
    a0 = NS()
    a0.wq = ein("a0wq", [D, 512], F32)
    a0.wk = ein("a0wk", [D, 512], F32)
    a0.wv = ein("a0wv", [D, 512], F32)
    a0.tbl = ein("a0tbl", [32, 4], F32)
    a0.oh = ein("a0oh", [32, 1536], F32)
    a0.mbig = ein("mbig", [128, CBW], F32)
    a0.cosr = ein("a0cosr", [128, S], BF16)
    a0.sinr = ein("a0sinr", [128, S], BF16)
    a0.ident = ein("ident", [128, 128], BF16)
    a0.id16 = ein("id16", [16, 16], F32)
    a0.rot = ein("a0rot", [128, 128], BF16)
    a0.jrev = ein("jrev", [128, 128], F32)
    dense_w = ein("dense_w", [D, D], F32)
    q = NS()
    q.w1s = ein("w1s", [1024, D], F32)
    q.w2s = ein("w2s", [256, 8192], F32)
    q.ident = a0.ident
    f0 = NS()
    f0.b1t = ein("b1t", [128, 64], F32)
    f0.b2t = ein("b2t", [128, 16], F32)
    f0.ident = a0.ident
    f0.ident32 = ein("ident32", [128, 128], F32)
    a1 = NS()
    a1.wq = ein("a1wq", [D, 512], F32)
    a1.wk = ein("a1wk", [D, 128], F32)
    a1.wv = ein("a1wv", [D, 128], F32)
    a1.mbig = a0.mbig
    a1.cosr = ein("a1cosr", [64, S], BF16)
    a1.sinr = ein("a1sinr", [64, S], BF16)
    a1.ident = a0.ident
    a1.rot = ein("a1rot", [128, 128], BF16)
    wo_w = ein("wo_w", [D, D], F32)
    f1 = NS()
    wg_i8 = ein("wg", [D, 8192], I8)
    wu_i8 = ein("wu", [D, 8192], I8)
    wd_i8 = ein("wd", [8192, D], I8)
    wg_b = nc.dram_tensor("wg_b", [D, 8192], BF16, kind="Internal").ap()
    wu_b = nc.dram_tensor("wu_b", [D, 8192], BF16, kind="Internal").ap()
    wd_b = nc.dram_tensor("wd_b", [8192, D], BF16, kind="Internal").ap()
    a0wq_b = nc.dram_tensor("a0wq_b", [D, 512], BF16, kind="Internal").ap()
    a0wk_b = nc.dram_tensor("a0wk_b", [D, 512], BF16, kind="Internal").ap()
    a0wv_b = nc.dram_tensor("a0wv_b", [D, 512], BF16, kind="Internal").ap()
    a1wq_b = nc.dram_tensor("a1wq_b", [D, 512], BF16, kind="Internal").ap()
    a1wk_b = nc.dram_tensor("a1wk_b", [D, 128], BF16, kind="Internal").ap()
    a1wv_b = nc.dram_tensor("a1wv_b", [D, 128], BF16, kind="Internal").ap()
    dense_b = nc.dram_tensor("dense_b", [D, D], BF16, kind="Internal").ap()
    wo_b = nc.dram_tensor("wo_b", [D, D], BF16, kind="Internal").ap()
    f1.wg = wg_b
    f1.wu = wu_b
    f1.wd = wd_b
    f1.sg = ein("sg", [128, 64], F32)
    f1.su = ein("su", [128, 64], F32)
    f1.sd = ein("sd", [128, 16], F32)
    f1.ident = a0.ident
    f1.ident32 = f0.ident32
    consts_in = ein("consts", [128, len(CONSTS)], F32)
    xo = nc.dram_tensor("xo", [TOK, D], F32, kind="ExternalOutput").ap()

    x_d = nc.dram_tensor("x_d", [TOK, D], F32, kind="Internal").ap()
    x_d2 = nc.dram_tensor("x_d2", [TOK, D], F32, kind="Internal").ap()
    xnT_in = nc.dram_tensor("xnT_in", [D, TOK], BF16, kind="Internal")
    xg_d = nc.dram_tensor("xg_d", [4 * D, TOK], BF16, kind="Internal")
    ctx_in = nc.dram_tensor("ctx_in", [512, S], BF16, kind="Internal")
    ctx_g = nc.dram_tensor("ctx_g", [D, S], BF16, kind="Internal")
    q.w1qT = nc.dram_tensor("w1qT_in", [D, 1024], BF16, kind="Internal").ap()
    q.w2qT = nc.dram_tensor("w2qT_in", [8192, 256], BF16, kind="Internal").ap()
    w1g = nc.dram_tensor("w1qT_g", [8 * D, 1024], BF16, kind="Internal")
    w2g = nc.dram_tensor("w2qT_g", [8 * 8192, 256], BF16, kind="Internal")

    def xg_fn(kt, w):
        return xg_d.ap()[D * (w // 2) + 128 * kt:D * (w // 2) + 128 * (kt + 1),
                         512 * (w % 2):512 * (w % 2 + 1)]

    def w1_fn(mi):
        return w1g.ap()[D * (mi // 8):D * (mi // 8 + 1),
                        128 * (mi % 8):128 * (mi % 8 + 1)]

    def w2_fn(half, me):
        base = 8192 * (me // 2) + 4096 * half
        return w2g.ap()[base:base + 4096, 128 * (me % 2):128 * (me % 2 + 1)]

    a0.xg = xg_fn
    a0.ctxT = ctx_in.ap()
    a0.sfx = "_a0"
    a0wq_f32, a0wk_f32, a0wv_f32 = a0.wq, a0.wk, a0.wv
    a0.wq, a0.wk, a0.wv = a0wq_b, a0wk_b, a0wv_b
    a1.xg = xg_fn
    a1.ctxT = ctx_in.ap()
    a1wq_f32, a1wk_f32, a1wv_f32 = a1.wq, a1.wk, a1.wv
    a1.wq, a1.wk, a1.wv = a1wq_b, a1wk_b, a1wv_b
    f0.w1qT = w1_fn
    f0.w2qT = w2_fn
    G4 = [[0, 1, 2, 3], [4, 5, 6, 7]]
    G8 = [list(range(8))]

    with contextlib.ExitStack() as es:
        csb = es.enter_context(nc.sbuf_tensor("csb", [128, len(CONSTS)], F32))
        register_consts(nc, csb)
        block = es.enter_context(nc.Block())
        ctx = Ctx(nc, block, es)
        ctx.op_switch = True

        @block.sync
        def _(sync):
            ctx.ld.inc(sync.dma_start(out=csb[:, :], in_=consts_in), 16)

        def fence():
            emit_fence(ctx)

        def precast(pairs):
            @block.gpsimd
            def _(gpsimd):
                for src_ap, dst_ap, rstep, nr in pairs:
                    for r0 in range(0, nr, rstep):
                        ctx.pc.inc(gpsimd.dma_start(
                            out=dst_ap[r0:r0 + rstep, :],
                            in_=src_ap[r0:r0 + rstep, :]), 16)

        precast([(a0wq_f32, a0wq_b, D, D), (a0wk_f32, a0wk_b, D, D),
                 (a0wv_f32, a0wv_b, D, D)])
        pc_a0 = ctx.pc.n

        with nc.named_scope("p1_quant"):
            with contextlib.ExitStack() as pes:
                phase_quant(nc, ctx, pes, q)
                fence()
        precast([(dense_w, dense_b, 512, D),
                 (a1wq_f32, a1wq_b, D, D), (a1wk_f32, a1wk_b, D, D),
                 (a1wv_f32, a1wv_b, D, D),
                 (wo_w, wo_b, 512, D),
                 (wg_i8, wg_b, 512, D), (wu_i8, wu_b, 512, D),
                 (wd_i8, wd_b, 2048, 8192)])
        pc_dense = pc_a0 + 4 * 16
        pc_a1 = pc_dense + 3 * 16
        pc_wo = pc_a1 + 4 * 16
        pc_f1 = ctx.pc.n
        with nc.named_scope("p2_ln"):
            with contextlib.ExitStack() as pes:
                phase_ln(nc, ctx, pes, x0, a0.ident, xnT_in.ap())
                fence()
        with nc.named_scope("p1_agw"):
            emit_ag(ctx, q.w1qT, w1g.ap(), G8, sem=ctx.cc2)
            emit_ag(ctx, q.w2qT, w2g.ap(), G8, sem=ctx.cc2)
        cc_agw = ctx.cc2.n
        with nc.named_scope("p2_ag"):
            emit_ag(ctx, xnT_in.ap(), xg_d.ap(), G4)
        cc_xg0 = ctx.cc.n

        with nc.named_scope("p3_attn0"):
            with contextlib.ExitStack() as pes:
                ctx.rhs_gate = (ctx.cc, cc_xg0)
                ctx.w_gate = (ctx.pc, pc_a0)
                phase_attn0(nc, ctx, pes, a0)
                ctx.rhs_gate = None
                ctx.w_gate = None
                fence()
        with nc.named_scope("p3_ag"):
            emit_ag(ctx, ctx_in.ap(), ctx_g.ap(), G4)
        cc_ctx0 = ctx.cc.n

        with nc.named_scope("p4_outproj"):
            with contextlib.ExitStack() as pes:
                ctx.op_gate = (ctx.cc, cc_ctx0)
                ctx.w_gate = (ctx.pc, pc_dense)
                phase_outproj(nc, ctx, pes, ctx_g.ap(), dense_b, x0, x_d, sfx="_p4")
                ctx.op_gate = None
                ctx.w_gate = None
                fence()

        f0.pcw = (ctx.cc2, cc_agw)
        with nc.named_scope("p5_ffn0"):
            with contextlib.ExitStack() as pes:
                phase_ffn0(nc, ctx, pes, f0, x_d, x_d2)
                fence()

        with nc.named_scope("p6_ln"):
            with contextlib.ExitStack() as pes:
                phase_ln(nc, ctx, pes, x_d2, a0.ident, xnT_in.ap(), sfx="_p6")
                fence()
        with nc.named_scope("p6_ag"):
            emit_ag(ctx, xnT_in.ap(), xg_d.ap(), G4)
        cc_xg1 = ctx.cc.n

        with nc.named_scope("p7_attn1"):
            with contextlib.ExitStack() as pes:
                ctx.rhs_gate = (ctx.cc, cc_xg1)
                ctx.w_gate = (ctx.pc, pc_a1)
                phase_attn1(nc, ctx, pes, a1)
                ctx.rhs_gate = None
                ctx.w_gate = None
                fence()
        with nc.named_scope("p7_ag"):
            emit_ag(ctx, ctx_in.ap(), ctx_g.ap(), G4)
        cc_ctx1 = ctx.cc.n

        with nc.named_scope("p8_outproj"):
            with contextlib.ExitStack() as pes:
                ctx.op_gate = (ctx.cc, cc_ctx1)
                ctx.w_gate = (ctx.pc, pc_wo)
                phase_outproj(nc, ctx, pes, ctx_g.ap(), wo_b, x_d2, x_d, sfx="_p8")
                ctx.op_gate = None
                ctx.w_gate = None
                fence()

        f1.pcw = (ctx.pc, pc_f1)
        with nc.named_scope("p9_ffn1"):
            with contextlib.ExitStack() as pes:
                phase_ffn1(nc, ctx, pes, f1, x_d, xo)
                fence()

        @block.sync
        def _(sync):
            sync.wait_ge(ctx.st.h, ctx.st.n)
            sync.wait_ge(ctx.stp[0].h, ctx.stp[0].n)
            sync.wait_ge(ctx.stp[1].h, ctx.stp[1].n)
    return nc


def kernel_fused(**inputs):
    f32 = np.float32
    x = np.ascontiguousarray(np.asarray(inputs["L0_attn_x"], dtype=f32).reshape(B * S, D))
    qkv_w = np.asarray(inputs["L0_attn_qkv_w"], dtype=f32)
    dense_w = np.ascontiguousarray(np.asarray(inputs["L0_attn_dense_w"], dtype=f32))
    table = np.asarray(inputs["L0_attn_rel_attn_table"], dtype=f32)
    w1 = np.asarray(inputs["L0_ffn_w1"], dtype=f32)
    b1 = np.asarray(inputs["L0_ffn_b1"], dtype=f32)
    w2 = np.asarray(inputs["L0_ffn_w2"], dtype=f32)
    b2 = np.asarray(inputs["L0_ffn_b2"], dtype=f32)
    Wq = np.asarray(inputs["L1_attn_Wq"], dtype=f32)
    Wk = np.asarray(inputs["L1_attn_Wk"], dtype=f32)
    Wv = np.asarray(inputs["L1_attn_Wv"], dtype=f32)
    Wo = np.ascontiguousarray(np.asarray(inputs["L1_attn_Wo"], dtype=f32))
    wgI = np.ascontiguousarray(np.asarray(inputs["L1_ffn_w_gate"], dtype=np.int8))
    wuI = np.ascontiguousarray(np.asarray(inputs["L1_ffn_w_up"], dtype=np.int8))
    wdI = np.ascontiguousarray(np.asarray(inputs["L1_ffn_w_down"], dtype=np.int8))
    sg = np.asarray(inputs["L1_ffn_s_gate"], dtype=f32)
    su = np.asarray(inputs["L1_ffn_s_up"], dtype=f32)
    sd = np.asarray(inputs["L1_ffn_s_down"], dtype=f32)

    CH = consts_host()
    EYE = np.eye(128, dtype=ml_dtypes.bfloat16)
    EYE32 = np.eye(128, dtype=np.float32)
    cos0, sin0 = _l0_cos_sin_rows()
    cos1, sin1 = _l1_cos_sin_interleaved()
    cores = list(range(NCORES))

    nc = _get_nc("fused", build_fused)
    in_maps = []
    for c in cores:
        j = c % 4
        in_maps.append({
            "x0": np.ascontiguousarray(x[TOK * c:TOK * (c + 1)]),
            "a0wq": np.ascontiguousarray(qkv_w[:, 512 * j:512 * (j + 1)]),
            "a0wk": np.ascontiguousarray(qkv_w[:, D + 512 * j:D + 512 * (j + 1)]),
            "a0wv": np.ascontiguousarray(qkv_w[:, 2 * D + 512 * j:2 * D + 512 * (j + 1)]),
            "a0tbl": np.ascontiguousarray(table[:, 4 * j:4 * (j + 1)]),
            "a0oh": _l0_onehot(), "mbig": _mbig(),
            "a0cosr": cos0, "a0sinr": sin0,
            "ident": EYE, "id16": np.eye(16, dtype=np.float32),
            "a0rot": _rot_host(0), "jrev": EYE32[::-1].copy(),
            "dense_w": dense_w,
            "w1s": np.ascontiguousarray(w1[1024 * c:1024 * (c + 1)]),
            "w2s": np.ascontiguousarray(w2[256 * c:256 * (c + 1)]),
            "b1t": np.ascontiguousarray(b1.reshape(64, 128).T),
            "b2t": np.ascontiguousarray(b2.reshape(16, 128).T),
            "ident32": EYE32,
            "a1wq": np.ascontiguousarray(Wq[:, 512 * j:512 * (j + 1)]),
            "a1wk": np.ascontiguousarray(Wk[:, 128 * j:128 * (j + 1)]),
            "a1wv": np.ascontiguousarray(Wv[:, 128 * j:128 * (j + 1)]),
            "a1cosr": cos1, "a1sinr": sin1, "a1rot": _rot_l1_host(),
            "wo_w": Wo,
            "wg": wgI, "wu": wuI, "wd": wdI,
            "sg": np.ascontiguousarray(sg.reshape(64, 128).T),
            "su": np.ascontiguousarray(su.reshape(64, 128).T),
            "sd": np.ascontiguousarray(sd.reshape(16, 128).T),
            "consts": CH,
        })
    res = run_bass_kernel_spmd(nc, in_maps, core_ids=cores)
    xout = np.concatenate([res.results[c]["xo"] for c in cores], axis=0)
    return xout.reshape(B, S, D)


def kernel(**inputs):
    return kernel_fused(**inputs)


# version 17
# speedup vs baseline: 1.5119x; 1.0183x over previous
import math
import numpy as np
import ml_dtypes
import concourse.bass as bass
import concourse.mybir as mybir
from concourse.bass_utils import run_bass_kernel_spmd

F32 = mybir.dt.float32
BF16 = mybir.dt.bfloat16
I8 = mybir.dt.int8
AF = mybir.ActivationFunctionType
ALU = mybir.AluOpType
AX = mybir.AxisListType

B, S, D = 2, 4096, 2048
NCORES = 8
TOK = (B * S) // NCORES
NT = TOK // 128
ND = D // 128
NEG = -1.0e30
MAGIC = 1.5 * 2.0 ** 23

L0_NH, L0_HD = 16, 128
L1_NHQ, L1_NHKV, L1_HD = 32, 8, 64

CBW = 1152
CB0 = 384

CONSTS = [1e-5, MAGIC, -MAGIC, 1.0, 0.5, -1.0, 1e-8, 0.0]


def consts_host():
    return np.tile(np.array(CONSTS, dtype=np.float32)[None, :], (128, 1))


def register_consts(nc, csb):
    for i, v in enumerate(CONSTS):
        nc.const_aps.register(F32, float(v), csb[:, i:i + 1])



def _l0_rope_tables():
    inv_freq = (1.0 / (10000.0 ** (np.arange(0, L0_HD, 2, dtype=np.float32) / np.float32(L0_HD)))).astype(np.float32)
    ang = np.arange(S, dtype=np.float32)[:, None] * inv_freq[None, :]
    cos = np.cos(ang).astype(np.float32)
    sin = np.sin(ang).astype(np.float32)
    return (np.ascontiguousarray(cos.T).astype(ml_dtypes.bfloat16),
            np.ascontiguousarray(sin.T).astype(ml_dtypes.bfloat16))


def _l1_rope_tables():
    half = L1_HD // 2
    inv_freq = (1.0 / (10000.0 ** (np.arange(half, dtype=np.float32) / np.float32(half)))).astype(np.float32)
    pos = np.arange(S, dtype=np.float32)
    freqs = np.outer(pos, inv_freq).astype(np.float32)
    cos = np.cos(freqs).astype(np.float32)
    sin = np.sin(freqs).astype(np.float32)
    return (np.ascontiguousarray(cos.T).astype(ml_dtypes.bfloat16),
            np.ascontiguousarray(sin.T).astype(ml_dtypes.bfloat16))


def _l0_bucket(d):
    NB, MAXD = 32, 128
    ME = NB // 2
    if d < ME:
        return d
    nf = np.float32(max(d, 1))
    large = int(np.float32(np.log(nf / np.float32(ME))) /
                np.float32(math.log(MAXD / ME)) * np.float32(NB - ME)) + ME
    return min(large, NB - 1)


def _l0_onehot():
    oh = np.zeros((32, 1536), dtype=np.float32)
    for d in range(256):
        oh[_l0_bucket(d), 512 + d] += 1.0
        oh[31, 512 + d] -= 1.0
    return oh


def _mbig():
    k = np.arange(128)[:, None]
    u = np.arange(CBW)[None, :] - CB0
    return np.where(u >= k, 0.0, NEG).astype(np.float32)


def _identity_bf16():
    return np.eye(128, dtype=ml_dtypes.bfloat16)



class Sem:
    def __init__(self, handle):
        self.h = handle
        self.n = 0

    def inc(self, instr, k=1):
        instr.then_inc(self.h, k)
        self.n += k
        return instr


class Ctx:

    def __init__(self, nc, block, es):
        self.nc = nc
        self.block = block
        def mk(name):
            return Sem(es.enter_context(nc.semaphore(name)))
        self.ld = mk("ld")
        self.ldg = mk("ldg")
        self.st = mk("st")
        self.mm = mk("mm")
        self.act = mk("act")
        self.dve = mk("dve")
        self.pool = mk("pool")
        self.cc = mk("cc")
        self.cc2 = mk("cc2")
        self.ev = mk("ev")
        self.rs = [mk("rs0"), mk("rs1")]
        self.stp = [mk("stp0"), mk("stp1")]
        self.wgp = [mk("wgp0"), mk("wgp1")]
        self.pc = mk("pc")


def emit_ln_transpose(ctx, x_sb, xn_sb, ident, tr_ps, stat, xnT_sb,
                      xnT_dram=None, wait=None):
    block = ctx.block
    dve0, mm0, act0 = ctx.dve.n, ctx.mm.n, ctx.act.n

    sq = ctx.sq_scratch

    @block.scalar
    def _(scalar):
        if wait is not None:
            scalar.wait_ge(wait[0].h, wait[1])
        for t in range(NT):
            if t >= 1:
                scalar.wait_ge(ctx.dve.h, dve0 + t)
            ctx.act.inc(scalar.activation(
                out=sq[:, :], in_=x_sb[:, D * t:D * (t + 1)], func=AF.Square))

    @block.vector
    def _(vector):
        if wait is not None:
            vector.wait_ge(wait[0].h, wait[1])
        for t in range(NT):
            xt = x_sb[:, D * t:D * (t + 1)]
            vector.reduce_sum(stat[:, t:t + 1], xt, axis=AX.X)
            vector.wait_ge(ctx.act.h, act0 + t + 1)
            ctx.dve.inc(vector.reduce_sum(stat[:, 8 + t:9 + t], sq[:, :], axis=AX.X))

    @block.vector
    def _(vector):
        for t in range(NT):
            ctx.dve.inc(vector.tensor_scalar(stat[:, 16 + t:17 + t],
                                             stat[:, t:t + 1],
                                             1.0 / D, None, ALU.mult))
            ctx.dve.inc(vector.tensor_scalar(stat[:, 24 + t:25 + t],
                                             stat[:, 8 + t:9 + t],
                                             1.0 / D, None, ALU.mult))
            vector.wait_ge(ctx.dve.h, ctx.dve.n)
            ctx.dve.inc(vector.tensor_tensor(stat[:, 48:49],
                                             stat[:, 16 + t:17 + t],
                                             stat[:, 16 + t:17 + t], ALU.mult))
            vector.wait_ge(ctx.dve.h, ctx.dve.n)
            ctx.dve.inc(vector.tensor_tensor(stat[:, 24 + t:25 + t],
                                             stat[:, 24 + t:25 + t],
                                             stat[:, 48:49], ALU.subtract))

    @block.scalar
    def _(scalar):
        for t in range(NT):
            scalar.wait_ge(ctx.dve.h, dve0 + NT + 4 * (t + 1))
            ctx.act.inc(scalar.activation(
                out=stat[:, 32 + t:33 + t], in_=stat[:, 24 + t:25 + t],
                func=AF.Sqrt, bias=1e-5, scale=1.0))

    @block.vector
    def _(vector):
        for t in range(NT):
            vector.wait_ge(ctx.act.h, act0 + NT + t + 1)
            ctx.dve.inc(vector.reciprocal(stat[:, 40 + t:41 + t],
                                          stat[:, 32 + t:33 + t]))
            vector.wait_ge(ctx.dve.h, ctx.dve.n)
            ctx.dve.inc(vector.tensor_scalar(
                xn_sb[:, D * t:D * (t + 1)], x_sb[:, D * t:D * (t + 1)],
                stat[:, 16 + t:17 + t], stat[:, 40 + t:41 + t],
                ALU.subtract, ALU.mult))

    ln_done = ctx.dve.n
    ev0 = ctx.ev.n

    @block.tensor
    def _(tensor):
        tensor.wait_ge(ctx.dve.h, ln_done)
        single = tr_ps[0] is tr_ps[1]
        i = 0
        for f in range(ND):
            for t in range(NT):
                if i >= (1 if single else 2):
                    tensor.wait_ge(ctx.ev.h, ev0 + (i if single else i - 1))
                ctx.mm.inc(tensor.transpose(
                    tr_ps[i % 2][:, :],
                    xn_sb[:, D * t + 128 * f:D * t + 128 * (f + 1)],
                    ident))
                i += 1

    @block.vector
    def _(vector):
        i = 0
        for f in range(ND):
            for t in range(NT):
                vector.wait_ge(ctx.mm.h, mm0 + i + 1)
                dst = xnT_sb[:, TOK * f + 128 * t:TOK * f + 128 * (t + 1)]
                ctx.ev.inc(vector.tensor_copy(dst, tr_ps[i % 2][:, :]))
                i += 1

    if xnT_dram is not None:
        @block.sync
        def _(sync):
            for f in range(ND):
                sync.wait_ge(ctx.ev.h, ev0 + NT * (f + 1))
                ctx.st.inc(sync.dma_start(
                    out=xnT_dram[128 * f:128 * (f + 1), :],
                    in_=xnT_sb[:, TOK * f:TOK * (f + 1)]), 16)
    return ctx.ev.n


import contextlib

RACE_DETECT = [False]


def _mk_bass():
    return bass.Bass("TRN2", target_bir_lowering=False, debug=False,
                     detect_race_conditions=RACE_DETECT[0])


def phase_ln(nc, ctx, es, x_ap, ident_ap, xnT_dram, sfx=""):
    block = ctx.block
    sqs = es.enter_context(nc.sbuf_tensor("sqs_ln" + sfx, [128, D], F32))
    ctx.sq_scratch = sqs[:, :]
    x_sb = es.enter_context(nc.sbuf_tensor("x_sb_ln" + sfx, [128, NT * D], F32))
    xn_sb = es.enter_context(nc.sbuf_tensor("xn_sb_ln" + sfx, [128, NT * D], BF16))
    xnT_sb = es.enter_context(nc.sbuf_tensor("xnT_sb_ln" + sfx, [128, ND * TOK], BF16))
    ident = es.enter_context(nc.sbuf_tensor("ident_ln" + sfx, [128, 128], BF16))
    stat = es.enter_context(nc.sbuf_tensor("stat_ln" + sfx, [128, 64], F32))
    tp0 = es.enter_context(nc.psum_tensor("tp0_ln" + sfx, [128, 128], BF16))
    tp1 = es.enter_context(nc.psum_tensor("tp1_ln" + sfx, [128, 128], BF16))
    ld0 = ctx.ld.n

    @block.sync
    def _(sync):
        ctx.ld.inc(sync.dma_start(
            out=x_sb[:, :].rearrange("p (t d) -> p t d", t=NT),
            in_=x_ap.rearrange("(t p) d -> p t d", p=128)), 16)
        ctx.ld.inc(sync.dma_start(out=ident[:, :], in_=ident_ap), 16)

    emit_ln_transpose(ctx, x_sb[:, :], xn_sb[:, :], ident[:, :],
                      (tp0, tp1), stat[:, :], xnT_sb[:, :],
                      xnT_dram=xnT_dram, wait=(ctx.ld, ctx.ld.n))


def build_ln_launch():
    nc = _mk_bass()
    x = nc.dram_tensor("x", [TOK, D], F32, kind="ExternalInput").ap()
    ident_in = nc.dram_tensor("ident", [128, 128], BF16, kind="ExternalInput").ap()
    consts_in = nc.dram_tensor("consts", [128, len(CONSTS)], F32, kind="ExternalInput").ap()
    xnT = nc.dram_tensor("xnT", [D, TOK], BF16, kind="ExternalOutput").ap()
    with contextlib.ExitStack() as es:
        csb = es.enter_context(nc.sbuf_tensor("csb", [128, len(CONSTS)], F32))
        register_consts(nc, csb)
        block = es.enter_context(nc.Block())
        ctx = Ctx(nc, block, es)

        @block.sync
        def _(sync):
            ctx.ld.inc(sync.dma_start(out=csb[:, :], in_=consts_in), 16)

        phase_ln(nc, ctx, es, x, ident_in, xnT)

        @block.sync
        def _(sync):
            sync.wait_ge(ctx.st.h, ctx.st.n)
    return nc


def _test_a0():
    import jax.numpy as jnp
    rng = np.random.default_rng(7)
    xs = [rng.standard_normal((TOK, D), dtype=np.float32) for _ in range(8)]
    nc = build_ln_launch()
    ident = np.eye(128, dtype=ml_dtypes.bfloat16)
    in_maps = [{"x": xs[c], "ident": ident, "consts": consts_host()} for c in range(8)]
    res = run_bass_kernel_spmd(nc, in_maps, core_ids=list(range(8)))
    for c in [0, 5]:
        x = xs[c]
        m = x.mean(axis=-1, keepdims=True)
        v = x.var(axis=-1, keepdims=True)
        ref = ((x - m) / np.sqrt(v + 1e-5)).astype(ml_dtypes.bfloat16).T
        got = res.results[c]["xnT"]
        diff = np.abs(got.astype(np.float32) - ref.astype(np.float32))
        rel = np.abs(diff).max()
        ok = rel < 0.05
        print(f"core{c} A0 ln+T: maxabs={rel:.4f} {'OK' if ok else 'FAIL'}")





class NS:
    def __init__(self, **kw):
        self.__dict__.update(kw)


NW = S // 512
PARANOID = [False]


def _emit_proj(ctx, bufs, xg, wlist, specs, rot):
    block = ctx.block
    if not callable(xg):
        _xgap = xg
        xg = lambda kt, w: _xgap[128 * kt:128 * (kt + 1), 512 * w:512 * (w + 1)]
    nm = len(wlist)
    rhs = (bufs.rhs0, bufs.rhs1, bufs.rhs2, bufs.rhs3)
    wsb = bufs.wsb
    qslot, vslot = 0, 0
    for sp in specs:
        for s in sp:
            if s.kind == "qk":
                s.slot = qslot
                qslot += 1
            else:
                s.slot = vslot
                vslot += 1

    def qraw_ap(s):
        return bufs.qraw[s.r0:s.r1, 512 * s.slot:512 * (s.slot + 1)]

    def qs_ap(s):
        return bufs.qs_sb[s.r0:s.r1, 512 * s.slot:512 * (s.slot + 1)]

    def vraw_ap(s, sc):
        return bufs.vraw[s.r0:s.r1, 512 * s.slot + 128 * sc:512 * s.slot + 128 * (sc + 1)]

    @block.sync
    def _(sync):
        sync.wait_ge(ctx.mm.h, ctx.mm.n)
        wg = getattr(ctx, "w_gate", None)
        if wg is not None:
            sync.wait_ge(wg[0].h, wg[1])
        for m, went in enumerate(wlist):
            parts = went if isinstance(went, list) else [(went, 0)]
            for wap, c0 in parts:
                cw = wap.shape[1]
                ctx.ld.inc(sync.dma_start(
                    out=wsb[:, 16 * 128 * m:16 * 128 * (m + 1)].rearrange(
                        "p (kt c) -> p kt c", kt=16)[:, :, c0:c0 + cw],
                    in_=wap.rearrange("(kt p) c -> p kt c", p=128)), 16)
    wready = ctx.ld.n

    mm0, act0, ev0 = ctx.mm.n, ctx.act.n, ctx.ev.n
    n_v = sum(1 for sp in specs for s in sp if s.kind == "v")
    n_qk = sum(1 for sp in specs for s in sp if s.kind == "qk")
    nsub = sum(len(sp) for sp in specs)
    mm_per_w = nm * ND + n_qk + n_v * 4
    act_per_w = nsub + n_qk

    rs_base = (ctx.rs[0].n, ctx.rs[1].n)

    def rhs_ld_after(i):
        return rs_base[i % 2] + 16 * (i // 2 + 1)

    def mm_after_kt(w, kt):
        return mm0 + w * mm_per_w + (kt + 1) * nm

    def mm_after_tail(w, j):
        return mm0 + w * mm_per_w + nm * ND + j + 1

    def act_after_copies(w):
        return act0 + act_per_w * w + nsub

    def act_after_w(w):
        return act0 + act_per_w * (w + 1)

    ev_per_w = n_qk + n_v * 4
    rs0_base = ctx.rs[0].n

    @block.sync
    def _(sync):
        rg = getattr(ctx, "rhs_gate", None)
        if rg is not None:
            sync.wait_ge(rg[0].h, rg[1])
        for i in range(NW * ND):
            w, kt = divmod(i, ND)
            if i >= 4:
                pw, pkt = divmod(i - 4, ND)
                sync.wait_ge(ctx.mm.h, mm_after_kt(pw, pkt))
            ctx.rs[0].inc(sync.dma_start(
                out=rhs[i % 4][:, :], in_=xg(kt, w)), 16)

    dve_entry = ctx.dve.n

    @block.tensor
    def _(tensor):
        tensor.wait_ge(ctx.ld.h, wready)
        tensor.wait_ge(ctx.dve.h, dve_entry)
        tensor.wait_ge(ctx.act.h, act0)
        cumc = []
        tot = 0
        for sp in specs:
            tot += len(sp)
            cumc.append(tot)
        for w in range(NW):
            for kt in range(ND):
                i = w * ND + kt
                tensor.wait_ge(ctx.rs[0].h, rs0_base + 16 * (i + 1))
                for m in range(nm):
                    if w > 0 and kt == 0:
                        tensor.wait_ge(ctx.act.h,
                                       act0 + act_per_w * (w - 1) + cumc[m])
                    ctx.mm.inc(tensor.matmul(
                        bufs.pp[m][:, :],
                        wsb[:, 128 * (16 * m + kt):128 * (16 * m + kt + 1)],
                        rhs[i % 4][:, :],
                        start=(kt == 0), stop=(kt == ND - 1)))
            ci = 0
            qkj = 0
            for m, sp in enumerate(specs):
                for s in sp:
                    ci += 1
                    if s.kind != "qk":
                        continue
                    tensor.wait_ge(ctx.act.h, act0 + act_per_w * w + ci)
                    if qkj > 0:
                        tensor.wait_ge(ctx.act.h,
                                       act0 + act_per_w * w + nsub + qkj)
                    elif w > 0:
                        tensor.wait_ge(ctx.act.h, act_after_w(w - 1))
                    hd = s.r1 - s.r0
                    ctx.mm.inc(tensor.matmul(
                        bufs.psh[s.r0:s.r1, :], rot[0:hd, s.r0:s.r1],
                        qraw_ap(s), start=True, stop=True))
                    qkj += 1
            ci = 0
            vj = 0
            for m, sp in enumerate(specs):
                for s in sp:
                    ci += 1
                    if s.kind != "v":
                        continue
                    tensor.wait_ge(ctx.act.h, act0 + act_per_w * w + ci)
                    hd = s.r1 - s.r0
                    for sc in range(4):
                        g = w * (n_v * 4) + vj
                        if g >= 1:
                            pw2, pj2 = divmod(g - 1, n_v * 4)
                            tensor.wait_ge(ctx.ev.h,
                                           ev0 + pw2 * ev_per_w + n_qk + pj2 + 1)
                        ctx.mm.inc(tensor.transpose(
                            bufs.tp[0:128, 0:hd],
                            vraw_ap(s, sc),
                            bufs.ident[s.r0:s.r1, s.r0:s.r1]))
                        vj += 1

    @block.scalar
    def _(scalar):
        for w in range(NW):
            for m, sp in enumerate(specs):
                scalar.wait_ge(ctx.mm.h, mm_after_kt(w, ND - 1) - (nm - m - 1))
                for s in sp:
                    if w > 0:
                        if s.kind == "qk":
                            scalar.wait_ge(ctx.ev.h, ev0 + ev_per_w * (w - 1) + n_qk)
                        else:
                            scalar.wait_ge(ctx.mm.h, mm0 + w * mm_per_w)
                    dst = (qraw_ap(s) if s.kind == "qk"
                           else bufs.vraw[s.r0:s.r1, 512 * s.slot:512 * (s.slot + 1)])
                    ctx.act.inc(scalar.activation(
                        out=dst, in_=bufs.pp[m][s.r0:s.r1, :], func=AF.Copy,
                        bias=0.0,
                        scale=(s.scale if (s.kind == "qk" and s.scale) else 1.0)))
            sj = 0
            for m, sp in enumerate(specs):
                for s in sp:
                    if s.kind != "qk":
                        continue
                    scalar.wait_ge(ctx.mm.h, mm_after_tail(w, sj))
                    if w > 0:
                        scalar.wait_ge(ctx.ev.h, ev0 + ev_per_w * (w - 1) + n_qk)
                    ctx.act.inc(scalar.activation(
                        out=qs_ap(s),
                        in_=bufs.psh[s.r0:s.r1, :], func=AF.Copy,
                        bias=0.0, scale=1.0))
                    sj += 1

    @block.vector
    def _(vector):
        for w in range(NW):
            qki = 0
            for m, sp in enumerate(specs):
                for s in sp:
                    if s.kind != "qk":
                        continue
                    vector.wait_ge(ctx.act.h,
                                   act0 + act_per_w * w + nsub + qki + 1)
                    cw = slice(512 * w, 512 * (w + 1))
                    rr = slice(s.r0, s.r1)
                    vector.tensor_tensor(bufs.tmp1[rr, :], qraw_ap(s),
                                         s.cosrows[rr, cw], ALU.mult)
                    ctx.dve.inc(vector.tensor_tensor(bufs.tmp2[rr, :], qs_ap(s),
                                                     s.sinrows[rr, cw], ALU.mult))
                    vector.wait_ge(ctx.dve.h, ctx.dve.n)
                    ctx.ev.inc(vector.tensor_tensor(
                        s.dst(w)[rr, :], bufs.tmp1[rr, :], bufs.tmp2[rr, :],
                        ALU.add))
                    qki += 1
            vj = 0
            for m, sp in enumerate(specs):
                for s in sp:
                    if s.kind != "v":
                        continue
                    hd = s.r1 - s.r0
                    vst = getattr(s, "vstep", hd)
                    for sc in range(4):
                        vector.wait_ge(ctx.mm.h, mm_after_tail(w, n_qk + vj))
                        slot = s.vslot0 + 4 * w + sc
                        ctx.ev.inc(vector.tensor_copy(
                            s.vtm[:, vst * slot:vst * slot + hd],
                            bufs.tp[0:128, 0:hd]))
                        vj += 1
    return ctx.ev.n


def _emit_norm_max(ctx, bufs, src_fn, hd, out_cell, wait_ev=None):
    block = ctx.block
    act0, mm0, dve0 = ctx.act.n, ctx.mm.n, ctx.dve.n
    prow = (bufs.pp[4], bufs.pp[5])

    @block.scalar
    def _(scalar):
        if wait_ev is not None:
            scalar.wait_ge(ctx.ev.h, wait_ev)
        for c in range(NW):
            if c >= 1:
                scalar.wait_ge(ctx.mm.h, mm0 + c)
            ctx.act.inc(scalar.activation(
                out=bufs.sqs[0:hd, :], in_=src_fn(c), func=AF.Square))

    ev_entry = ctx.ev.n

    @block.tensor
    def _(tensor):
        tensor.wait_ge(ctx.dve.h, dve0)
        tensor.wait_ge(ctx.ev.h, ev_entry)
        for c in range(NW):
            tensor.wait_ge(ctx.act.h, act0 + c + 1)
            if c >= 2:
                tensor.wait_ge(ctx.dve.h, dve0 + c - 1)
            ctx.mm.inc(tensor.matmul(
                prow[c % 2][0:1, :], bufs.onescol[0:hd, :], bufs.sqs[0:hd, :],
                start=True, stop=True))

    @block.vector
    def _(vector):
        for c in range(NW):
            vector.wait_ge(ctx.mm.h, mm0 + c + 1)
            ctx.dve.inc(vector.reduce_max(
                bufs.nrow[0:1, c:c + 1], prow[c % 2][0:1, :], axis=AX.X))
        vector.wait_ge(ctx.dve.h, ctx.dve.n)
        ctx.dve.inc(vector.reduce_max(out_cell, bufs.nrow[0:1, 0:NW], axis=AX.X))
        vector.wait_ge(ctx.dve.h, ctx.dve.n)


def _emit_bound(ctx, bufs, qcell, kcell, mb_cell, expb_dst):
    block = ctx.block
    nr = bufs.nrow

    @block.vector
    def _(vector):
        vector.wait_ge(ctx.dve.h, ctx.dve.n)
        ctx.dve.inc(vector.tensor_tensor(nr[0:1, 10:11], qcell, kcell, ALU.mult))
        vector.wait_ge(ctx.dve.h, ctx.dve.n)
    dve_t = ctx.dve.n

    @block.scalar
    def _(scalar):
        scalar.wait_ge(ctx.dve.h, dve_t)
        ctx.act.inc(scalar.activation(out=nr[0:1, 11:12], in_=nr[0:1, 10:11],
                                      func=AF.Sqrt, bias=0.0, scale=1.0404))
    act_b = ctx.act.n

    @block.vector
    def _(vector):
        vector.wait_ge(ctx.act.h, act_b)
        if mb_cell is not None:
            ctx.dve.inc(vector.tensor_tensor(nr[0:1, 11:12], nr[0:1, 11:12],
                                             mb_cell, ALU.add))
            vector.wait_ge(ctx.dve.h, ctx.dve.n)
        ctx.dve.inc(vector.tensor_scalar(nr[0:1, 12:13], nr[0:1, 11:12],
                                         -1.0, None, ALU.mult))
        vector.wait_ge(ctx.dve.h, ctx.dve.n)
    dve_nb = ctx.dve.n

    @block.tensor
    def _(tensor):
        tensor.wait_ge(ctx.dve.h, dve_nb)
        ctx.mm.inc(tensor.matmul(bufs.pp[5][:, 0:1], bufs.onesrow32[0:1, :],
                                 nr[0:1, 12:13], start=True, stop=True))
    mm_b = ctx.mm.n

    @block.vector
    def _(vector):
        vector.wait_ge(ctx.mm.h, mm_b)
        ctx.dve.inc(vector.tensor_copy(expb_dst, bufs.pp[5][:, 0:1]))
    return ctx.dve.n


def _emit_flash(ctx, bufs, lay, hd, kt_ap_fn, vtm, qt_fn, expb, cb, ctxT_dram,
                row0):
    block = ctx.block
    dep = 3
    pt = (bufs.ptile0, bufs.ptile1, bufs.ptile2)
    if lay == 1:
        s_ps = (bufs.pp[0], bufs.pp[1], bufs.pp[3])
        b_ps = (bufs.pp[4], bufs.pp[5])
    else:
        s_ps = (bufs.pp[0], bufs.pp[1], bufs.pp[4])
        b_ps = (bufs.pp[5], bufs.psh)
    o_ps = bufs.pp[2]
    d_ps = bufs.pp[3]
    ctile = (bufs.ctile0, bufs.ctile1)
    fold = 1 if lay == 1 else 0
    vstep = hd + fold
    oraw = bufs.oraw[ctx._fl_call % 2]
    hist = ctx._fl_hist

    items = []
    for qc in range(NW):
        for kt in range(4 * qc + 4):
            items.append((qc, kt))
    n_it = len(items)
    mm0, act0, dve0, ev0 = ctx.mm.n, ctx.act.n, ctx.dve.n, ctx.ev.n

    def is_near(qc, kt):
        if lay == 0:
            return kt >= 4 * qc - 1
        return kt >= 4 * qc

    mm_s_after, mm_pv_after, mm_d_after, mm_bc_after = {}, {}, {}, {}
    exp_after, evac_after, add_after, dencp_after = {}, {}, {}, {}
    mult_ev_after, st_after, recip_after = {}, {}, {}
    mm_c, act_c, dve_c, ev_c = mm0, act0, dve0, ev0
    for j in range(min(dep, n_it)):
        mm_c += 1
        mm_s_after[j] = mm_c
    for j, (qc, kt) in enumerate(items):
        if is_near(qc, kt):
            dve_c += 1
            add_after[j] = dve_c
        act_c += 1
        exp_after[j] = act_c
        mm_c += 1
        mm_pv_after[j] = mm_c
        if lay == 0:
            mm_c += 1
            mm_d_after[j] = mm_c
        if j + dep < n_it:
            mm_c += 1
            mm_s_after[j + dep] = mm_c
        if kt == 4 * qc + 3:
            act_c += 1
            evac_after[qc] = act_c
            dve_c += 1
            dencp_after[qc] = dve_c
    for qc in range(NW):
        mm_c += 1
        mm_bc_after[qc] = mm_c
        dve_c += 1
        recip_after[qc] = dve_c
        ev_c += 1
        mult_ev_after[qc] = ev_c
        st_after[qc] = ctx.stp[qc % 2].n + 16 * (qc // 2 + 1)

    @block.tensor
    def _(tensor):
        def emit_s(j):
            qc, kt = items[j]
            if j >= dep:
                tensor.wait_ge(ctx.act.h, exp_after[j - dep])
            else:
                tensor.wait_ge(ctx.act.h, act0)
                tensor.wait_ge(ctx.dve.h, dve0)
            ctx.mm.inc(tensor.matmul(
                s_ps[j % dep][:, :], kt_ap_fn(kt), qt_fn(qc),
                start=True, stop=True))

        for j in range(min(dep, n_it)):
            emit_s(j)
        for j, (qc, kt) in enumerate(items):
            tensor.wait_ge(ctx.act.h, exp_after[j])
            if kt == 0 and qc > 0:
                tensor.wait_ge(ctx.act.h, evac_after[qc - 1])
                tensor.wait_ge(ctx.dve.h, dencp_after[qc - 1])
            last = (kt == 4 * qc + 3)
            ctx.mm.inc(tensor.matmul(
                o_ps[0:hd + fold, :],
                vtm[:, vstep * kt:vstep * kt + hd + fold],
                pt[j % dep][:, :], start=(kt == 0), stop=last))
            if lay == 0:
                ctx.mm.inc(tensor.matmul(
                    d_ps[0:1, :], bufs.onescol[0:128, :], pt[j % dep][:, :],
                    start=(kt == 0), stop=last))
            if j + dep < n_it:
                emit_s(j + dep)
        for qc in range(NW):
            if qc == 0:
                tensor.wait_ge(ctx.act.h, act0)
            tensor.wait_ge(ctx.dve.h, dencp_after[qc])
            if qc >= 2:
                tensor.wait_ge(ctx.dve.h, recip_after[qc - 2])
            elif hist:
                tensor.wait_ge(ctx.dve.h, hist[-1]["recip"][6 + qc])
            ctx.mm.inc(tensor.matmul(
                b_ps[qc % 2][:, :], bufs.onesrow32[0:1, :],
                bufs.den_row[0:1, 512 * qc:512 * (qc + 1)],
                start=True, stop=True))

    @block.scalar
    def _(scalar):
        for j, (qc, kt) in enumerate(items):
            if j == 0:
                scalar.wait_ge(ctx.dve.h, dve0)
            if is_near(qc, kt):
                scalar.wait_ge(ctx.dve.h, add_after[j])
            else:
                scalar.wait_ge(ctx.mm.h, mm_s_after[j])
            if j >= dep:
                scalar.wait_ge(ctx.mm.h,
                               (mm_d_after if lay == 0 else mm_pv_after)[j - dep])
            ctx.act.inc(scalar.activation(
                out=pt[j % dep][:, :], in_=s_ps[j % dep][:, :], func=AF.Exp,
                bias=expb, scale=1.0))
            if kt == 4 * qc + 3:
                scalar.wait_ge(ctx.mm.h, mm_pv_after[j])
                if len(hist) >= 2:
                    scalar.wait_ge(ctx.ev.h, hist[-2]["mult_ev"][qc])
                ctx.act.inc(scalar.activation(
                    out=oraw[0:hd, 512 * qc:512 * (qc + 1)],
                    in_=o_ps[0:hd, :], func=AF.Copy, bias=0.0, scale=1.0))

    @block.vector
    def _(vector):
        for j, (qc, kt) in enumerate(items):
            if is_near(qc, kt):
                vector.wait_ge(ctx.mm.h, mm_s_after[j])
                cst = CB0 + 512 * qc - 128 * kt
                src = cb if cb is not None else bufs.mbig
                ctx.dve.inc(vector.tensor_tensor(
                    s_ps[j % dep][:, :], s_ps[j % dep][:, :],
                    src[:, cst:cst + 512], ALU.add))
            if kt == 4 * qc + 3:
                vector.wait_ge(ctx.mm.h,
                               (mm_pv_after if lay == 1 else mm_d_after)[j])
                ctx.dve.inc(vector.tensor_copy(
                    bufs.den_row[0:1, 512 * qc:512 * (qc + 1)],
                    o_ps[hd:hd + 1, :] if lay == 1 else d_ps[0:1, :]))
        for qc in range(NW):
            vector.wait_ge(ctx.mm.h, mm_bc_after[qc])
            ctx.dve.inc(vector.reciprocal(bufs.brec[qc % 2][0:hd, :],
                                          b_ps[qc % 2][0:hd, :]))
            vector.wait_ge(ctx.act.h, evac_after[qc])
            if qc >= 2:
                vector.wait_ge(ctx.stp[qc % 2].h, st_after[qc - 2])
            elif hist:
                vector.wait_ge(ctx.stp[qc % 2].h, hist[-1]["st"][6 + qc])
            ctx.ev.inc(vector.tensor_tensor(
                ctile[qc % 2][0:hd, :], oraw[0:hd, 512 * qc:512 * (qc + 1)],
                bufs.brec[qc % 2][0:hd, :], ALU.mult))

    @block.gpsimd
    def _(gpsimd):
        for qc in range(NW):
            gpsimd.wait_ge(ctx.ev.h, mult_ev_after[qc])
            ctx.stp[qc % 2].inc(gpsimd.dma_start(
                out=ctxT_dram[row0:row0 + hd, 512 * qc:512 * (qc + 1)],
                in_=ctile[qc % 2][0:hd, :]), 16)

    hist.append({"mult_ev": mult_ev_after, "st": st_after,
                 "recip": recip_after})
    if len(hist) > 2:
        hist.pop(0)
    ctx._fl_call += 1
    return None


L0SC = 1.0 / math.sqrt(L0_HD)
L1SC = 1.0 / math.sqrt(L1_HD)


def _rot_host(lay):
    if lay == 0:
        r = np.roll(np.eye(L0_HD, dtype=np.float32), L0_HD // 2, axis=0)
    else:
        r64 = np.roll(np.eye(64, dtype=np.float32), 32, axis=0)
        r = np.zeros((128, 128), dtype=np.float32)
        r[0:64, 0:64] = r64
        r[64:128, 64:128] = r64
    return r.astype(ml_dtypes.bfloat16)


def _l0_cos_sin_rows():
    c, s = _l0_rope_tables()
    cosr = np.concatenate([c, c], axis=0)
    sinr = np.concatenate([-s.astype(np.float32), s.astype(np.float32)],
                          axis=0).astype(ml_dtypes.bfloat16)
    return cosr, sinr


def _l1_cos_sin_rows():
    c, s = _l1_rope_tables()
    cosr = np.concatenate([c, c, c, c], axis=0)
    sf = s.astype(np.float32)
    sinr = np.concatenate([-sf, sf, -sf, sf], axis=0).astype(ml_dtypes.bfloat16)
    return cosr, sinr


def _alloc_attn_bufs(nc, es, lay):
    sb = lambda n, shp, dt: es.enter_context(nc.sbuf_tensor(n, shp, dt))[:, :]
    bufs = NS()
    bufs.rhs0 = sb("rhs0", [128, 512], BF16)
    bufs.rhs1 = sb("rhs1", [128, 512], BF16)
    bufs.rhs2 = sb("rhs2", [128, 512], BF16)
    bufs.rhs3 = sb("rhs3", [128, 512], BF16)
    bufs.wsb = sb("wsb", [128, 6 * 16 * 128], BF16)
    bufs.qraw = sb("qraw", [128, 4 * 512], BF16)
    bufs.qs_sb = sb("qs_sb", [128, 4 * 512], BF16)
    bufs.vraw = sb("vraw", [128, 2 * 512], BF16)
    bufs.tmp1 = sb("tmp1", [128, 512], BF16)
    bufs.tmp2 = sb("tmp2", [128, 512], BF16)
    bufs.sqs = sb("sqs", [128, 512], BF16)
    bufs.ptile0 = sb("ptile0", [128, 512], BF16)
    bufs.ptile1 = sb("ptile1", [128, 512], BF16)
    bufs.ptile2 = sb("ptile2", [128, 512], BF16)
    bufs.ctile0 = sb("ctile0", [128, 512], BF16)
    bufs.ctile1 = sb("ctile1", [128, 512], BF16)
    bufs.oraw = [sb("oraw0", [128, NW * 512], F32),
                 sb("oraw1", [128, NW * 512], F32)]
    bufs.den_row = sb("den_row", [1, NW * 512], F32)
    bufs.brec = [sb("brec0", [128, 512], F32), sb("brec1", [128, 512], F32)]
    bufs.dbgrow = sb("dbgrow", [1, 1024], F32)
    bufs.nrow = sb("nrow", [1, 64], F32)
    bufs.expb = sb("expb", [128, 8], F32)
    bufs.onescol = sb("onescol", [128, 1], BF16)
    bufs.onesrow32 = sb("onesrow32", [1, 128], F32)
    bufs.ident = sb("identsb", [128, 128], BF16)
    bufs.rot = sb("rotsb", [128, 128], BF16)
    bufs.cosr = sb("cosr_sb", [128, S], BF16)
    bufs.sinr = sb("sinr_sb", [128, S], BF16)
    bufs.mbig = sb("mbigsb", [128, CBW], F32)
    bufs.pp = [es.enter_context(nc.psum_tensor(f"pp{i}", [128, 512], F32))[:, :]
               for i in range(6)]
    bufs.tp = es.enter_context(nc.psum_tensor("tpps", [128, 128], BF16))[:, :]
    bufs.psh = es.enter_context(nc.psum_tensor("pshps", [128, 512], F32))[:, :]
    if lay == 0:
        bufs.QT = sb("QT", [128, 2 * S], BF16)
        bufs.KT = sb("KT", [128, 2 * S], BF16)
        bufs.VT = [sb("VT0", [128, 32 * 128], BF16), sb("VT1", [128, 32 * 128], BF16)]
        bufs.cb = [sb(f"cb{h}", [128, CBW], F32) for h in range(4)]
        bufs.vex = sb("vex", [4, 1536], F32)
        bufs.tbl = sb("tbl_sb", [32, 4], F32)
        bufs.oh = bufs.oraw[1][0:32, 0:1536]
        bufs.id16 = sb("id16_sb", [16, 16], F32)
        bufs.mbrow = sb("mbrow", [1, 4], F32)
        bufs.mbcol = sb("mbcol", [4, 1], F32)
        bufs.jrev = sb("jrev_sb", [128, 128], F32)
        bufs.cbtmp = bufs.oraw[0][:, 0:CBW]
    else:
        bufs.QT = sb("QT", [128, 2 * S], BF16)
        bufs.KT = sb("KT", [128, S], BF16)
        bufs.VT = [sb("VT0", [128, 32 * 64], BF16)]
        bufs.cb = None
    return bufs




def _dbg_dump(ctx, src_ap, dst_ap, wait=None):
    block = ctx.block

    @block.sync
    def _(sync):
        if wait is not None:
            kind, val = wait
            sem = {"ev": ctx.ev, "dve": ctx.dve, "act": ctx.act,
                   "mm": ctx.mm}[kind]
            sync.wait_ge(sem.h, val)
        ctx.st.inc(sync.dma_start(out=dst_ap, in_=src_ap), 16)


def _dbg_stage_attn0(ctx, bufs, ctxT, stage):
    if stage == 1.5:
        _dbg_dump(ctx, bufs.cb[0][:, 0:512], ctxT[0:128, 0:1024].bitcast(F32),
                  wait=("dve", ctx.dve.n))
        _dbg_dump(ctx, bufs.cb[3][:, 512:1024], ctxT[0:128, 1024:2048].bitcast(F32))
        _dbg_dump(ctx, bufs.mbrow, ctxT[384:385, 0:32].bitcast(F32))


def phase_attn0(nc, ctx, es, io):
    import dataclasses
    block = ctx.block
    sfx = getattr(io, "sfx", "")
    vext_d = nc.dram_tensor("vext_d" + sfx, [4, 1536], F32, kind="Internal")
    bufs = _alloc_attn_bufs(nc, es, 0)
    ctx.fdbg = None
    ctx.pdbg = None
    ctx._fl_hist = []
    ctx._fl_call = 0

    @block.sync
    def _(sync):
        for dst, srcap in [(bufs.mbig, io.mbig), (bufs.cosr, io.cosr),
                           (bufs.sinr, io.sinr), (bufs.ident, io.ident),
                           (bufs.id16, io.id16), (bufs.rot, io.rot),
                           (bufs.tbl, io.tbl), (bufs.oh, io.oh),
                           (bufs.jrev, io.jrev)]:
            ctx.ld.inc(sync.dma_start(out=dst, in_=srcap), 16)
    ld_pre = ctx.ld.n

    @block.vector
    def _(vector):
        vector.wait_ge(ctx.ld.h, ld_pre)
        vector.memset(bufs.onescol, 1.0)
        ctx.dve.inc(vector.memset(bufs.onesrow32, 1.0))
    dve_ones = ctx.dve.n

    @block.tensor
    def _(tensor):
        tensor.wait_ge(ctx.ld.h, ld_pre)
        tensor.wait_ge(ctx.dve.h, ctx.dve.n)
        for c in range(3):
            ctx.mm.inc(tensor.matmul(
                bufs.pp[c][0:4, :], bufs.tbl, bufs.oh[:, 512 * c:512 * (c + 1)],
                start=True, stop=True))
    mm_vex = ctx.mm.n

    @block.scalar
    def _(scalar):
        scalar.wait_ge(ctx.mm.h, mm_vex)
        for c in range(3):
            ctx.act.inc(scalar.activation(
                out=bufs.vex[0:4, 512 * c:512 * (c + 1)],
                in_=bufs.pp[c][0:4, :], func=AF.Copy, bias=0.0, scale=1.0))
    act_vex = ctx.act.n

    @block.sync
    def _(sync):
        sync.wait_ge(ctx.act.h, act_vex)
        ctx.st.inc(sync.dma_start(out=vext_d.ap(), in_=bufs.vex[0:4, :]), 16)
    st_vex = ctx.st.n

    @block.gpsimd
    def _(gpsimd):
        gpsimd.wait_ge(ctx.st.h, st_vex)
        for h in range(4):
            shifted = dataclasses.replace(
                vext_d.ap(), offset=1536 * h + 1, ap=[[1, 128], [1, CBW]])
            ctx.ldg.inc(gpsimd.dma_start(out=bufs.cb[h], in_=shifted), 16)
    ld_cb = ctx.ldg.n

    NCHUNK = [512, 512, 128]
    ctx._cb_dve0 = ctx.dve.n

    @block.tensor
    def _(tensor):
        tensor.wait_ge(ctx.ldg.h, ld_cb)
        for h in range(4):
            if h > 0:
                tensor.wait_ge(ctx.dve.h, ctx._cb_dve0 + h * 3)
            off = 0
            for ci, cn in enumerate(NCHUNK):
                ctx.mm.inc(tensor.matmul(
                    bufs.pp[ci][:, 0:cn], bufs.jrev,
                    bufs.cb[h][:, off:off + cn], start=True, stop=True))
                off += cn
    mm_cb0 = ctx.mm.n - 12

    @block.vector
    def _(vector):
        for h in range(4):
            off = 0
            for ci, cn in enumerate(NCHUNK):
                vector.wait_ge(ctx.mm.h, mm_cb0 + h * 3 + ci + 1)
                ctx.dve.inc(vector.tensor_tensor(
                    bufs.cb[h][:, off:off + cn], bufs.pp[ci][:, 0:cn],
                    bufs.mbig[:, off:off + cn], ALU.add))
                off += cn

        vector.wait_ge(ctx.act.h, act_vex)
        ctx.dve.inc(vector.reduce_max(bufs.mbcol[0:4, 0:1], bufs.vex[0:4, :],
                                      axis=AX.X))
    dve_mb = ctx.dve.n

    @block.tensor
    def _(tensor):
        tensor.wait_ge(ctx.dve.h, dve_mb)
        ctx.mm.inc(tensor.transpose(bufs.pp[3][0:1, 0:4],
                                    bufs.mbcol[0:4, 0:1], bufs.id16[0:4, 0:4]))
    mm_mb = ctx.mm.n

    @block.vector
    def _(vector):
        vector.wait_ge(ctx.mm.h, mm_mb)
        ctx.dve.inc(vector.tensor_scalar(bufs.mbrow, bufs.pp[3][0:1, 0:4],
                                         0.0, None, ALU.max))

    for p in range(2):
        wlist, specs = [], []
        for lh in range(2):
            l = 2 * p + lh
            for kind, wap in (("q", io.wq), ("k", io.wk), ("v", io.wv)):
                wlist.append(wap[:, 128 * l:128 * (l + 1)])
                if kind == "q":
                    specs.append([NS(kind="qk", r0=0, r1=128,
                                     dst=(lambda w, lh=lh: bufs.QT[:, S * lh + 512 * w:S * lh + 512 * (w + 1)]),
                                     scale=L0SC, cosrows=bufs.cosr,
                                     sinrows=bufs.sinr, vtm=None, vslot0=0)])
                elif kind == "k":
                    specs.append([NS(kind="qk", r0=0, r1=128,
                                     dst=(lambda w, lh=lh: bufs.KT[:, S * lh + 512 * w:S * lh + 512 * (w + 1)]),
                                     scale=None, cosrows=bufs.cosr,
                                     sinrows=bufs.sinr, vtm=None, vslot0=0)])
                else:
                    specs.append([NS(kind="v", r0=0, r1=128, dst=None,
                                     scale=None, cosrows=None, sinrows=None,
                                     vtm=bufs.VT[lh], vslot0=0)])
        ev_proj = _emit_proj(ctx, bufs, io.xg, wlist, specs, bufs.rot)
        for lh in range(2):
            l = 2 * p + lh
            _emit_norm_max(ctx, bufs,
                           (lambda c, lh=lh: bufs.QT[0:128, S * lh + 512 * c:S * lh + 512 * (c + 1)]),
                           128, bufs.nrow[0:1, 8:9], wait_ev=ev_proj)
            _emit_norm_max(ctx, bufs,
                           (lambda c, lh=lh: bufs.KT[0:128, S * lh + 512 * c:S * lh + 512 * (c + 1)]),
                           128, bufs.nrow[0:1, 9:10])
            _emit_bound(ctx, bufs, bufs.nrow[0:1, 8:9], bufs.nrow[0:1, 9:10],
                        bufs.mbrow[0:1, l:l + 1], bufs.expb[:, l:l + 1])
            _emit_flash(ctx, bufs, 0, 128,
                        (lambda kt, lh=lh: bufs.KT[0:128, S * lh + 128 * kt:S * lh + 128 * (kt + 1)]),
                        bufs.VT[lh],
                        (lambda qc, lh=lh: bufs.QT[0:128, S * lh + 512 * qc:S * lh + 512 * (qc + 1)]),
                        bufs.expb[:, l:l + 1], bufs.cb[l], io.ctxT, 128 * l)


def build_attn0_launch(stage=4):
    nc = _mk_bass()
    io = NS()
    io.xg = nc.dram_tensor("xg", [D, S], BF16, kind="ExternalInput").ap()
    io.wq = nc.dram_tensor("wq", [D, 512], F32, kind="ExternalInput").ap()
    io.wk = nc.dram_tensor("wk", [D, 512], F32, kind="ExternalInput").ap()
    io.wv = nc.dram_tensor("wv", [D, 512], F32, kind="ExternalInput").ap()
    io.tbl = nc.dram_tensor("tbl", [32, 4], F32, kind="ExternalInput").ap()
    io.oh = nc.dram_tensor("oh", [32, 1536], F32, kind="ExternalInput").ap()
    io.mbig = nc.dram_tensor("mbig", [128, CBW], F32, kind="ExternalInput").ap()
    io.cosr = nc.dram_tensor("cosr", [128, S], BF16, kind="ExternalInput").ap()
    io.sinr = nc.dram_tensor("sinr", [128, S], BF16, kind="ExternalInput").ap()
    io.ident = nc.dram_tensor("ident", [128, 128], BF16, kind="ExternalInput").ap()
    io.id16 = nc.dram_tensor("id16", [16, 16], F32, kind="ExternalInput").ap()
    io.rot = nc.dram_tensor("rot", [128, 128], BF16, kind="ExternalInput").ap()
    io.jrev = nc.dram_tensor("jrev", [128, 128], F32, kind="ExternalInput").ap()
    consts_in = nc.dram_tensor("consts", [128, len(CONSTS)], F32, kind="ExternalInput").ap()
    io.ctxT = nc.dram_tensor("ctxT", [512, S], BF16, kind="ExternalOutput").ap()
    with contextlib.ExitStack() as es:
        csb = es.enter_context(nc.sbuf_tensor("csb", [128, len(CONSTS)], F32))
        register_consts(nc, csb)
        block = es.enter_context(nc.Block())
        ctx = Ctx(nc, block, es)

        @block.sync
        def _(sync):
            ctx.ld.inc(sync.dma_start(out=csb[:, :], in_=consts_in), 16)

        phase_attn0(nc, ctx, es, io)

        @block.sync
        def _(sync):
            sync.wait_ge(ctx.st.h, ctx.st.n)
            sync.wait_ge(ctx.stp[0].h, ctx.stp[0].n)
            sync.wait_ge(ctx.stp[1].h, ctx.stp[1].n)
    return nc


def _test_a1(stage=4):
    import jax
    import jax.numpy as jnp
    rng = np.random.default_rng(3)
    xn = rng.standard_normal((S, D), dtype=np.float32) * 0.9
    xnT_bf = xn.T.astype(ml_dtypes.bfloat16)
    qkv_w = (rng.standard_normal((D, 3 * D), dtype=np.float32) * 0.02)
    table = (rng.standard_normal((32, 16), dtype=np.float32) * 0.1)

    nc = build_attn0_launch(stage)
    cosr, sinr = _l0_cos_sin_rows()
    in_maps = []
    for c in range(8):
        j = c % 4
        in_maps.append({
            "xg": xnT_bf,
            "wq": np.ascontiguousarray(qkv_w[:, 512 * j:512 * (j + 1)]),
            "wk": np.ascontiguousarray(qkv_w[:, D + 512 * j:D + 512 * (j + 1)]),
            "wv": np.ascontiguousarray(qkv_w[:, 2 * D + 512 * j:2 * D + 512 * (j + 1)]),
            "tbl": np.ascontiguousarray(table[:, 4 * j:4 * (j + 1)]), "oh": _l0_onehot(), "mbig": _mbig(),
            "cosr": cosr, "sinr": sinr,
            "ident": np.eye(128, dtype=ml_dtypes.bfloat16),
            "id16": np.eye(16, dtype=np.float32),
            "rot": _rot_host(0), "consts": consts_host(),
            "jrev": np.eye(128, dtype=np.float32)[::-1].copy(),
        })
    res = run_bass_kernel_spmd(nc, in_maps, core_ids=list(range(8)))

    cpu = jax.devices("cpu")[0]
    jax.config.update("jax_default_device", cpu)
    x = jnp.asarray(xn)[None]
    qkv = x @ jnp.asarray(qkv_w)
    q, k, v = jnp.split(qkv, 3, axis=-1)
    q = q.reshape(1, S, 16, 128)
    k = k.reshape(1, S, 16, 128)
    v = v.reshape(1, S, 16, 128)
    inv_freq = 1.0 / 10000.0 ** (jnp.arange(0, 128, 2, dtype=jnp.float32) / 128)
    ang = jnp.arange(S, dtype=jnp.float32)[:, None] * inv_freq[None, :]
    cos, sin = jnp.cos(ang), jnp.sin(ang)

    def rope(t):
        t1, t2 = t[..., :64], t[..., 64:]
        c = cos[None, :, None, :]
        s = sin[None, :, None, :]
        return jnp.concatenate([t1 * c - t2 * s, t2 * c + t1 * s], axis=-1)

    q, k = rope(q), rope(k)
    scl = 1.0 / math.sqrt(128.0)
    scores = jnp.einsum("bqhd,bkhd->bhqk", q, k) * scl
    ctxq = jnp.arange(S)[:, None]
    mem = jnp.arange(S)[None, :]
    n = jnp.maximum(ctxq - mem, 0)
    me = 16
    is_small = n < me
    nf = jnp.maximum(n, 1).astype(jnp.float32)
    large = (jnp.log(nf / me) / math.log(128 / me) * 16).astype(jnp.int32) + me
    bucket = jnp.where(is_small, n, jnp.minimum(large, 31))
    bias = jnp.asarray(table)[bucket]
    scores = scores + jnp.transpose(bias, (2, 0, 1))[None]
    causal = jnp.tril(jnp.ones((S, S), dtype=bool))
    scores = jnp.where(causal[None, None], scores, jnp.asarray(np.float32(-3.4e38)))
    probs = jax.nn.softmax(scores, axis=-1)
    ctx_ref = jnp.einsum("bhqk,bkhd->bqhd", probs, v)
    ctx_ref = np.asarray(ctx_ref)[0]

    qn = np.asarray(q)[0]
    kn = np.asarray(k)[0]
    vn = np.asarray(v)[0]
    for c in range(8):
        j = c % 4
        got = res.results[c]["ctxT"].astype(np.float32)
        if stage == 1.5:
            raw = res.results[c]["ctxT"]
            cb0 = np.frombuffer(raw[0:128, 0:1024].view(np.uint16).tobytes(),
                                dtype=np.float32).reshape(128, 512)
            vext = np.zeros(1536, dtype=np.float32)
            tblf = table.astype(np.float32)
            for dd in range(256):
                vext[512 + dd] = tblf[_l0_bucket(dd), 0] - tblf[31, 0]
            kk = np.arange(128)[:, None]
            cc = np.arange(512)[None, :]
            exp0 = vext[128 + cc - kk] + np.where((cc - CB0) >= kk, 0.0, NEG)
            mask = (cc - CBW // 2 - (CB0 - CBW // 2)) >= kk
            mask = (cc - CB0) >= kk
            d = np.abs(cb0 - exp0)
            print(f"core{c} cb0 unmasked: maxabs={d[mask].max():.4e} "
                  f"masked rel: {(d[~mask] / 1e30).max():.1e}")
            mrow = np.frombuffer(raw[384:385, 0:32].view(np.uint16).tobytes(),
                                 dtype=np.float32)[0:16]
            mexp = np.maximum(0.0, np.array([max(tblf[_l0_bucket(dd), h] - tblf[31, h]
                                for dd in range(256)) for h in range(16)]))
            print(f"core{c} mbrow err: {np.abs(mrow - mexp).max():.4e}")
            continue
        if stage < 3:
            qe = qn[:, 4 * j, :] * scl
            ke = kn[:, 4 * j, :]
            g = got[0:128].T
            print(f"core{c} QT: rel={np.linalg.norm(g - qe) / np.linalg.norm(qe):.4e}")
            g = got[128:256].T
            print(f"core{c} KT: rel={np.linalg.norm(g - ke) / np.linalg.norm(ke):.4e}")
            g256 = got[256:384]
            v_rec = np.concatenate([g256[:, 128 * t:128 * (t + 1)] for t in range(32)], axis=0)
            ve = vn[:, 4 * j, :]
            print(f"core{c} VT: rel={np.linalg.norm(v_rec - ve) / np.linalg.norm(ve):.4e}")
            continue
        if stage < 4:
            raw = res.results[c]["ctxT"]
            ex = np.frombuffer(raw[0:128, 0:16].view(np.uint16).tobytes(),
                               dtype=np.float32).reshape(128, 8)
            print(f"core{c} expb row0: {ex[0, 0:4]} row64: {ex[64, 0:4]}")
            nr = np.frombuffer(raw[200:201, 0:128].view(np.uint16).tobytes(),
                               dtype=np.float32)
            print(f"core{c} nrow chunks: {nr[0:8]}")
            print(f"core{c} q/k/t/B/nB: {nr[8]:.3f} {nr[9]:.3f} {nr[10]:.3f} {nr[11]:.3f} {nr[12]:.3f}")
            sq = raw[256:384, 0:512].astype(np.float32)
            qe = (qn[:, 4 * j, :] * scl)
            sq_exp = (qe[512 * 7:512 * 8].astype(ml_dtypes.bfloat16).astype(np.float32) ** 2).T
            print(f"core{c} sqs(last chunk K?) sample got={sq[0, 0:3]}")
            qmax_exp = ((qe.astype(ml_dtypes.bfloat16).astype(np.float32) ** 2).sum(-1)).max()
            kmax_exp = ((kn[:, 4 * j, :].astype(ml_dtypes.bfloat16).astype(np.float32) ** 2).sum(-1)).max()
            print(f"core{c} expected qmax={qmax_exp:.3f} kmax={kmax_exp:.3f} "
                  f"B={1.02 * math.sqrt(qmax_exp * kmax_exp):.3f}")
            continue
        for lh in ([0, 3] if c < 2 else [0]):
            g = got[128 * lh:128 * (lh + 1)].T
            ref = ctx_ref[:, 4 * j + lh, :]
            d = np.abs(g - ref)
            rel = np.linalg.norm(g - ref) / np.linalg.norm(ref)
            print(f"core{c} head{4 * j + lh}: rel_l2={rel:.4e} maxabs={d.max():.4f}")





def emit_outproj(ctx, bufs, ctxT_my, w_dram, x_sb, wait_ld=None):
    block = ctx.block

    @block.sync
    def _(sync):
        sync.wait_ge(ctx.mm.h, ctx.mm.n)
        if wait_ld is not None:
            sync.wait_ge(wait_ld[0].h, wait_ld[1])
        wg = getattr(ctx, "w_gate", None)
        if wg is not None:
            sync.wait_ge(wg[0].h, wg[1])
        ctx.ld.inc(sync.dma_start(
            out=bufs.wo_sb.rearrange("p (kt c) -> p kt c", kt=16),
            in_=w_dram.rearrange("(kt p) c -> p kt c", p=128)), 16)
    w_ready = ctx.ld.n

    rs_base = (ctx.rs[0].n, ctx.rs[1].n)
    mm0, act0, dve0, ev0 = ctx.mm.n, ctx.act.n, ctx.dve.n, ctx.ev.n

    def mm_after_t(t):
        return mm0 + 64 * (t + 1)

    def ev_after_t(t):
        return ev0 + 4 * (t + 1)

    @block.sync
    def _(sync):
        def emit_loads(tok0):
            for t in range(NT):
                if t >= 2:
                    sync.wait_ge(ctx.mm.h, mm_after_t(t - 2))
                sync.dma_start(
                    out=bufs.cxt[t % 2].rearrange("p (kt c) -> p kt c", kt=16),
                    in_=ctxT_my[:, tok0 + 128 * t:tok0 + 128 * (t + 1)].rearrange(
                        "(kt p) c -> p kt c", p=128)).then_inc(
                    ctx.rs[t % 2].h, 16)

        og = getattr(ctx, "op_gate", None)
        if og is not None:
            sync.wait_ge(og[0].h, og[1])
        if getattr(ctx, "op_switch", False):
            pid = sync.partition_id()
            for bi in sync.Switch(pid, 8):
                emit_loads(TOK * (bi % 4))
        else:
            emit_loads(0)
        for t in range(NT):
            ctx.rs[t % 2].n += 16

    @block.tensor
    def _(tensor):
        tensor.wait_ge(ctx.ld.h, w_ready)
        tensor.wait_ge(ctx.dve.h, dve0)
        tensor.wait_ge(ctx.act.h, act0)
        for t in range(NT):
            tensor.wait_ge(ctx.rs[t % 2].h, rs_base[t % 2] + 16 * (t // 2 + 1))
            if t >= 1:
                tensor.wait_ge(ctx.ev.h, ev_after_t(t - 1))
            for kt in range(ND):
                for nch in range(4):
                    ctx.mm.inc(tensor.matmul(
                        bufs.pp[nch][:, :],
                        bufs.cxt[t % 2][:, 128 * kt:128 * (kt + 1)],
                        bufs.wo_sb[:, D * kt + 512 * nch:D * kt + 512 * (nch + 1)],
                        start=(kt == 0), stop=(kt == ND - 1)))

    def dv(t, k):
        return dve0 + 9 * t + k

    @block.vector
    def _(vector):
        for t in range(NT):
            vector.wait_ge(ctx.mm.h, mm_after_t(t))
            if t >= 1:
                vector.wait_ge(ctx.act.h, act0 + 2 * (t - 1) + 1)
            for nch in range(4):
                ctx.ev.inc(vector.tensor_copy(
                    bufs.y[:, 512 * nch:512 * (nch + 1)], bufs.pp[nch][:, :]))
            ctx.dve.inc(vector.reduce_sum(bufs.stat[:, t:t + 1], bufs.y,
                                          axis=AX.X))
            vector.wait_ge(ctx.act.h, act0 + 2 * t + 1)
            ctx.dve.inc(vector.reduce_sum(bufs.stat[:, 8 + t:9 + t], bufs.tmp,
                                          axis=AX.X))
            vector.wait_ge(ctx.dve.h, dv(t, 2))
            ctx.dve.inc(vector.tensor_scalar(bufs.stat[:, 16 + t:17 + t],
                                             bufs.stat[:, t:t + 1], 1.0 / D,
                                             None, ALU.mult))
            ctx.dve.inc(vector.tensor_scalar(bufs.stat[:, 24 + t:25 + t],
                                             bufs.stat[:, 8 + t:9 + t], 1.0 / D,
                                             None, ALU.mult))
            vector.wait_ge(ctx.dve.h, dv(t, 4))
            ctx.dve.inc(vector.tensor_tensor(bufs.stat[:, 48:49],
                                             bufs.stat[:, 16 + t:17 + t],
                                             bufs.stat[:, 16 + t:17 + t], ALU.mult))
            vector.wait_ge(ctx.dve.h, dv(t, 5))
            ctx.dve.inc(vector.tensor_tensor(bufs.stat[:, 24 + t:25 + t],
                                             bufs.stat[:, 24 + t:25 + t],
                                             bufs.stat[:, 48:49], ALU.subtract))
            vector.wait_ge(ctx.act.h, act0 + 2 * t + 2)
            ctx.dve.inc(vector.reciprocal(bufs.stat[:, 40 + t:41 + t],
                                          bufs.stat[:, 32 + t:33 + t]))
            vector.wait_ge(ctx.dve.h, dv(t, 7))
            ctx.dve.inc(vector.tensor_scalar(
                bufs.tmp, bufs.y, bufs.stat[:, 16 + t:17 + t],
                bufs.stat[:, 40 + t:41 + t], ALU.subtract, ALU.mult))
            vector.wait_ge(ctx.dve.h, dv(t, 8))
            ctx.dve.inc(vector.tensor_tensor(
                x_sb[:, D * t:D * (t + 1)], x_sb[:, D * t:D * (t + 1)],
                bufs.tmp, ALU.add))

    @block.scalar
    def _(scalar):
        for t in range(NT):
            scalar.wait_ge(ctx.ev.h, ev_after_t(t))
            if t >= 1:
                scalar.wait_ge(ctx.dve.h, dv(t - 1, 9))
            ctx.act.inc(scalar.activation(out=bufs.tmp, in_=bufs.y,
                                          func=AF.Square))
            scalar.wait_ge(ctx.dve.h, dv(t, 6))
            ctx.act.inc(scalar.activation(
                out=bufs.stat[:, 32 + t:33 + t], in_=bufs.stat[:, 24 + t:25 + t],
                func=AF.Sqrt, bias=1e-5, scale=1.0))
    return ctx.dve.n


def phase_outproj(nc, ctx, es, ctxT_my, w_ap, x_src, x_dst, sfx=""):
    block = ctx.block
    sb = lambda n, shp, dt: es.enter_context(nc.sbuf_tensor(n + sfx, shp, dt))[:, :]
    bufs = NS()
    bufs.wo_sb = sb("wo_sb", [128, 16 * D], BF16)
    bufs.cxt = [sb("cxt0", [128, 16 * 128], BF16), sb("cxt1", [128, 16 * 128], BF16)]
    bufs.y = sb("y_sb", [128, D], F32)
    bufs.tmp = sb("tmp_sb", [128, D], F32)
    bufs.stat = sb("stat", [128, 64], F32)
    bufs.pp = [es.enter_context(nc.psum_tensor(f"pp{i}" + sfx, [128, 512], F32))[:, :]
               for i in range(4)]
    x_sb = es.enter_context(nc.sbuf_tensor("x_sb" + sfx, [128, NT * D], F32))
    ld0 = ctx.ld.n

    @block.sync
    def _(sync):
        ctx.ld.inc(sync.dma_start(
            out=x_sb[:, :].rearrange("p (t d) -> p t d", t=NT),
            in_=x_src.rearrange("(t p) d -> p t d", p=128)), 16)

    emit_outproj(ctx, bufs, ctxT_my, w_ap, x_sb[:, :], wait_ld=(ctx.ld, ctx.ld.n))

    @block.sync
    def _(sync):
        sync.wait_ge(ctx.dve.h, ctx.dve.n)
        ctx.st.inc(sync.dma_start(
            out=x_dst.rearrange("(t p) d -> p t d", p=128),
            in_=x_sb[:, :].rearrange("p (t d) -> p t d", t=NT)), 16)


def build_outproj_launch():
    nc = _mk_bass()
    x_in = nc.dram_tensor("x", [TOK, D], F32, kind="ExternalInput").ap()
    ctxT_my = nc.dram_tensor("ctxT_my", [D, TOK], BF16, kind="ExternalInput").ap()
    w_in = nc.dram_tensor("w", [D, D], F32, kind="ExternalInput").ap()
    consts_in = nc.dram_tensor("consts", [128, len(CONSTS)], F32, kind="ExternalInput").ap()
    xo = nc.dram_tensor("xo", [TOK, D], F32, kind="ExternalOutput").ap()
    with contextlib.ExitStack() as es:
        csb = es.enter_context(nc.sbuf_tensor("csb", [128, len(CONSTS)], F32))
        register_consts(nc, csb)
        block = es.enter_context(nc.Block())
        ctx = Ctx(nc, block, es)

        @block.sync
        def _(sync):
            ctx.ld.inc(sync.dma_start(out=csb[:, :], in_=consts_in), 16)

        phase_outproj(nc, ctx, es, ctxT_my, w_in, x_in, xo)

        @block.sync
        def _(sync):
            sync.wait_ge(ctx.st.h, ctx.st.n)
    return nc


def _test_a2():
    rng = np.random.default_rng(11)
    x = rng.standard_normal((TOK, D), dtype=np.float32)
    cT = (rng.standard_normal((D, TOK), dtype=np.float32) * 0.5).astype(ml_dtypes.bfloat16)
    w = rng.standard_normal((D, D), dtype=np.float32) * 0.02
    nc = build_outproj_launch()
    in_maps = [{"x": x, "ctxT_my": cT, "w": w, "consts": consts_host()}
               for _ in range(8)]
    res = run_bass_kernel_spmd(nc, in_maps, core_ids=list(range(8)))
    y = cT.astype(np.float32).T @ w
    m = y.mean(-1, keepdims=True); v = y.var(-1, keepdims=True)
    ref = x + (y - m) / np.sqrt(v + 1e-5)
    for c in [0, 5]:
        got = res.results[c]["xo"]
        rel = np.linalg.norm(got - ref) / np.linalg.norm(ref)
        print(f"core{c} A2 outproj: rel={rel:.4e} maxabs={np.abs(got - ref).max():.4f}")


def _l1_cos_sin_interleaved():
    half = 32
    inv_freq = (1.0 / (10000.0 ** (np.arange(half, dtype=np.float32) / np.float32(half)))).astype(np.float32)
    pos = np.arange(S, dtype=np.float32)
    fr = np.outer(pos, inv_freq).astype(np.float32)
    c = np.cos(fr).astype(np.float32).T
    s = np.sin(fr).astype(np.float32).T
    cosr = np.repeat(c, 2, axis=0)
    sinr = np.empty((64, S), dtype=np.float32)
    sinr[0::2] = -s
    sinr[1::2] = s
    return (cosr.astype(ml_dtypes.bfloat16), sinr.astype(ml_dtypes.bfloat16))


def _rot_l1_host():
    r = np.zeros((128, 128), dtype=np.float32)
    for i in range(32):
        r[2 * i, 2 * i + 1] = 1.0
        r[2 * i + 1, 2 * i] = 1.0
    return r.astype(ml_dtypes.bfloat16)


def phase_attn1(nc, ctx, es, io):
    block = ctx.block
    sb = lambda n, shp, dt: es.enter_context(nc.sbuf_tensor(n + "_a1", shp, dt))[:, :]
    bufs = NS()
    bufs.rhs0 = sb("rhs0", [128, 512], BF16)
    bufs.rhs1 = sb("rhs1", [128, 512], BF16)
    bufs.rhs2 = sb("rhs2", [128, 512], BF16)
    bufs.rhs3 = sb("rhs3", [128, 512], BF16)
    bufs.wsb = sb("wsb", [128, 5 * 16 * 128], BF16)
    bufs.qraw = sb("qraw", [128, 5 * 512], BF16)
    bufs.qs_sb = sb("qs_sb", [128, 5 * 512], BF16)
    bufs.vraw = sb("vraw", [128, 512], BF16)
    bufs.tmp1 = sb("tmp1", [128, 512], BF16)
    bufs.tmp2 = sb("tmp2", [128, 512], BF16)
    bufs.sqs = sb("sqs", [128, 512], BF16)
    bufs.ptile0 = sb("ptile0", [128, 512], BF16)
    bufs.ptile1 = sb("ptile1", [128, 512], BF16)
    bufs.ptile2 = sb("ptile2", [128, 512], BF16)
    bufs.ctile0 = sb("ctile0", [128, 512], BF16)
    bufs.ctile1 = sb("ctile1", [128, 512], BF16)
    bufs.oraw = [sb("oraw0", [128, NW * 512], F32),
                 sb("oraw1", [128, NW * 512], F32)]
    bufs.den_row = sb("den_row", [1, NW * 512], F32)
    bufs.brec = [sb("brec0", [128, 512], F32), sb("brec1", [128, 512], F32)]
    bufs.dbgrow = sb("dbgrow", [1, 1024], F32)
    bufs.nrow = sb("nrow", [1, 64], F32)
    bufs.expb = sb("expb", [128, 8], F32)
    bufs.onescol = sb("onescol", [128, 1], BF16)
    bufs.onesrow32 = sb("onesrow32", [1, 128], F32)
    bufs.ident = sb("identsb", [128, 128], BF16)
    bufs.rot = sb("rotsb", [128, 128], BF16)
    bufs.cosr = sb("cosr_sb", [64, S], BF16)
    bufs.sinr = sb("sinr_sb", [64, S], BF16)
    bufs.mbig = sb("mbigsb", [128, CBW], F32)
    bufs.QT = sb("QT", [64, 4 * S], BF16)
    bufs.KT = sb("KT", [64, S], BF16)
    bufs.VT = [sb("VT0", [128, 32 * 65], BF16)]
    bufs.cb = None
    bufs.pp = [es.enter_context(nc.psum_tensor(f"pp{i}_a1", [128, 512], F32))[:, :]
               for i in range(6)]
    bufs.tp = es.enter_context(nc.psum_tensor("tpps_a1", [128, 128], BF16))[:, :]
    bufs.psh = es.enter_context(nc.psum_tensor("pshps_a1", [128, 512], F32))[:, :]
    ctx.fdbg = None
    ctx.pdbg = None
    ctx._fl_hist = []
    ctx._fl_call = 0

    @block.sync
    def _(sync):
        for dst, srcap in [(bufs.mbig, io.mbig), (bufs.cosr, io.cosr),
                           (bufs.sinr, io.sinr), (bufs.ident, io.ident),
                           (bufs.rot, io.rot)]:
            ctx.ld.inc(sync.dma_start(out=dst, in_=srcap), 16)
    ld_pre = ctx.ld.n

    @block.vector
    def _(vector):
        vector.wait_ge(ctx.ld.h, ld_pre)
        vector.memset(bufs.onescol, 1.0)
        for k in range(32):
            vector.memset(bufs.VT[0][:, 65 * k + 64:65 * k + 65], 1.0)
        ctx.dve.inc(vector.memset(bufs.onesrow32, 1.0))

    for g in range(2):
        wlist = []
        specs = []
        for hh in range(4):
            wlist.append(io.wq[:, 64 * (4 * g + hh):64 * (4 * g + hh + 1)])
            specs.append([NS(kind="qk", r0=0, r1=64,
                             dst=(lambda w, hh=hh: bufs.QT[0:64, S * hh + 512 * w:S * hh + 512 * (w + 1)]),
                             scale=L1SC, cosrows=bufs.cosr,
                             sinrows=bufs.sinr, vtm=None, vslot0=0)])
        wlist.append([(io.wk[:, 64 * g:64 * (g + 1)], 0),
                      (io.wv[:, 64 * g:64 * (g + 1)], 64)])
        specs.append([NS(kind="qk", r0=0, r1=64,
                         dst=(lambda w: bufs.KT[0:64, 512 * w:512 * (w + 1)]),
                         scale=None, cosrows=bufs.cosr,
                         sinrows=bufs.sinr, vtm=None, vslot0=0),
                      NS(kind="v", r0=64, r1=128, dst=None, scale=None,
                         cosrows=None, sinrows=None, vtm=bufs.VT[0], vslot0=0,
                         vstep=65)])
        ev_proj = _emit_proj(ctx, bufs, io.xg, wlist, specs, bufs.rot)
        _emit_norm_max(ctx, bufs,
                       (lambda c: bufs.KT[0:64, 512 * c:512 * (c + 1)]),
                       64, bufs.nrow[0:1, 9:10], wait_ev=ev_proj)
        for hh in range(4):
            l = 4 * g + hh
            _emit_norm_max(ctx, bufs,
                           (lambda c, hh=hh: bufs.QT[0:64, S * hh + 512 * c:S * hh + 512 * (c + 1)]),
                           64, bufs.nrow[0:1, 8:9])
            _emit_bound(ctx, bufs, bufs.nrow[0:1, 8:9], bufs.nrow[0:1, 9:10],
                        None, bufs.expb[:, l:l + 1])
            _emit_flash(ctx, bufs, 1, 64,
                        (lambda kt: bufs.KT[0:64, 128 * kt:128 * (kt + 1)]),
                        bufs.VT[0],
                        (lambda qc, hh=hh: bufs.QT[0:64, S * hh + 512 * qc:S * hh + 512 * (qc + 1)]),
                        bufs.expb[:, l:l + 1], None, io.ctxT, 64 * l)


def build_attn1_launch():
    nc = _mk_bass()
    io = NS()
    io.xg = nc.dram_tensor("xg", [D, S], BF16, kind="ExternalInput").ap()
    io.wq = nc.dram_tensor("wq", [D, 512], F32, kind="ExternalInput").ap()
    io.wk = nc.dram_tensor("wk", [D, 128], F32, kind="ExternalInput").ap()
    io.wv = nc.dram_tensor("wv", [D, 128], F32, kind="ExternalInput").ap()
    io.mbig = nc.dram_tensor("mbig", [128, CBW], F32, kind="ExternalInput").ap()
    io.cosr = nc.dram_tensor("cosr", [64, S], BF16, kind="ExternalInput").ap()
    io.sinr = nc.dram_tensor("sinr", [64, S], BF16, kind="ExternalInput").ap()
    io.ident = nc.dram_tensor("ident", [128, 128], BF16, kind="ExternalInput").ap()
    io.rot = nc.dram_tensor("rot", [128, 128], BF16, kind="ExternalInput").ap()
    consts_in = nc.dram_tensor("consts", [128, len(CONSTS)], F32, kind="ExternalInput").ap()
    io.ctxT = nc.dram_tensor("ctxT", [512, S], BF16, kind="ExternalOutput").ap()
    with contextlib.ExitStack() as es:
        csb = es.enter_context(nc.sbuf_tensor("csb", [128, len(CONSTS)], F32))
        register_consts(nc, csb)
        block = es.enter_context(nc.Block())
        ctx = Ctx(nc, block, es)

        @block.sync
        def _(sync):
            ctx.ld.inc(sync.dma_start(out=csb[:, :], in_=consts_in), 16)

        phase_attn1(nc, ctx, es, io)

        @block.sync
        def _(sync):
            sync.wait_ge(ctx.st.h, ctx.st.n)
            sync.wait_ge(ctx.stp[0].h, ctx.stp[0].n)
            sync.wait_ge(ctx.stp[1].h, ctx.stp[1].n)
    return nc


def _test_b1():
    import jax
    jax.config.update("jax_default_device", jax.devices("cpu")[0])
    import jax.numpy as jnp
    rng = np.random.default_rng(5)
    xn = rng.standard_normal((S, D), dtype=np.float32) * 0.9
    xnT_bf = xn.T.astype(ml_dtypes.bfloat16)
    Wq = rng.standard_normal((D, D), dtype=np.float32) * 0.02
    Wk = rng.standard_normal((D, 512), dtype=np.float32) * 0.02
    Wv = rng.standard_normal((D, 512), dtype=np.float32) * 0.02
    nc = build_attn1_launch()
    cosr, sinr = _l1_cos_sin_interleaved()
    in_maps = []
    for c in range(8):
        j = c % 4
        in_maps.append({
            "xg": xnT_bf,
            "wq": np.ascontiguousarray(Wq[:, 512 * j:512 * (j + 1)]),
            "wk": np.ascontiguousarray(Wk[:, 128 * j:128 * (j + 1)]),
            "wv": np.ascontiguousarray(Wv[:, 128 * j:128 * (j + 1)]),
            "mbig": _mbig(), "cosr": cosr, "sinr": sinr,
            "ident": np.eye(128, dtype=ml_dtypes.bfloat16),
            "rot": _rot_l1_host(), "consts": consts_host(),
        })
    res = run_bass_kernel_spmd(nc, in_maps, core_ids=list(range(8)))

    x = jnp.asarray(xn)[None]
    q = (x @ jnp.asarray(Wq)).reshape(1, S, 32, 64).transpose(0, 2, 1, 3)
    k = (x @ jnp.asarray(Wk)).reshape(1, S, 8, 64).transpose(0, 2, 1, 3)
    v = (x @ jnp.asarray(Wv)).reshape(1, S, 8, 64).transpose(0, 2, 1, 3)
    half = 32
    inv_freq = 1.0 / 10000.0 ** (jnp.arange(half, dtype=jnp.float32) / half)
    pos = jnp.arange(S, dtype=jnp.float32)
    fr = jnp.outer(pos, inv_freq)
    cos, sin = jnp.cos(fr), jnp.sin(fr)

    def rope(t):
        e = t[..., ::2]; o = t[..., 1::2]
        c = cos[None, None]; s = sin[None, None]
        re = e * c - o * s
        ro = e * s + o * c
        return jnp.stack((re, ro), axis=-1).reshape(t.shape)

    q, k = rope(q), rope(k)
    kr = jnp.repeat(k, 4, axis=1)
    vr = jnp.repeat(v, 4, axis=1)
    sc = jnp.einsum("bhqd,bhkd->bhqk", q, kr) / jnp.sqrt(jnp.float32(64))
    causal = jnp.tril(jnp.ones((S, S), dtype=bool))
    sc = jnp.where(causal[None, None], sc, -jnp.finfo(jnp.float32).max)
    pr = jax.nn.softmax(sc.astype(jnp.float32), axis=-1)
    ctx_ref = np.asarray(jnp.einsum("bhqk,bhkd->bhqd", pr, vr))[0]

    for c in [0, 1]:
        j = c % 4
        got = res.results[c]["ctxT"].astype(np.float32)
        for l in [0, 5]:
            h = 8 * j + l
            g = got[64 * l:64 * (l + 1)].T
            ref = ctx_ref[h]
            rel = np.linalg.norm(g - ref) / np.linalg.norm(ref)
            print(f"core{c} qhead{h}: rel={rel:.4e} maxabs={np.abs(g - ref).max():.4f}")




def phase_ffn1(nc, ctx, es, io, x_src, x_dst):
    block = ctx.block
    sb = lambda n, shp, dt: es.enter_context(nc.sbuf_tensor(n + "_f1", shp, dt))[:, :]
    bufs = NS()
    x_sb = es.enter_context(nc.sbuf_tensor("x_sb_f1", [128, NT * D], F32))
    bufs.xnT = sb("xnT", [128, ND * TOK], BF16)
    bufs.fuseT = sb("fuseT", [128, 32 * 512], BF16)
    bufs.xn = bufs.fuseT
    bufs.wsb = sb("wsb", [128, 2 * 32 * 128], BF16)
    bufs.wdb = sb("wdb", [128, 3 * 16 * 128], BF16)
    bufs.yfm = sb("yfm", [128, 16 * 512], F32)
    bufs.y = sb("y_sb", [128, D], F32)
    bufs.tmp = sb("tmp_sb", [128, D], F32)
    bufs.gt = sb("gt", [128, 512], BF16)
    bufs.stat = sb("stat", [128, 64], F32)
    bufs.ident = sb("ident_sb", [128, 128], BF16)
    bufs.ident32 = sb("ident32_sb", [128, 128], F32)
    sgs = sb("sgs", [128, 64], F32)
    sus = sb("sus", [128, 64], F32)
    sds = sb("sds", [128, 16], F32)
    bufs.pg = [es.enter_context(nc.psum_tensor(f"pg{i}_f1", [128, 512], F32))[:, :] for i in range(2)]
    bufs.pu = [es.enter_context(nc.psum_tensor(f"pu{i}_f1", [128, 512], F32))[:, :] for i in range(2)]
    bufs.pd = [es.enter_context(nc.psum_tensor(f"pd{i}_f1", [128, 512], F32))[:, :] for i in range(2)]
    bufs.tp32 = es.enter_context(nc.psum_tensor("tp32_f1", [128, 128], F32))[:, :]
    bufs.tpL = es.enter_context(nc.psum_tensor("tpL_f1", [128, 128], BF16))
    bufs.tp32b = bufs.pd[0][:, 0:128]
    bufs.tpL2 = bufs.pd[1][:, 0:64].bitcast(BF16)

    @block.sync
    def _(sync):
        ctx.ld.inc(sync.dma_start(
            out=x_sb[:, :].rearrange("p (t d) -> p t d", t=NT),
            in_=x_src.rearrange("(t p) d -> p t d", p=128)), 16)
        for dst, srcap in [(bufs.ident, io.ident), (bufs.ident32, io.ident32),
                           (sgs, io.sg), (sus, io.su), (sds, io.sd)]:
            ctx.ld.inc(sync.dma_start(out=dst, in_=srcap), 16)
    ld_pre = ctx.ld.n

    ctx.sq_scratch = bufs.tmp
    emit_ffn1(ctx, bufs, x_sb[:, :], io.wg, io.wu, io.wd, sgs, sus, sds,
              wait=(ctx.ld, ld_pre), pcw=getattr(io, "pcw", None))

    @block.sync
    def _(sync):
        sync.wait_ge(ctx.dve.h, ctx.dve.n)
        ctx.st.inc(sync.dma_start(
            out=x_dst.rearrange("(t p) d -> p t d", p=128),
            in_=x_sb[:, :].rearrange("p (t d) -> p t d", t=NT)), 16)


def build_ffn1_launch():
    nc = _mk_bass()
    io = NS()
    x_in = nc.dram_tensor("x", [TOK, D], F32, kind="ExternalInput").ap()
    wg_i8 = nc.dram_tensor("wg", [D, 8192], I8, kind="ExternalInput").ap()
    wu_i8 = nc.dram_tensor("wu", [D, 8192], I8, kind="ExternalInput").ap()
    wd_i8 = nc.dram_tensor("wd", [8192, D], I8, kind="ExternalInput").ap()
    io.wg = nc.dram_tensor("wg_b", [D, 8192], BF16, kind="Internal").ap()
    io.wu = nc.dram_tensor("wu_b", [D, 8192], BF16, kind="Internal").ap()
    io.wd = nc.dram_tensor("wd_b", [8192, D], BF16, kind="Internal").ap()
    io.sg = nc.dram_tensor("sg", [128, 64], F32, kind="ExternalInput").ap()
    io.su = nc.dram_tensor("su", [128, 64], F32, kind="ExternalInput").ap()
    io.sd = nc.dram_tensor("sd", [128, 16], F32, kind="ExternalInput").ap()
    io.ident = nc.dram_tensor("ident", [128, 128], BF16, kind="ExternalInput").ap()
    io.ident32 = nc.dram_tensor("ident32", [128, 128], F32, kind="ExternalInput").ap()
    consts_in = nc.dram_tensor("consts", [128, len(CONSTS)], F32, kind="ExternalInput").ap()
    xo = nc.dram_tensor("xo", [TOK, D], F32, kind="ExternalOutput").ap()
    with contextlib.ExitStack() as es:
        csb = es.enter_context(nc.sbuf_tensor("csb", [128, len(CONSTS)], F32))
        register_consts(nc, csb)
        block = es.enter_context(nc.Block())
        ctx = Ctx(nc, block, es)

        @block.sync
        def _(sync):
            ctx.ld.inc(sync.dma_start(out=csb[:, :], in_=consts_in), 16)

        @block.gpsimd
        def _(gpsimd):
            for src_ap, dst_ap, rstep, nr in (
                    (wg_i8, io.wg, 512, D), (wu_i8, io.wu, 512, D),
                    (wd_i8, io.wd, 2048, 8192)):
                for r0 in range(0, nr, rstep):
                    ctx.pc.inc(gpsimd.dma_start(
                        out=dst_ap[r0:r0 + rstep, :],
                        in_=src_ap[r0:r0 + rstep, :]), 16)

        io.pcw = (ctx.pc, 192)
        phase_ffn1(nc, ctx, es, io, x_in, xo)

        @block.sync
        def _(sync):
            sync.wait_ge(ctx.st.h, ctx.st.n)
    return nc


def _test_f1():
    rng = np.random.default_rng(13)
    x = rng.standard_normal((TOK, D), dtype=np.float32)
    wg = rng.integers(-128, 128, size=(D, 8192)).astype(np.int8)
    wu = rng.integers(-128, 128, size=(D, 8192)).astype(np.int8)
    wd = rng.integers(-128, 128, size=(8192, D)).astype(np.int8)
    sg = (rng.random(8192, dtype=np.float32) * 0.02 + 0.001)
    su = (rng.random(8192, dtype=np.float32) * 0.02 + 0.001)
    sd = (rng.random(D, dtype=np.float32) * 0.02 + 0.001)
    nc = build_ffn1_launch()
    in_maps = [{
        "x": x, "wg": wg, "wu": wu, "wd": wd,
        "sg": np.ascontiguousarray(sg.reshape(64, 128).T),
        "su": np.ascontiguousarray(su.reshape(64, 128).T),
        "sd": np.ascontiguousarray(sd.reshape(16, 128).T),
        "ident": np.eye(128, dtype=ml_dtypes.bfloat16),
        "ident32": np.eye(128, dtype=np.float32),
        "consts": consts_host(),
    } for _ in range(8)]
    res = run_bass_kernel_spmd(nc, in_maps, core_ids=list(range(8)))
    m = x.mean(-1, keepdims=True); v = x.var(-1, keepdims=True)
    xn = (x - m) / np.sqrt(v + 1e-5)
    gate = xn @ wg.astype(np.float32) * sg
    gate = gate / (1.0 + np.exp(-gate))
    up = xn @ wu.astype(np.float32) * su
    y = (gate * up) @ wd.astype(np.float32) * sd
    m2 = y.mean(-1, keepdims=True); v2 = y.var(-1, keepdims=True)
    ref = x + (y - m2) / np.sqrt(v2 + 1e-5)
    for c in [0, 3]:
        got = res.results[c]["xo"]
        rel = np.linalg.norm(got - ref) / np.linalg.norm(ref)
        print(f"core{c} F1 ffn1: rel={rel:.4e} maxabs={np.abs(got - ref).max():.4f}")


def emit_ln_residual(ctx, bufs, x_sb, t, wait_y):
    block = ctx.block
    tt = t % 8
    act0, dve0 = ctx.act.n, ctx.dve.n

    @block.scalar
    def _(scalar):
        scalar.wait_ge(wait_y[0].h, wait_y[1])
        scalar.wait_ge(ctx.dve.h, dve0)
        ctx.act.inc(scalar.activation(out=bufs.tmp, in_=bufs.y, func=AF.Square))
        scalar.wait_ge(ctx.dve.h, dve0 + 6)
        ctx.act.inc(scalar.activation(
            out=bufs.stat[:, 32 + tt:33 + tt], in_=bufs.stat[:, 24 + tt:25 + tt],
            func=AF.Sqrt, bias=1e-5, scale=1.0))

    @block.vector
    def _(vector):
        vector.wait_ge(wait_y[0].h, wait_y[1])
        ctx.dve.inc(vector.reduce_sum(bufs.stat[:, tt:tt + 1], bufs.y, axis=AX.X))
        vector.wait_ge(ctx.act.h, act0 + 1)
        ctx.dve.inc(vector.reduce_sum(bufs.stat[:, 8 + tt:9 + tt], bufs.tmp,
                                      axis=AX.X))
        vector.wait_ge(ctx.dve.h, dve0 + 2)
        ctx.dve.inc(vector.tensor_scalar(bufs.stat[:, 16 + tt:17 + tt],
                                         bufs.stat[:, tt:tt + 1], 1.0 / D,
                                         None, ALU.mult))
        ctx.dve.inc(vector.tensor_scalar(bufs.stat[:, 24 + tt:25 + tt],
                                         bufs.stat[:, 8 + tt:9 + tt], 1.0 / D,
                                         None, ALU.mult))
        vector.wait_ge(ctx.dve.h, dve0 + 4)
        ctx.dve.inc(vector.tensor_tensor(bufs.stat[:, 48:49],
                                         bufs.stat[:, 16 + tt:17 + tt],
                                         bufs.stat[:, 16 + tt:17 + tt], ALU.mult))
        vector.wait_ge(ctx.dve.h, dve0 + 5)
        ctx.dve.inc(vector.tensor_tensor(bufs.stat[:, 24 + tt:25 + tt],
                                         bufs.stat[:, 24 + tt:25 + tt],
                                         bufs.stat[:, 48:49], ALU.subtract))
        vector.wait_ge(ctx.act.h, act0 + 2)
        ctx.dve.inc(vector.reciprocal(bufs.stat[:, 40 + tt:41 + tt],
                                      bufs.stat[:, 32 + tt:33 + tt]))
        vector.wait_ge(ctx.dve.h, dve0 + 7)
        ctx.dve.inc(vector.tensor_scalar(
            bufs.tmp, bufs.y, bufs.stat[:, 16 + tt:17 + tt],
            bufs.stat[:, 40 + tt:41 + tt], ALU.subtract, ALU.mult))
        vector.wait_ge(ctx.dve.h, dve0 + 8)
        ctx.dve.inc(vector.tensor_tensor(
            x_sb[:, D * t:D * (t + 1)], x_sb[:, D * t:D * (t + 1)],
            bufs.tmp, ALU.add))
    return ctx.act.n, ctx.dve.n


def emit_ffn1(ctx, bufs, x_sb, wg_d, wu_d, wd_d, sg_sb, su_sb, sd_sb, wait=None, pcw=None):
    block = ctx.block

    emit_ln_transpose(ctx, x_sb, bufs.xn, bufs.ident, (bufs.tpL, bufs.tpL2),
                      bufs.stat, bufs.xnT, xnT_dram=None, wait=wait)
    ln_ev = ctx.ev.n

    for tc in range(2):
        for half in range(2):
            mi0 = 32 * half
            wg_base = (ctx.wgp[0].n, ctx.wgp[1].n)
            mm0, act0, dve0 = ctx.mm.n, ctx.act.n, ctx.dve.n

            @block.sync
            def _(sync):
                if tc == 0 and half == 0 and pcw is not None:
                    sync.wait_ge(pcw[0].h, pcw[1])
                for i in range(32):
                    mi = mi0 + i
                    sb0 = 4096 * (i % 2)
                    sync.wait_ge(ctx.mm.h, mm0 + 32 * (i - 1) if i >= 2 else mm0)
                    ctx.wgp[0].inc(sync.dma_start(
                        out=bufs.wsb[:, sb0:sb0 + 16 * 128].rearrange(
                            "p (kt c) -> p kt c", kt=16),
                        in_=wg_d[:, 128 * mi:128 * (mi + 1)].rearrange(
                            "(kt p) c -> p kt c", p=128)), 16)
                    ctx.wgp[0].inc(sync.dma_start(
                        out=bufs.wsb[:, sb0 + 16 * 128:sb0 + 32 * 128].rearrange(
                            "p (kt c) -> p kt c", kt=16),
                        in_=wu_d[:, 128 * mi:128 * (mi + 1)].rearrange(
                            "(kt p) c -> p kt c", p=128)), 16)

            @block.tensor
            def _(tensor):
                tensor.wait_ge(ctx.ev.h, ln_ev)
                tensor.wait_ge(ctx.act.h, act0)
                tensor.wait_ge(ctx.dve.h, dve0)
                for i in range(32):
                    tensor.wait_ge(ctx.wgp[0].h, wg_base[0] + 32 * (i + 1))
                    if i >= 2:
                        tensor.wait_ge(ctx.act.h, act0 + 2 * (i - 1))
                    sb0 = 4096 * (i % 2)
                    for kt in range(ND):
                        ctx.mm.inc(tensor.matmul(
                            bufs.pg[i % 2][:, :],
                            bufs.wsb[:, sb0 + 128 * kt:sb0 + 128 * (kt + 1)],
                            bufs.xnT[:, TOK * kt + 512 * tc:TOK * kt + 512 * (tc + 1)],
                            start=(kt == 0), stop=(kt == ND - 1)))
                        ctx.mm.inc(tensor.matmul(
                            bufs.pu[i % 2][:, :],
                            bufs.wsb[:, sb0 + 128 * (16 + kt):sb0 + 128 * (17 + kt)],
                            bufs.xnT[:, TOK * kt + 512 * tc:TOK * kt + 512 * (tc + 1)],
                            start=(kt == 0), stop=(kt == ND - 1)))

            @block.scalar
            def _(scalar):
                for i in range(32):
                    mi = mi0 + i
                    scalar.wait_ge(ctx.mm.h, mm0 + 32 * (i + 1))
                    if i >= 1:
                        scalar.wait_ge(ctx.dve.h, dve0 + i)
                    ctx.act.inc(scalar.activation(
                        out=bufs.gt, in_=bufs.pg[i % 2][:, :], func=AF.Silu,
                        bias=0.0, scale=sg_sb[:, mi:mi + 1]))
                    ctx.act.inc(scalar.activation(
                        out=bufs.fuseT[:, 512 * i:512 * (i + 1)],
                        in_=bufs.pu[i % 2][:, :], func=AF.Copy,
                        bias=0.0, scale=su_sb[:, mi:mi + 1]))

            @block.vector
            def _(vector):
                for i in range(32):
                    vector.wait_ge(ctx.act.h, act0 + 2 * (i + 1))
                    ctx.dve.inc(vector.tensor_tensor(
                        bufs.fuseT[:, 512 * i:512 * (i + 1)], bufs.gt,
                        bufs.fuseT[:, 512 * i:512 * (i + 1)], ALU.mult))
            fuse_dve = ctx.dve.n

            wd_base = ctx.wgp[1].n
            mm1, act1, dve1 = ctx.mm.n, ctx.act.n, ctx.dve.n

            @block.sync
            def _(sync):
                for me in range(ND):
                    for h in range(2):
                        g = 2 * me + h
                        s = g % 3
                        if g >= 3:
                            pme, ph = divmod(g - 3, 2)
                            sync.wait_ge(ctx.mm.h,
                                         mm1 + 32 * pme + 16 * (ph + 1))
                        else:
                            sync.wait_ge(ctx.mm.h, mm0)
                        ctx.wgp[1].inc(sync.dma_start(
                            out=bufs.wdb[:, 2048 * s:2048 * (s + 1)].rearrange(
                                "p (kt c) -> p kt c", kt=16),
                            in_=wd_d[32 * 128 * half + 16 * 128 * h:
                                     32 * 128 * half + 16 * 128 * (h + 1),
                                     128 * me:128 * (me + 1)].rearrange(
                                "(kt p) c -> p kt c", p=128)), 16)

            @block.tensor
            def _(tensor):
                tensor.wait_ge(ctx.dve.h, fuse_dve)
                for me in range(ND):
                    if me >= 2:
                        tensor.wait_ge(ctx.dve.h, dve1 + me - 1)
                    for kt in range(32):
                        if kt % 16 == 0:
                            g = 2 * me + kt // 16
                            tensor.wait_ge(ctx.wgp[1].h,
                                           wd_base + 16 * (g + 1))
                            db0 = 2048 * (g % 3)
                        ctx.mm.inc(tensor.matmul(
                            bufs.pd[me % 2][:, :],
                            bufs.wdb[:, db0 + 128 * (kt % 16):
                                     db0 + 128 * (kt % 16 + 1)],
                            bufs.fuseT[:, 512 * kt:512 * (kt + 1)],
                            start=(kt == 0), stop=(kt == 31)))

            @block.vector
            def _(vector):
                for me in range(ND):
                    vector.wait_ge(ctx.mm.h, mm1 + 32 * (me + 1))
                    yslice = bufs.yfm[:, 512 * me:512 * (me + 1)]
                    if half == 0:
                        ctx.dve.inc(vector.tensor_scalar(
                            yslice, bufs.pd[me % 2][:, :], sd_sb[:, me:me + 1],
                            None, ALU.mult))
                    else:
                        ctx.dve.inc(vector.scalar_tensor_tensor(
                            yslice, bufs.pd[me % 2][:, :], sd_sb[:, me:me + 1],
                            yslice, ALU.mult, ALU.add))
            if half == 1:
                yfm_dve = ctx.dve.n

        if getattr(ctx, "f1dbg", None) is not None and tc == 0:
            @block.sync
            def _(sync):
                sync.wait_ge(ctx.dve.h, yfm_dve)
                ctx.st.inc(sync.dma_start(out=ctx.f1dbg, in_=bufs.yfm), 16)

        for st in range(4):
            t = 4 * tc + st
            mm2, ev2, dve2 = ctx.mm.n, ctx.ev.n, ctx.dve.n

            tpp = (bufs.tp32, bufs.tp32b)

            @block.tensor
            def _(tensor):
                tensor.wait_ge(ctx.dve.h, yfm_dve)
                tensor.wait_ge(ctx.ev.h, ev2)
                for me in range(ND):
                    if me >= 2:
                        tensor.wait_ge(ctx.ev.h, ev2 + me - 1)
                    ctx.mm.inc(tensor.transpose(
                        tpp[me % 2][:, :],
                        bufs.yfm[:, 512 * me + 128 * st:512 * me + 128 * (st + 1)],
                        bufs.ident32))

            @block.vector
            def _(vector):
                vector.wait_ge(ctx.act.h, ctx.act.n)
                for me in range(ND):
                    vector.wait_ge(ctx.mm.h, mm2 + me + 1)
                    ctx.ev.inc(vector.tensor_copy(
                        bufs.y[:, 128 * me:128 * (me + 1)], tpp[me % 2][:, :]))
            emit_ln_residual(ctx, bufs, x_sb, t, (ctx.ev, ctx.ev.n))
    return ctx.dve.n




def emit_ffn0(ctx, bufs, x_sb, w1_d, w2_d, b1_sb, b2_sb, wait=None, pcw=None):
    if not callable(w1_d):
        _w1 = w1_d
        w1_d = lambda mi: _w1[:, 128 * mi:128 * (mi + 1)]
    if not callable(w2_d):
        _w2 = w2_d
        w2_d = lambda half, me: _w2[32 * 128 * half:32 * 128 * (half + 1),
                                    128 * me:128 * (me + 1)]
    """L0 FFN (token-local): x += LN(gelu(LN(x) @ W1qT + b1) @ W2qT + b2).
    w1_d [D, 8192] bf16 DRAM (quantized, transposed), w2_d [8192, D] bf16.
    b1_sb [128, 64] f32, b2_sb [128, 16] f32 (host-transposed layouts).
    """
    block = ctx.block

    emit_ln_transpose(ctx, x_sb, bufs.xn, bufs.ident, (bufs.tpL, bufs.tpL2),
                      bufs.stat, bufs.xnT, xnT_dram=None, wait=wait)
    ln_ev = ctx.ev.n

    for tc in range(2):
        for half in range(2):
            mi0 = 32 * half
            wg_base = ctx.wgp[0].n
            mm0, act0, dve0 = ctx.mm.n, ctx.act.n, ctx.dve.n

            @block.gpsimd
            def _(gps):
                if tc == 0 and half == 0 and pcw is not None:
                    gps.wait_ge(pcw[0].h, pcw[1])
                for i in range(32):
                    mi = mi0 + i
                    sb0 = 2048 * (i % 2)
                    gps.wait_ge(ctx.mm.h, mm0 + 16 * (i - 1) if i >= 2 else mm0)
                    ctx.wgp[0].inc(gps.dma_start(
                        out=bufs.wsb[:, sb0:sb0 + 16 * 128].rearrange(
                            "p (kt c) -> p kt c", kt=16),
                        in_=w1_d(mi).rearrange(
                            "(kt p) c -> p kt c", p=128)), 16)

            @block.tensor
            def _(tensor):
                tensor.wait_ge(ctx.ev.h, ln_ev)
                tensor.wait_ge(ctx.act.h, act0)
                tensor.wait_ge(ctx.dve.h, dve0)
                for i in range(32):
                    tensor.wait_ge(ctx.wgp[0].h, wg_base + 16 * (i + 1))
                    if i >= 2:
                        tensor.wait_ge(ctx.act.h, act0 + (i - 1))
                    sb0 = 2048 * (i % 2)
                    for kt in range(ND):
                        ctx.mm.inc(tensor.matmul(
                            bufs.pg[i % 2][:, :],
                            bufs.wsb[:, sb0 + 128 * kt:sb0 + 128 * (kt + 1)],
                            bufs.xnT[:, TOK * kt + 512 * tc:TOK * kt + 512 * (tc + 1)],
                            start=(kt == 0), stop=(kt == ND - 1)))

            @block.scalar
            def _(scalar):
                for i in range(32):
                    mi = mi0 + i
                    scalar.wait_ge(ctx.mm.h, mm0 + 16 * (i + 1))
                    if i >= 2:
                        pass
                    ctx.act.inc(scalar.activation(
                        out=bufs.fuseT[:, 512 * i:512 * (i + 1)],
                        in_=bufs.pg[i % 2][:, :], func=AF.Gelu,
                        bias=b1_sb[:, mi:mi + 1], scale=1.0))
            h_act = ctx.act.n

            wd_base = ctx.wgp[1].n
            mm1, act1, dve1 = ctx.mm.n, ctx.act.n, ctx.dve.n

            @block.gpsimd
            def _(gps):
                for me in range(ND):
                    for h in range(2):
                        g = 2 * me + h
                        s = g % 3
                        if g >= 3:
                            pme, ph = divmod(g - 3, 2)
                            gps.wait_ge(ctx.mm.h,
                                        mm1 + 32 * pme + 16 * (ph + 1))
                        else:
                            gps.wait_ge(ctx.mm.h, mm0)
                        w2t = w2_d(half, me)
                        ctx.wgp[1].inc(gps.dma_start(
                            out=bufs.wdb[:, 2048 * s:2048 * (s + 1)].rearrange(
                                "p (kt c) -> p kt c", kt=16),
                            in_=w2t[16 * 128 * h:16 * 128 * (h + 1), :].rearrange(
                                "(kt p) c -> p kt c", p=128)), 16)

            @block.tensor
            def _(tensor):
                tensor.wait_ge(ctx.act.h, h_act)
                tensor.wait_ge(ctx.dve.h, dve1)
                for me in range(ND):
                    if me >= 2:
                        tensor.wait_ge(ctx.dve.h, dve1 + me - 1)
                    for kt in range(32):
                        if kt % 16 == 0:
                            g = 2 * me + kt // 16
                            tensor.wait_ge(ctx.wgp[1].h,
                                           wd_base + 16 * (g + 1))
                            db0 = 2048 * (g % 3)
                        ctx.mm.inc(tensor.matmul(
                            bufs.pd[me % 2][:, :],
                            bufs.wdb[:, db0 + 128 * (kt % 16):
                                     db0 + 128 * (kt % 16 + 1)],
                            bufs.fuseT[:, 512 * kt:512 * (kt + 1)],
                            start=(kt == 0), stop=(kt == 31)))

            @block.vector
            def _(vector):
                for me in range(ND):
                    vector.wait_ge(ctx.mm.h, mm1 + 32 * (me + 1))
                    yslice = bufs.yfm[:, 512 * me:512 * (me + 1)]
                    if half == 0:
                        ctx.dve.inc(vector.tensor_scalar(
                            yslice, bufs.pd[me % 2][:, :], b2_sb[:, me:me + 1],
                            None, ALU.add))
                    else:
                        ctx.dve.inc(vector.tensor_tensor(
                            yslice, bufs.pd[me % 2][:, :], yslice, ALU.add))
            if half == 1:
                yfm_dve = ctx.dve.n

        for st in range(4):
            t = 4 * tc + st
            mm2, ev2, dve2 = ctx.mm.n, ctx.ev.n, ctx.dve.n

            tpp = (bufs.tp32, bufs.tp32b)

            @block.tensor
            def _(tensor):
                tensor.wait_ge(ctx.dve.h, yfm_dve)
                tensor.wait_ge(ctx.ev.h, ev2)
                for me in range(ND):
                    if me >= 2:
                        tensor.wait_ge(ctx.ev.h, ev2 + me - 1)
                    ctx.mm.inc(tensor.transpose(
                        tpp[me % 2][:, :],
                        bufs.yfm[:, 512 * me + 128 * st:512 * me + 128 * (st + 1)],
                        bufs.ident32))

            @block.vector
            def _(vector):
                vector.wait_ge(ctx.act.h, ctx.act.n)
                for me in range(ND):
                    vector.wait_ge(ctx.mm.h, mm2 + me + 1)
                    ctx.ev.inc(vector.tensor_copy(
                        bufs.y[:, 128 * me:128 * (me + 1)], tpp[me % 2][:, :]))
            emit_ln_residual(ctx, bufs, x_sb, t, (ctx.ev, ctx.ev.n))
    return ctx.dve.n


def phase_quant(nc, ctx, es, io):
    block = ctx.block
    ar_in = nc.dram_tensor("ar_in_q", [1, 8], F32, kind="Internal")
    ar_out = nc.dram_tensor("ar_out_q", [1, 8], F32, kind="Internal")
    sb = lambda n, shp, dt: es.enter_context(nc.sbuf_tensor(n + "_q", shp, dt))[:, :]
    wt = [sb("wt0", [128, 8192], F32), sb("wt1", [128, 8192], F32)]
    qt = sb("qt", [128, 8192], F32)
    qb = sb("qb", [128, 8192], BF16)
    pc = [sb("pc0", [128, 512], I8), sb("pc1", [128, 512], I8)]
    acc = sb("acc", [128, 32], F32)
    row = sb("row", [1, 16], F32)
    rsb = sb("rsb", [128, 2], F32)
    onescol32 = sb("onescol32", [128, 1], F32)
    onesrow32 = sb("onesrow32", [1, 128], F32)
    ident = sb("ident_sb", [128, 128], BF16)
    ps = es.enter_context(nc.psum_tensor("ps_q", [128, 512], F32))[:, :]
    tpg = [es.enter_context(nc.psum_tensor(f"tpg{i}_q", [128, 512], BF16))[:, :]
           for i in range(2)]



    @block.sync
    def _(sync):
        ctx.ld.inc(sync.dma_start(out=ident, in_=io.ident), 16)
    ld_pre = ctx.ld.n

    @block.vector
    def _(vector):
        vector.wait_ge(ctx.ld.h, ld_pre)
        vector.memset(onescol32, 1.0)
        ctx.dve.inc(vector.memset(onesrow32, 1.0))

    tiles = ([("w1", i, io.w1s[128 * i:128 * (i + 1), :], 2048) for i in range(8)]
             + [("w2", i, io.w2s[128 * i:128 * (i + 1), :], 8192) for i in range(2)])
    ntr_before = [0]
    for _, _, _, wdt in tiles:
        ntr_before.append(ntr_before[-1] + wdt // 128)

    rs_base = (ctx.rs[0].n, ctx.rs[1].n)
    p1_dve0 = ctx.dve.n

    @block.sync
    def _(sync):
        for ti, (_, _, ap, wdt) in enumerate(tiles):
            if ti >= 2:
                sync.wait_ge(ctx.dve.h, p1_dve0 + ti - 1)
            ctx.rs[ti % 2].inc(sync.dma_start(out=wt[ti % 2][:, 0:wdt], in_=ap), 16)

    @block.vector
    def _(vector):
        for ti, (wn, i, _, wdt) in enumerate(tiles):
            vector.wait_ge(ctx.rs[ti % 2].h, rs_base[ti % 2] + 16 * (ti // 2 + 1))
            ctx.dve.inc(vector.tensor_reduce(
                acc[:, ti:ti + 1], wt[ti % 2][:, 0:wdt], AX.X, ALU.add,
                apply_absolute_value=True))
        vector.wait_ge(ctx.dve.h, ctx.dve.n)
        ctx.dve.inc(vector.tensor_reduce(acc[:, 16:17], acc[:, 0:8], AX.X, ALU.add))
        ctx.dve.inc(vector.tensor_reduce(acc[:, 17:18], acc[:, 8:10], AX.X, ALU.add))
        vector.wait_ge(ctx.dve.h, ctx.dve.n)
    sums_dve = ctx.dve.n

    @block.tensor
    def _(tensor):
        tensor.wait_ge(ctx.dve.h, sums_dve)
        ctx.mm.inc(tensor.matmul(ps[0:1, 0:2], onescol32, acc[:, 16:18],
                                 start=True, stop=True))
    mm_sum = ctx.mm.n

    @block.vector
    def _(vector):
        vector.wait_ge(ctx.mm.h, mm_sum)
        vector.memset(row[0:1, 2:8], 0.0)
        ctx.dve.inc(vector.tensor_copy(row[0:1, 0:2], ps[0:1, 0:2]))
        vector.wait_ge(ctx.dve.h, ctx.dve.n)
    row_dve = ctx.dve.n

    @block.sync
    def _(sync):
        sync.wait_ge(ctx.dve.h, row_dve)
        ctx.st.inc(sync.dma_start(out=ar_in.ap(), in_=row[0:1, 0:8]), 16)
    st_ar = ctx.st.n

    @block.gpsimd
    def _(gpsimd):
        gpsimd.wait_ge(ctx.st.h, st_ar)
        gpsimd.collective_compute(
            "AllReduce", ALU.add, replica_groups=[list(range(8))],
            ins=[ar_in.ap()], outs=[ar_out.ap()]).then_inc(ctx.cc.h, 1)
        ctx.cc.n += 1
        gpsimd.wait_ge(ctx.cc.h, ctx.cc.n)
        ctx.ldg.inc(gpsimd.dma_start(out=row[0:1, 8:16], in_=ar_out.ap()), 16)
    ar_ld = ctx.ldg.n

    @block.vector
    def _(vector):
        vector.wait_ge(ctx.ldg.h, ar_ld)
        ctx.dve.inc(vector.tensor_scalar(row[0:1, 2:3], row[0:1, 8:9],
                                         1.0 / (8192.0 * D), 1e-8,
                                         ALU.mult, ALU.max))
        ctx.dve.inc(vector.tensor_scalar(row[0:1, 3:4], row[0:1, 9:10],
                                         1.0 / (8192.0 * D), 1e-8,
                                         ALU.mult, ALU.max))
        vector.wait_ge(ctx.dve.h, ctx.dve.n)
        ctx.dve.inc(vector.reciprocal(row[0:1, 4:6], row[0:1, 2:4]))
        vector.wait_ge(ctx.dve.h, ctx.dve.n)
    rs_dve = ctx.dve.n

    @block.tensor
    def _(tensor):
        tensor.wait_ge(ctx.dve.h, rs_dve)
        ctx.mm.inc(tensor.matmul(ps[:, 0:2], onesrow32, row[0:1, 4:6],
                                 start=True, stop=True))
    mm_bc = ctx.mm.n

    @block.vector
    def _(vector):
        vector.wait_ge(ctx.mm.h, mm_bc)
        ctx.dve.inc(vector.tensor_copy(rsb[:, 0:2], ps[:, 0:2]))
        vector.wait_ge(ctx.dve.h, ctx.dve.n)
    rsb_dve = ctx.dve.n

    rs_base2 = (ctx.rs[0].n, ctx.rs[1].n)
    act0 = ctx.act.n
    dve0 = ctx.dve.n
    mm0 = ctx.mm.n
    ev0 = ctx.ev.n
    st0s = (ctx.stp[0].n, ctx.stp[1].n)

    @block.sync
    def _(sync):
        for ti, (_, _, ap, wdt) in enumerate(tiles):
            if ti < 2:
                sync.wait_ge(ctx.dve.h, sums_dve)
            else:
                sync.wait_ge(ctx.act.h, act0 + ti - 1)
            ctx.rs[ti % 2].inc(sync.dma_start(out=wt[ti % 2][:, 0:wdt], in_=ap), 16)

    @block.scalar
    def _(scalar):
        for ti, (wn, i, _, wdt) in enumerate(tiles):
            scalar.wait_ge(ctx.rs[ti % 2].h, rs_base2[ti % 2] + 16 * (ti // 2 + 1))
            scalar.wait_ge(ctx.dve.h, rsb_dve)
            if ti >= 1:
                scalar.wait_ge(ctx.dve.h, dve0 + 2 * ti)
            ridx = 0 if wn == "w1" else 1
            ctx.act.inc(scalar.activation(
                out=qt[:, 0:wdt], in_=wt[ti % 2][:, 0:wdt], func=AF.Copy,
                bias=MAGIC, scale=rsb[:, ridx:ridx + 1]))

    ngrp_before = [0]
    for _, _, _, wdt in tiles:
        ngrp_before.append(ngrp_before[-1] + wdt // 512)

    @block.tensor
    def _(tensor):
        g = 0
        for ti, (wn, i, _, wdt) in enumerate(tiles):
            tensor.wait_ge(ctx.dve.h, dve0 + 2 * (ti + 1))
            for gl in range(wdt // 512):
                if g >= 2:
                    tensor.wait_ge(ctx.ev.h, ev0 + g - 1)
                for a in range(4):
                    sc = 4 * gl + a
                    ctx.mm.inc(tensor.transpose(
                        tpg[g % 2][:, 128 * a:128 * (a + 1)],
                        qb[:, 128 * sc:128 * (sc + 1)], ident))
                g += 1

    @block.vector
    def _(vector):
        g = 0
        for ti, (wn, i, _, wdt) in enumerate(tiles):
            vector.wait_ge(ctx.act.h, act0 + ti + 1)
            ctx.dve.inc(vector.tensor_scalar(
                qt[:, 0:wdt], qt[:, 0:wdt], MAGIC, -1.0,
                ALU.subtract, ALU.max))
            vector.wait_ge(ctx.dve.h, ctx.dve.n)
            if ti >= 1:
                vector.wait_ge(ctx.mm.h, mm0 + 4 * ngrp_before[ti])
            ctx.dve.inc(vector.tensor_scalar(
                qb[:, 0:wdt], qt[:, 0:wdt], 1.0, None, ALU.min))
            for gl in range(wdt // 512):
                vector.wait_ge(ctx.mm.h, mm0 + 4 * (g + 1))
                if g >= 2:
                    vector.wait_ge(ctx.stp[g % 2].h,
                                   st0s[g % 2] + 16 * (g // 2))
                ctx.ev.inc(vector.tensor_copy(pc[g % 2], tpg[g % 2]))
                g += 1

    @block.gpsimd
    def _(gpsimd):
        g = 0
        for ti, (wn, i, _, wdt) in enumerate(tiles):
            for gl in range(wdt // 512):
                gpsimd.wait_ge(ctx.ev.h, ev0 + g + 1)
                if wn == "w1":
                    dst = io.w1qT[512 * gl:512 * (gl + 1),
                                  128 * i:128 * (i + 1)]
                else:
                    dst = io.w2qT[512 * gl:512 * (gl + 1),
                                  128 * i:128 * (i + 1)]
                ctx.stp[g % 2].inc(gpsimd.dma_start(
                    out=dst.rearrange("(a p) c -> p a c", p=128),
                    in_=pc[g % 2].rearrange("p (a c) -> p a c", a=4)), 16)
                g += 1
        gpsimd.wait_ge(ctx.stp[0].h, ctx.stp[0].n)
        gpsimd.wait_ge(ctx.stp[1].h, ctx.stp[1].n)


def build_quant_launch():
    nc = _mk_bass()
    io = NS()
    io.w1s = nc.dram_tensor("w1s", [1024, D], F32, kind="ExternalInput").ap()
    io.w2s = nc.dram_tensor("w2s", [256, 8192], F32, kind="ExternalInput").ap()
    io.ident = nc.dram_tensor("ident", [128, 128], BF16, kind="ExternalInput").ap()
    consts_in = nc.dram_tensor("consts", [128, len(CONSTS)], F32, kind="ExternalInput").ap()
    io.w1qT = nc.dram_tensor("w1qT", [D, 1024], BF16, kind="ExternalOutput").ap()
    io.w2qT = nc.dram_tensor("w2qT", [8192, 256], BF16, kind="ExternalOutput").ap()
    with contextlib.ExitStack() as es:
        csb = es.enter_context(nc.sbuf_tensor("csb", [128, len(CONSTS)], F32))
        register_consts(nc, csb)
        block = es.enter_context(nc.Block())
        ctx = Ctx(nc, block, es)

        @block.sync
        def _(sync):
            ctx.ld.inc(sync.dma_start(out=csb[:, :], in_=consts_in), 16)

        phase_quant(nc, ctx, es, io)

        @block.sync
        def _(sync):
            sync.wait_ge(ctx.stp[0].h, ctx.stp[0].n)
            sync.wait_ge(ctx.stp[1].h, ctx.stp[1].n)
    return nc


def phase_ffn0(nc, ctx, es, io, x_src, x_dst):
    block = ctx.block
    sb = lambda n, shp, dt: es.enter_context(nc.sbuf_tensor(n + "_f0", shp, dt))[:, :]
    bufs = NS()
    x_sb = es.enter_context(nc.sbuf_tensor("x_sb_f0", [128, NT * D], F32))
    bufs.xnT = sb("xnT", [128, ND * TOK], BF16)
    bufs.fuseT = sb("fuseT", [128, 32 * 512], BF16)
    bufs.xn = bufs.fuseT
    bufs.wsb = sb("wsb", [128, 2 * 16 * 128], BF16)
    bufs.wdb = sb("wdb", [128, 3 * 16 * 128], BF16)
    bufs.yfm = sb("yfm", [128, 16 * 512], F32)
    bufs.y = sb("y_sb", [128, D], F32)
    bufs.tmp = sb("tmp_sb", [128, D], F32)
    bufs.stat = sb("stat", [128, 64], F32)
    bufs.ident = sb("ident_sb", [128, 128], BF16)
    bufs.ident32 = sb("ident32_sb", [128, 128], F32)
    b1s = sb("b1s", [128, 64], F32)
    b2s = sb("b2s", [128, 16], F32)
    bufs.pg = [es.enter_context(nc.psum_tensor(f"pg{i}_f0", [128, 512], F32))[:, :] for i in range(2)]
    bufs.pd = [es.enter_context(nc.psum_tensor(f"pd{i}_f0", [128, 512], F32))[:, :] for i in range(2)]
    bufs.tp32 = es.enter_context(nc.psum_tensor("tp32_f0", [128, 128], F32))[:, :]
    bufs.tpL = es.enter_context(nc.psum_tensor("tpL_f0", [128, 128], BF16))
    bufs.tp32b = es.enter_context(nc.psum_tensor("tp32b_f0", [128, 128], F32))[:, :]
    bufs.tpL2 = es.enter_context(nc.psum_tensor("tpL2_f0", [128, 128], BF16))

    @block.sync
    def _(sync):
        ctx.ld.inc(sync.dma_start(
            out=x_sb[:, :].rearrange("p (t d) -> p t d", t=NT),
            in_=x_src.rearrange("(t p) d -> p t d", p=128)), 16)
        for dst, srcap in [(bufs.ident, io.ident), (bufs.ident32, io.ident32),
                           (b1s, io.b1t), (b2s, io.b2t)]:
            ctx.ld.inc(sync.dma_start(out=dst, in_=srcap), 16)
    ld_pre = ctx.ld.n

    ctx.sq_scratch = bufs.tmp
    emit_ffn0(ctx, bufs, x_sb[:, :], io.w1qT, io.w2qT, b1s, b2s,
              wait=(ctx.ld, ld_pre), pcw=getattr(io, "pcw", None))

    @block.sync
    def _(sync):
        sync.wait_ge(ctx.dve.h, ctx.dve.n)
        ctx.st.inc(sync.dma_start(
            out=x_dst.rearrange("(t p) d -> p t d", p=128),
            in_=x_sb[:, :].rearrange("p (t d) -> p t d", t=NT)), 16)


def build_ffn0_launch():
    nc = _mk_bass()
    io = NS()
    x_in = nc.dram_tensor("x", [TOK, D], F32, kind="ExternalInput").ap()
    io.w1qT = nc.dram_tensor("w1qT", [D, 8192], BF16, kind="ExternalInput").ap()
    io.w2qT = nc.dram_tensor("w2qT", [8192, D], BF16, kind="ExternalInput").ap()
    io.b1t = nc.dram_tensor("b1t", [128, 64], F32, kind="ExternalInput").ap()
    io.b2t = nc.dram_tensor("b2t", [128, 16], F32, kind="ExternalInput").ap()
    io.ident = nc.dram_tensor("ident", [128, 128], BF16, kind="ExternalInput").ap()
    io.ident32 = nc.dram_tensor("ident32", [128, 128], F32, kind="ExternalInput").ap()
    consts_in = nc.dram_tensor("consts", [128, len(CONSTS)], F32, kind="ExternalInput").ap()
    xo = nc.dram_tensor("xo", [TOK, D], F32, kind="ExternalOutput").ap()
    with contextlib.ExitStack() as es:
        csb = es.enter_context(nc.sbuf_tensor("csb", [128, len(CONSTS)], F32))
        register_consts(nc, csb)
        block = es.enter_context(nc.Block())
        ctx = Ctx(nc, block, es)

        @block.sync
        def _(sync):
            ctx.ld.inc(sync.dma_start(out=csb[:, :], in_=consts_in), 16)

        phase_ffn0(nc, ctx, es, io, x_in, xo)

        @block.sync
        def _(sync):
            sync.wait_ge(ctx.st.h, ctx.st.n)
    return nc


def _np_ternary(w):
    s = max(np.abs(w).mean(dtype=np.float64), 1e-8)
    return np.clip(np.round(w / np.float32(s)), -1, 1).astype(np.float32)


def _test_q0():
    rng = np.random.default_rng(17)
    w1 = (rng.standard_normal((8192, D), dtype=np.float32) * 0.02)
    w2 = (rng.standard_normal((D, 8192), dtype=np.float32) * 0.02)
    nc = build_quant_launch()
    in_maps = []
    for c in range(8):
        in_maps.append({
            "w1s": np.ascontiguousarray(w1[1024 * c:1024 * (c + 1)]),
            "w2s": np.ascontiguousarray(w2[256 * c:256 * (c + 1)]),
            "ident": np.eye(128, dtype=ml_dtypes.bfloat16),
            "consts": consts_host(),
        })
    res = run_bass_kernel_spmd(nc, in_maps, core_ids=list(range(8)))
    w1q = _np_ternary(w1)
    w2q = _np_ternary(w2)
    for c in [0, 6]:
        g1 = res.results[c]["w1qT"].astype(np.float32)
        e1 = w1q[1024 * c:1024 * (c + 1)].T
        g2 = res.results[c]["w2qT"].astype(np.float32)
        e2 = w2q[256 * c:256 * (c + 1)].T
        print(f"core{c} q0: w1 mismatches={int((g1 != e1).sum())} "
              f"w2 mismatches={int((g2 != e2).sum())}")


def _test_f0():
    rng = np.random.default_rng(19)
    x = rng.standard_normal((TOK, D), dtype=np.float32)
    w1 = (rng.standard_normal((8192, D), dtype=np.float32) * 0.02)
    w2 = (rng.standard_normal((D, 8192), dtype=np.float32) * 0.02)
    b1 = rng.standard_normal(8192).astype(np.float32) * 0.02
    b2 = rng.standard_normal(D).astype(np.float32) * 0.02
    w1q = _np_ternary(w1)
    w2q = _np_ternary(w2)
    nc = build_ffn0_launch()
    in_maps = [{
        "x": x,
        "w1qT": np.ascontiguousarray(w1q.T).astype(ml_dtypes.bfloat16),
        "w2qT": np.ascontiguousarray(w2q.T).astype(ml_dtypes.bfloat16),
        "b1t": np.ascontiguousarray(b1.reshape(64, 128).T),
        "b2t": np.ascontiguousarray(b2.reshape(16, 128).T),
        "ident": np.eye(128, dtype=ml_dtypes.bfloat16),
        "ident32": np.eye(128, dtype=np.float32),
        "consts": consts_host(),
    } for _ in range(8)]
    res = run_bass_kernel_spmd(nc, in_maps, core_ids=list(range(8)))
    import scipy.special as sps
    m = x.mean(-1, keepdims=True); v = x.var(-1, keepdims=True)
    xn = (x - m) / np.sqrt(v + 1e-5)
    h = xn @ w1q.T + b1
    h = h * 0.5 * (1.0 + sps.erf(h / np.sqrt(2.0)))
    y = h @ w2q.T + b2
    m2 = y.mean(-1, keepdims=True); v2 = y.var(-1, keepdims=True)
    ref = x + (y - m2) / np.sqrt(v2 + 1e-5)
    for c in [0, 4]:
        got = res.results[c]["xo"]
        rel = np.linalg.norm(got - ref) / np.linalg.norm(ref)
        print(f"core{c} F0 ffn0: rel={rel:.4e} maxabs={np.abs(got - ref).max():.4f}")



_NC_CACHE = {}


def _get_nc(name, builder):
    if name not in _NC_CACHE:
        _NC_CACHE[name] = builder()
    return _NC_CACHE[name]


def kernel_unfused(**inputs):
    f32 = np.float32
    x = np.ascontiguousarray(np.asarray(inputs["L0_attn_x"], dtype=f32).reshape(B * S, D))
    qkv_w = np.asarray(inputs["L0_attn_qkv_w"], dtype=f32)
    dense_w = np.ascontiguousarray(np.asarray(inputs["L0_attn_dense_w"], dtype=f32))
    table = np.asarray(inputs["L0_attn_rel_attn_table"], dtype=f32)
    w1 = np.asarray(inputs["L0_ffn_w1"], dtype=f32)
    b1 = np.asarray(inputs["L0_ffn_b1"], dtype=f32)
    w2 = np.asarray(inputs["L0_ffn_w2"], dtype=f32)
    b2 = np.asarray(inputs["L0_ffn_b2"], dtype=f32)
    Wq = np.asarray(inputs["L1_attn_Wq"], dtype=f32)
    Wk = np.asarray(inputs["L1_attn_Wk"], dtype=f32)
    Wv = np.asarray(inputs["L1_attn_Wv"], dtype=f32)
    Wo = np.ascontiguousarray(np.asarray(inputs["L1_attn_Wo"], dtype=f32))
    wgI = np.ascontiguousarray(np.asarray(inputs["L1_ffn_w_gate"], dtype=np.int8))
    wuI = np.ascontiguousarray(np.asarray(inputs["L1_ffn_w_up"], dtype=np.int8))
    wdI = np.ascontiguousarray(np.asarray(inputs["L1_ffn_w_down"], dtype=np.int8))
    sg = np.asarray(inputs["L1_ffn_s_gate"], dtype=f32)
    su = np.asarray(inputs["L1_ffn_s_up"], dtype=f32)
    sd = np.asarray(inputs["L1_ffn_s_down"], dtype=f32)

    CH = consts_host()
    EYE = np.eye(128, dtype=ml_dtypes.bfloat16)
    EYE32 = np.eye(128, dtype=np.float32)
    MB = _mbig()
    cos0, sin0 = _l0_cos_sin_rows()
    cos1, sin1 = _l1_cos_sin_interleaved()
    cores = list(range(NCORES))

    def xsl(c):
        return np.ascontiguousarray(x[TOK * c:TOK * (c + 1)])

    nc = _get_nc("quant", build_quant_launch)
    res = run_bass_kernel_spmd(nc, [{
        "w1s": np.ascontiguousarray(w1[1024 * c:1024 * (c + 1)]),
        "w2s": np.ascontiguousarray(w2[256 * c:256 * (c + 1)]),
        "ident": EYE, "consts": CH} for c in cores], core_ids=cores)
    w1qT = np.ascontiguousarray(np.concatenate(
        [res.results[c]["w1qT"] for c in cores], axis=1))
    w2qT = np.ascontiguousarray(np.concatenate(
        [res.results[c]["w2qT"] for c in cores], axis=1))

    def run_ln():
        nc = _get_nc("ln", build_ln_launch)
        res = run_bass_kernel_spmd(nc, [{
            "x": xsl(c), "ident": EYE, "consts": CH} for c in cores],
            core_ids=cores)
        xg = [np.ascontiguousarray(np.concatenate(
            [res.results[4 * b + k]["xnT"] for k in range(4)], axis=1))
            for b in range(2)]
        return xg

    def reshard_ctx(res):
        ctxf = [np.concatenate([res.results[4 * b + k]["ctxT"] for k in range(4)],
                               axis=0) for b in range(2)]
        return [np.ascontiguousarray(
            ctxf[c // 4][:, TOK * (c % 4):TOK * (c % 4 + 1)]) for c in cores]

    def run_outproj(ctx_my, w):
        nc = _get_nc("outproj", build_outproj_launch)
        res = run_bass_kernel_spmd(nc, [{
            "x": xsl(c), "ctxT_my": ctx_my[c], "w": w, "consts": CH}
            for c in cores], core_ids=cores)
        return np.concatenate([res.results[c]["xo"] for c in cores], axis=0)

    xg = run_ln()
    nc = _get_nc("attn0", build_attn0_launch)
    res = run_bass_kernel_spmd(nc, [{
        "xg": xg[c // 4],
        "wq": np.ascontiguousarray(qkv_w[:, 512 * (c % 4):512 * (c % 4 + 1)]),
        "wk": np.ascontiguousarray(qkv_w[:, D + 512 * (c % 4):D + 512 * (c % 4 + 1)]),
        "wv": np.ascontiguousarray(qkv_w[:, 2 * D + 512 * (c % 4):2 * D + 512 * (c % 4 + 1)]),
        "tbl": np.ascontiguousarray(table[:, 4 * (c % 4):4 * (c % 4 + 1)]),
        "oh": _l0_onehot(), "mbig": MB, "cosr": cos0, "sinr": sin0,
        "ident": EYE, "id16": np.eye(16, dtype=np.float32),
        "rot": _rot_host(0), "consts": CH, "jrev": EYE32[::-1].copy()}
        for c in cores], core_ids=cores)
    x = run_outproj(reshard_ctx(res), dense_w)

    nc = _get_nc("ffn0", build_ffn0_launch)
    res = run_bass_kernel_spmd(nc, [{
        "x": xsl(c), "w1qT": w1qT, "w2qT": w2qT,
        "b1t": np.ascontiguousarray(b1.reshape(64, 128).T),
        "b2t": np.ascontiguousarray(b2.reshape(16, 128).T),
        "ident": EYE, "ident32": EYE32, "consts": CH} for c in cores],
        core_ids=cores)
    x = np.concatenate([res.results[c]["xo"] for c in cores], axis=0)

    xg = run_ln()
    nc = _get_nc("attn1", build_attn1_launch)
    res = run_bass_kernel_spmd(nc, [{
        "xg": xg[c // 4],
        "wq": np.ascontiguousarray(Wq[:, 512 * (c % 4):512 * (c % 4 + 1)]),
        "wk": np.ascontiguousarray(Wk[:, 128 * (c % 4):128 * (c % 4 + 1)]),
        "wv": np.ascontiguousarray(Wv[:, 128 * (c % 4):128 * (c % 4 + 1)]),
        "mbig": MB, "cosr": cos1, "sinr": sin1,
        "ident": EYE, "rot": _rot_l1_host(), "consts": CH}
        for c in cores], core_ids=cores)
    x = run_outproj(reshard_ctx(res), Wo)

    nc = _get_nc("ffn1", build_ffn1_launch)
    res = run_bass_kernel_spmd(nc, [{
        "x": xsl(c), "wg": wgI, "wu": wuI, "wd": wdI,
        "sg": np.ascontiguousarray(sg.reshape(64, 128).T),
        "su": np.ascontiguousarray(su.reshape(64, 128).T),
        "sd": np.ascontiguousarray(sd.reshape(16, 128).T),
        "ident": EYE, "ident32": EYE32, "consts": CH} for c in cores],
        core_ids=cores)
    x = np.concatenate([res.results[c]["xo"] for c in cores], axis=0)

    return x.reshape(B, S, D)



def emit_fence(ctx):
    block = ctx.block
    sems = [ctx.ld, ctx.ldg, ctx.st, ctx.mm, ctx.act, ctx.dve, ctx.pool,
            ctx.cc, ctx.ev, ctx.rs[0], ctx.rs[1], ctx.stp[0], ctx.stp[1],
            ctx.wgp[0], ctx.wgp[1]]
    if getattr(ctx, "cc2", None) is not None:
        sems = sems + [ctx.cc2]
    vals = [(s.h, s.n) for s in sems]

    for mk in (block.sync, block.tensor, block.vector, block.scalar,
               block.gpsimd):
        @mk
        def _(eng):
            for h, v in vals:
                if v:
                    eng.wait_ge(h, v)


def emit_ag(ctx, in_ap, out_ap, groups, sem=None):
    block = ctx.block
    s = sem if sem is not None else ctx.cc

    @block.gpsimd
    def _(gpsimd):
        gpsimd.collective_compute(
            "AllGather", ALU.bypass, replica_groups=groups,
            ins=[in_ap], outs=[out_ap]).then_inc(s.h, 1)
        s.n += 1


def build_fused():
    nc = _mk_bass()
    io = NS()
    ein = lambda n, shp, dt: nc.dram_tensor(n, shp, dt, kind="ExternalInput").ap()
    x0 = ein("x0", [TOK, D], F32)
    a0 = NS()
    a0.wq = ein("a0wq", [D, 512], F32)
    a0.wk = ein("a0wk", [D, 512], F32)
    a0.wv = ein("a0wv", [D, 512], F32)
    a0.tbl = ein("a0tbl", [32, 4], F32)
    a0.oh = ein("a0oh", [32, 1536], F32)
    a0.mbig = ein("mbig", [128, CBW], F32)
    a0.cosr = ein("a0cosr", [128, S], BF16)
    a0.sinr = ein("a0sinr", [128, S], BF16)
    a0.ident = ein("ident", [128, 128], BF16)
    a0.id16 = ein("id16", [16, 16], F32)
    a0.rot = ein("a0rot", [128, 128], BF16)
    a0.jrev = ein("jrev", [128, 128], F32)
    dense_w = ein("dense_w", [D, D], F32)
    q = NS()
    q.w1s = ein("w1s", [1024, D], F32)
    q.w2s = ein("w2s", [256, 8192], F32)
    q.ident = a0.ident
    f0 = NS()
    f0.b1t = ein("b1t", [128, 64], F32)
    f0.b2t = ein("b2t", [128, 16], F32)
    f0.ident = a0.ident
    f0.ident32 = ein("ident32", [128, 128], F32)
    a1 = NS()
    a1.wq = ein("a1wq", [D, 512], F32)
    a1.wk = ein("a1wk", [D, 128], F32)
    a1.wv = ein("a1wv", [D, 128], F32)
    a1.mbig = a0.mbig
    a1.cosr = ein("a1cosr", [64, S], BF16)
    a1.sinr = ein("a1sinr", [64, S], BF16)
    a1.ident = a0.ident
    a1.rot = ein("a1rot", [128, 128], BF16)
    wo_w = ein("wo_w", [D, D], F32)
    f1 = NS()
    wg_i8 = ein("wg", [D, 8192], I8)
    wu_i8 = ein("wu", [D, 8192], I8)
    wd_i8 = ein("wd", [8192, D], I8)
    wg_b = nc.dram_tensor("wg_b", [D, 8192], BF16, kind="Internal").ap()
    wu_b = nc.dram_tensor("wu_b", [D, 8192], BF16, kind="Internal").ap()
    wd_b = nc.dram_tensor("wd_b", [8192, D], BF16, kind="Internal").ap()
    a0wq_b = nc.dram_tensor("a0wq_b", [D, 512], BF16, kind="Internal").ap()
    a0wk_b = nc.dram_tensor("a0wk_b", [D, 512], BF16, kind="Internal").ap()
    a0wv_b = nc.dram_tensor("a0wv_b", [D, 512], BF16, kind="Internal").ap()
    a1wq_b = nc.dram_tensor("a1wq_b", [D, 512], BF16, kind="Internal").ap()
    a1wk_b = nc.dram_tensor("a1wk_b", [D, 128], BF16, kind="Internal").ap()
    a1wv_b = nc.dram_tensor("a1wv_b", [D, 128], BF16, kind="Internal").ap()
    dense_b = nc.dram_tensor("dense_b", [D, D], BF16, kind="Internal").ap()
    wo_b = nc.dram_tensor("wo_b", [D, D], BF16, kind="Internal").ap()
    f1.wg = wg_b
    f1.wu = wu_b
    f1.wd = wd_b
    f1.sg = ein("sg", [128, 64], F32)
    f1.su = ein("su", [128, 64], F32)
    f1.sd = ein("sd", [128, 16], F32)
    f1.ident = a0.ident
    f1.ident32 = f0.ident32
    consts_in = ein("consts", [128, len(CONSTS)], F32)
    xo = nc.dram_tensor("xo", [TOK, D], F32, kind="ExternalOutput").ap()

    x_d = nc.dram_tensor("x_d", [TOK, D], F32, kind="Internal").ap()
    x_d2 = nc.dram_tensor("x_d2", [TOK, D], F32, kind="Internal").ap()
    xnT_in = nc.dram_tensor("xnT_in", [D, TOK], BF16, kind="Internal")
    xg_d = nc.dram_tensor("xg_d", [4 * D, TOK], BF16, kind="Internal")
    ctx_in = nc.dram_tensor("ctx_in", [512, S], BF16, kind="Internal")
    ctx_g = nc.dram_tensor("ctx_g", [D, S], BF16, kind="Internal")
    q.w1qT = nc.dram_tensor("w1qT_in", [D, 1024], I8, kind="Internal").ap()
    q.w2qT = nc.dram_tensor("w2qT_in", [8192, 256], I8, kind="Internal").ap()
    w1g = nc.dram_tensor("w1qT_g", [8 * D, 1024], I8, kind="Internal",
                         addr_space="Shared")
    w2g = nc.dram_tensor("w2qT_g", [8 * 8192, 256], I8, kind="Internal",
                         addr_space="Shared")

    def xg_fn(kt, w):
        return xg_d.ap()[D * (w // 2) + 128 * kt:D * (w // 2) + 128 * (kt + 1),
                         512 * (w % 2):512 * (w % 2 + 1)]

    def w1_fn(mi):
        return w1g.ap()[D * (mi // 8):D * (mi // 8 + 1),
                        128 * (mi % 8):128 * (mi % 8 + 1)]

    def w2_fn(half, me):
        base = 8192 * (me // 2) + 4096 * half
        return w2g.ap()[base:base + 4096, 128 * (me % 2):128 * (me % 2 + 1)]

    a0.xg = xg_fn
    a0.ctxT = ctx_in.ap()
    a0.sfx = "_a0"
    a0wq_f32, a0wk_f32, a0wv_f32 = a0.wq, a0.wk, a0.wv
    a0.wq, a0.wk, a0.wv = a0wq_b, a0wk_b, a0wv_b
    a1.xg = xg_fn
    a1.ctxT = ctx_in.ap()
    a1wq_f32, a1wk_f32, a1wv_f32 = a1.wq, a1.wk, a1.wv
    a1.wq, a1.wk, a1.wv = a1wq_b, a1wk_b, a1wv_b
    f0.w1qT = w1_fn
    f0.w2qT = w2_fn
    G4 = [[0, 1, 2, 3], [4, 5, 6, 7]]
    G8 = [list(range(8))]

    with contextlib.ExitStack() as es:
        csb = es.enter_context(nc.sbuf_tensor("csb", [128, len(CONSTS)], F32))
        register_consts(nc, csb)
        block = es.enter_context(nc.Block())
        ctx = Ctx(nc, block, es)
        ctx.op_switch = True

        @block.sync
        def _(sync):
            ctx.ld.inc(sync.dma_start(out=csb[:, :], in_=consts_in), 16)

        def fence():
            emit_fence(ctx)

        def precast(pairs):
            @block.gpsimd
            def _(gpsimd):
                for src_ap, dst_ap, rstep, nr in pairs:
                    for r0 in range(0, nr, rstep):
                        ctx.pc.inc(gpsimd.dma_start(
                            out=dst_ap[r0:r0 + rstep, :],
                            in_=src_ap[r0:r0 + rstep, :]), 16)

        precast([(a0wq_f32, a0wq_b, D, D), (a0wk_f32, a0wk_b, D, D),
                 (a0wv_f32, a0wv_b, D, D)])
        pc_a0 = ctx.pc.n

        with nc.named_scope("p1_quant"):
            with contextlib.ExitStack() as pes:
                phase_quant(nc, ctx, pes, q)
                fence()
        precast([(dense_w, dense_b, 512, D),
                 (a1wq_f32, a1wq_b, D, D), (a1wk_f32, a1wk_b, D, D),
                 (a1wv_f32, a1wv_b, D, D),
                 (wo_w, wo_b, 512, D),
                 (wg_i8, wg_b, 512, D), (wu_i8, wu_b, 512, D),
                 (wd_i8, wd_b, 2048, 8192)])
        pc_dense = pc_a0 + 4 * 16
        pc_a1 = pc_dense + 3 * 16
        pc_wo = pc_a1 + 4 * 16
        pc_f1 = ctx.pc.n
        with nc.named_scope("p2_ln"):
            with contextlib.ExitStack() as pes:
                phase_ln(nc, ctx, pes, x0, a0.ident, xnT_in.ap())
                fence()
        with nc.named_scope("p1_agw"):
            emit_ag(ctx, q.w1qT, w1g.ap(), G8, sem=ctx.cc2)
            emit_ag(ctx, q.w2qT, w2g.ap(), G8, sem=ctx.cc2)
        cc_agw = ctx.cc2.n
        with nc.named_scope("p2_ag"):
            emit_ag(ctx, xnT_in.ap(), xg_d.ap(), G4)
        cc_xg0 = ctx.cc.n

        with nc.named_scope("p3_attn0"):
            with contextlib.ExitStack() as pes:
                ctx.rhs_gate = (ctx.cc, cc_xg0)
                ctx.w_gate = (ctx.pc, pc_a0)
                phase_attn0(nc, ctx, pes, a0)
                ctx.rhs_gate = None
                ctx.w_gate = None
                fence()
        with nc.named_scope("p3_ag"):
            emit_ag(ctx, ctx_in.ap(), ctx_g.ap(), G4)
        cc_ctx0 = ctx.cc.n

        with nc.named_scope("p4_outproj"):
            with contextlib.ExitStack() as pes:
                ctx.op_gate = (ctx.cc, cc_ctx0)
                ctx.w_gate = (ctx.pc, pc_dense)
                phase_outproj(nc, ctx, pes, ctx_g.ap(), dense_b, x0, x_d, sfx="_p4")
                ctx.op_gate = None
                ctx.w_gate = None
                fence()

        f0.pcw = (ctx.cc2, cc_agw)
        with nc.named_scope("p5_ffn0"):
            with contextlib.ExitStack() as pes:
                phase_ffn0(nc, ctx, pes, f0, x_d, x_d2)
                fence()

        with nc.named_scope("p6_ln"):
            with contextlib.ExitStack() as pes:
                phase_ln(nc, ctx, pes, x_d2, a0.ident, xnT_in.ap(), sfx="_p6")
                fence()
        with nc.named_scope("p6_ag"):
            emit_ag(ctx, xnT_in.ap(), xg_d.ap(), G4)
        cc_xg1 = ctx.cc.n

        with nc.named_scope("p7_attn1"):
            with contextlib.ExitStack() as pes:
                ctx.rhs_gate = (ctx.cc, cc_xg1)
                ctx.w_gate = (ctx.pc, pc_a1)
                phase_attn1(nc, ctx, pes, a1)
                ctx.rhs_gate = None
                ctx.w_gate = None
                fence()
        with nc.named_scope("p7_ag"):
            emit_ag(ctx, ctx_in.ap(), ctx_g.ap(), G4)
        cc_ctx1 = ctx.cc.n

        with nc.named_scope("p8_outproj"):
            with contextlib.ExitStack() as pes:
                ctx.op_gate = (ctx.cc, cc_ctx1)
                ctx.w_gate = (ctx.pc, pc_wo)
                phase_outproj(nc, ctx, pes, ctx_g.ap(), wo_b, x_d2, x_d, sfx="_p8")
                ctx.op_gate = None
                ctx.w_gate = None
                fence()

        f1.pcw = (ctx.pc, pc_f1)
        with nc.named_scope("p9_ffn1"):
            with contextlib.ExitStack() as pes:
                phase_ffn1(nc, ctx, pes, f1, x_d, xo)
                fence()

        @block.sync
        def _(sync):
            sync.wait_ge(ctx.st.h, ctx.st.n)
            sync.wait_ge(ctx.stp[0].h, ctx.stp[0].n)
            sync.wait_ge(ctx.stp[1].h, ctx.stp[1].n)
    return nc


def kernel_fused(**inputs):
    f32 = np.float32
    x = np.ascontiguousarray(np.asarray(inputs["L0_attn_x"], dtype=f32).reshape(B * S, D))
    qkv_w = np.asarray(inputs["L0_attn_qkv_w"], dtype=f32)
    dense_w = np.ascontiguousarray(np.asarray(inputs["L0_attn_dense_w"], dtype=f32))
    table = np.asarray(inputs["L0_attn_rel_attn_table"], dtype=f32)
    w1 = np.asarray(inputs["L0_ffn_w1"], dtype=f32)
    b1 = np.asarray(inputs["L0_ffn_b1"], dtype=f32)
    w2 = np.asarray(inputs["L0_ffn_w2"], dtype=f32)
    b2 = np.asarray(inputs["L0_ffn_b2"], dtype=f32)
    Wq = np.asarray(inputs["L1_attn_Wq"], dtype=f32)
    Wk = np.asarray(inputs["L1_attn_Wk"], dtype=f32)
    Wv = np.asarray(inputs["L1_attn_Wv"], dtype=f32)
    Wo = np.ascontiguousarray(np.asarray(inputs["L1_attn_Wo"], dtype=f32))
    wgI = np.ascontiguousarray(np.asarray(inputs["L1_ffn_w_gate"], dtype=np.int8))
    wuI = np.ascontiguousarray(np.asarray(inputs["L1_ffn_w_up"], dtype=np.int8))
    wdI = np.ascontiguousarray(np.asarray(inputs["L1_ffn_w_down"], dtype=np.int8))
    sg = np.asarray(inputs["L1_ffn_s_gate"], dtype=f32)
    su = np.asarray(inputs["L1_ffn_s_up"], dtype=f32)
    sd = np.asarray(inputs["L1_ffn_s_down"], dtype=f32)

    CH = consts_host()
    EYE = np.eye(128, dtype=ml_dtypes.bfloat16)
    EYE32 = np.eye(128, dtype=np.float32)
    cos0, sin0 = _l0_cos_sin_rows()
    cos1, sin1 = _l1_cos_sin_interleaved()
    cores = list(range(NCORES))

    nc = _get_nc("fused", build_fused)
    in_maps = []
    for c in cores:
        j = c % 4
        in_maps.append({
            "x0": np.ascontiguousarray(x[TOK * c:TOK * (c + 1)]),
            "a0wq": np.ascontiguousarray(qkv_w[:, 512 * j:512 * (j + 1)]),
            "a0wk": np.ascontiguousarray(qkv_w[:, D + 512 * j:D + 512 * (j + 1)]),
            "a0wv": np.ascontiguousarray(qkv_w[:, 2 * D + 512 * j:2 * D + 512 * (j + 1)]),
            "a0tbl": np.ascontiguousarray(table[:, 4 * j:4 * (j + 1)]),
            "a0oh": _l0_onehot(), "mbig": _mbig(),
            "a0cosr": cos0, "a0sinr": sin0,
            "ident": EYE, "id16": np.eye(16, dtype=np.float32),
            "a0rot": _rot_host(0), "jrev": EYE32[::-1].copy(),
            "dense_w": dense_w,
            "w1s": np.ascontiguousarray(w1[1024 * c:1024 * (c + 1)]),
            "w2s": np.ascontiguousarray(w2[256 * c:256 * (c + 1)]),
            "b1t": np.ascontiguousarray(b1.reshape(64, 128).T),
            "b2t": np.ascontiguousarray(b2.reshape(16, 128).T),
            "ident32": EYE32,
            "a1wq": np.ascontiguousarray(Wq[:, 512 * j:512 * (j + 1)]),
            "a1wk": np.ascontiguousarray(Wk[:, 128 * j:128 * (j + 1)]),
            "a1wv": np.ascontiguousarray(Wv[:, 128 * j:128 * (j + 1)]),
            "a1cosr": cos1, "a1sinr": sin1, "a1rot": _rot_l1_host(),
            "wo_w": Wo,
            "wg": wgI, "wu": wuI, "wd": wdI,
            "sg": np.ascontiguousarray(sg.reshape(64, 128).T),
            "su": np.ascontiguousarray(su.reshape(64, 128).T),
            "sd": np.ascontiguousarray(sd.reshape(16, 128).T),
            "consts": CH,
        })
    res = run_bass_kernel_spmd(nc, in_maps, core_ids=cores)
    xout = np.concatenate([res.results[c]["xo"] for c in cores], axis=0)
    return xout.reshape(B, S, D)


def kernel(**inputs):
    return kernel_fused(**inputs)
